# version 08
# baseline (speedup 1.0000x reference)
import numpy as np
import ml_dtypes
import concourse.bass as bass
import concourse.mybir as mybir
from concourse.bass_utils import run_bass_kernel_spmd
from concourse.ordered_engine_set import OrderedEngineSet

F32 = mybir.dt.float32
BF16 = mybir.dt.bfloat16
U8 = mybir.dt.uint8
AF = mybir.ActivationFunctionType
ALU = mybir.AluOpType
AX = mybir.AxisListType

ENGINES = ("sync", "gpsimd", "tensor", "vector", "scalar")

T = 2048
C = 1024
NT = T // 128
NC = C // 128
HD = 64
NHEAD = 16
HMOE = 1408
NHK = HMOE // 128
NEXP = 9
NCORE = 8
GTOK = NCORE * T
KB = 1024


class Sem:
    __slots__ = ("h", "n", "name")

    def __init__(self, nc, name):
        self.h = nc.alloc_semaphore(name)
        self.n = 0
        self.name = name


class Prog:
    def __init__(self, nc):
        self.nc = nc
        self.q = {e: [] for e in ENGINES}
        self._sems = {}

    def sem(self, name):
        if name not in self._sems:
            self._sems[name] = Sem(self.nc, name)
        return self._sems[name]

    def add(self, eng, fn, inc=None, k=1):
        if inc is not None:
            inc.n += k
            self.q[eng].append(lambda e, fn=fn, s=inc.h, k=k: fn(e).then_inc(s, k))
            return inc.n
        self.q[eng].append(fn)
        return None

    def wait(self, eng, sem, n):
        if n and n > 0:
            self.q[eng].append(lambda e, s=sem.h, n=n: e.wait_ge(s, n))

    def addf(self, eng, fn):
        sem = self.sem("flush_" + eng)
        n = self.add(eng, fn, inc=sem)
        self.wait(eng, sem, n)
        return n

    def finalize(self):
        with self.nc.Block() as block:
            block.sync(lambda e: [f(e) for f in self.q["sync"]])
            block.gpsimd(lambda e: [f(e) for f in self.q["gpsimd"]])
            block.tensor(lambda e: [f(e) for f in self.q["tensor"]])
            block.vector(lambda e: [f(e) for f in self.q["vector"]])
            block.scalar(lambda e: [f(e) for f in self.q["scalar"]])


def _dt_size(dt):
    return {F32: 4, BF16: 2, U8: 1}[dt]



def _cm(W):
    Cin, Cout = np.asarray(W).shape
    return np.ascontiguousarray(
        np.asarray(W, np.float32).reshape(Cin // 128, 128, Cout)
        .transpose(1, 0, 2).reshape(128, -1))


def _pp(v):
    v = np.asarray(v, dtype=np.float32)
    K = v.shape[0] // 128
    return np.ascontiguousarray(v.reshape(K, 128).T)


def make_host_globals(inputs):
    g = {}
    g["w_qkv"] = _cm(inputs["L0_attn_Wqkv"])
    g["b_qkv_row"] = np.asarray(inputs["L0_attn_bqkv"], np.float32).reshape(1, 3 * C)
    g["w_proj"] = _cm(inputs["L0_attn_Wproj"])
    g["b_proj_pp"] = _pp(inputs["L0_attn_bproj"])
    g["w_r"] = _cm(np.asarray(inputs["L0_ffn_W_r"], np.float32).T)
    g["w_k"] = _cm(np.asarray(inputs["L0_ffn_W_k"], np.float32).T)
    g["w_v"] = _cm(np.asarray(inputs["L0_ffn_W_v"], np.float32).T)
    g["mu_pp"] = np.ascontiguousarray(
        np.concatenate([_pp(inputs["L0_ffn_mu_r"]), _pp(inputs["L0_ffn_mu_k"])],
                       axis=1))
    g["w_q1"] = _cm(np.asarray(inputs["L1_attn_Wq"], np.float32).T)
    g["w_k1"] = _cm(np.asarray(inputs["L1_attn_Wk"], np.float32).T)
    g["w_v1"] = _cm(np.asarray(inputs["L1_attn_Wv"], np.float32).T)
    g["b_qkv1_pp"] = np.ascontiguousarray(
        np.stack([np.asarray(inputs["L1_attn_bq"], np.float32),
                  np.asarray(inputs["L1_attn_bk"], np.float32),
                  np.asarray(inputs["L1_attn_bv"], np.float32)], axis=1))
    g["w_gate"] = _cm(np.asarray(inputs["L1_ffn_gate_w"], np.float32).T)
    g["b_gate_row"] = np.asarray(inputs["L1_ffn_gate_b"], np.float32).reshape(1, 8)

    w1 = np.asarray(inputs["L1_ffn_w1"], np.float32)
    w3 = np.asarray(inputs["L1_ffn_w3"], np.float32)
    w2 = np.asarray(inputs["L1_ffn_w2"], np.float32)
    sw1 = np.asarray(inputs["L1_ffn_sw1"], np.float32)
    sw3 = np.asarray(inputs["L1_ffn_sw3"], np.float32)
    sw2 = np.asarray(inputs["L1_ffn_sw2"], np.float32)

    def blocks13(w_all, sw):
        out = np.empty((NEXP * NHK, 128, NC * 128), np.float32)
        for e in range(NEXP):
            w = sw if e == 8 else w_all[e]
            wr = w.reshape(NC, 128, NHK, 128)
            for hk in range(NHK):
                out[e * NHK + hk] = wr[:, :, hk, :].transpose(1, 0, 2).reshape(128, -1)
        return np.ascontiguousarray(out)

    g["w1b"] = blocks13(w1, sw1)
    g["w3b"] = blocks13(w3, sw3)
    w2b = np.empty((NEXP * NHK, 128, C), np.float32)
    for e in range(NEXP):
        w2b[e * NHK:(e + 1) * NHK] = (sw2 if e == 8 else w2[e]).reshape(NHK, 128, C)
    g["w2b"] = np.ascontiguousarray(w2b)

    def b13(b_all, sb):
        cols = [(sb if e == 8 else b_all[e]).reshape(NHK, 128).T for e in range(NEXP)]
        return np.ascontiguousarray(np.concatenate(cols, axis=1), dtype=np.float32)

    g["b1_pp"] = b13(np.asarray(inputs["L1_ffn_b1"], np.float32),
                     np.asarray(inputs["L1_ffn_sb1"], np.float32))
    g["b3_pp"] = b13(np.asarray(inputs["L1_ffn_b3"], np.float32),
                     np.asarray(inputs["L1_ffn_sb3"], np.float32))
    g["b2cat"] = np.ascontiguousarray(
        np.concatenate([np.asarray(inputs["L1_ffn_b2"], np.float32),
                        np.asarray(inputs["L1_ffn_sb2"], np.float32).reshape(1, C)],
                       axis=0))

    m4 = np.zeros((128, 4 * 512), np.float32)
    for pp in range(4):
        for qtl in range(4):
            blk = m4[:, pp * 512 + qtl * 128:pp * 512 + (qtl + 1) * 128]
            if qtl < pp:
                blk[:] = -1e9
            elif qtl == pp:
                ii = np.arange(128)
                blk[:] = np.where(ii[:, None] <= ii[None, :], 0.0, -1e9)
    g["mask4t"] = m4.astype(ml_dtypes.bfloat16)
    g["ones_col"] = np.ones((128, 1), dtype=ml_dtypes.bfloat16)
    g["ident_bf"] = np.eye(128, dtype=ml_dtypes.bfloat16)
    g["ident_f32"] = np.eye(128, dtype=np.float32)
    g["mask_bf"] = np.where(np.tril(np.ones((128, 128), bool)), 0.0, -1e9
                            ).astype(ml_dtypes.bfloat16)
    g["ones_bf"] = np.ones((1, 512), dtype=ml_dtypes.bfloat16)
    g["ones_f32"] = np.ones((1, 128), dtype=np.float32)
    return g


GLOBAL_SPECS = [
    ("w_qkv", (128, NC * 3 * C), F32), ("b_qkv_row", (1, 3 * C), F32),
    ("w_proj", (128, NC * C), F32), ("b_proj_pp", (128, NC), F32),
    ("w_r", (128, NC * C), F32), ("w_k", (128, NC * C), F32),
    ("w_v", (128, NC * C), F32), ("mu_pp", (128, 16), F32),
    ("w_q1", (128, NC * HD), F32), ("w_k1", (128, NC * HD), F32),
    ("w_v1", (128, NC * HD), F32), ("b_qkv1_pp", (64, 3), F32),
    ("w_gate", (128, NC * 8), F32), ("b_gate_row", (1, 8), F32),
    ("w1b", (NEXP * NHK, 128, NC * 128), F32),
    ("w3b", (NEXP * NHK, 128, NC * 128), F32),
    ("w2b", (NEXP * NHK, 128, C), F32),
    ("b1_pp", (128, NEXP * NHK), F32), ("b3_pp", (128, NEXP * NHK), F32),
    ("b2cat", (NEXP, C), F32),
    ("mask4t", (128, 4 * 512), BF16), ("ones_col", (128, 1), BF16),
    ("ident_bf", (128, 128), BF16), ("ident_f32", (128, 128), F32),
    ("mask_bf", (128, 128), BF16), ("ones_bf", (1, 512), BF16),
    ("ones_f32", (1, 128), F32),
]



class BuildA:
    def __init__(self, stop="all", dbg=()):
        self.stop = stop
        self.dbg = set(dbg)
        nc = bass.Bass("TRN2", target_bir_lowering=False)
        self.nc = nc
        self.P = Prog(nc)
        self.x_in = nc.dram_tensor("x", [T, C], F32, kind="ExternalInput").ap()
        self.g = {}
        for name, shape, dt in GLOBAL_SPECS:
            self.g[name] = nc.dram_tensor(name, list(shape), dt,
                                          kind="ExternalInput").ap()
        self.xo = nc.dram_tensor("xo", [T, C], F32, kind="ExternalOutput").ap()
        self.qo = nc.dram_tensor("qo", [HD, T], BF16, kind="ExternalOutput").ap()
        self.ko = nc.dram_tensor("ko", [HD, T], BF16, kind="ExternalOutput").ap()
        self.vo = nc.dram_tensor("vo", [128, NT * HD], BF16, kind="ExternalOutput").ap()
        self._setup_sbuf()
        self._build()

    def _setup_sbuf(self):
        nc = self.nc
        sbt = nc.alloc_sbuf_tensor
        self.mask4t = sbt("s_mask4t", [128, 4 * 512], BF16).ap()
        self.ones_col = sbt("s_ones_col", [128, 1], BF16).ap()
        self.ident_bf = sbt("s_ident_bf", [128, 128], BF16).ap()
        self.ident_f32 = sbt("s_ident_f32", [128, 128], F32).ap()
        self.mask_bf = sbt("s_mask_bf", [128, 128], BF16).ap()
        self.ones_bf = sbt("s_ones_bf", [1, 512], BF16).ap()
        self.ones_f32 = sbt("s_ones_f32", [1, 128], F32).ap()
        self.bqkv_row = sbt("bqkv_row", [1, 3 * C], BF16).ap()
        self.bproj_pp = sbt("bproj_pp", [128, NC], F32).ap()
        self.mu_pp = sbt("s_mu_pp", [128, 16], F32).ap()
        self.bqkv1_pp = sbt("bqkv1_pp", [64, 3], F32).ap()
        self.bgate_row = sbt("bgate_row", [1, 8], F32).ap()
        self.b1_pp = sbt("s_b1_pp", [128, NEXP * NHK], F32).ap()
        self.b3_pp = sbt("s_b3_pp", [128, NEXP * NHK], F32).ap()
        self.b2cat = sbt("s_b2cat", [NEXP, C], BF16).ap()
        self.wgate = sbt("wgate", [128, NC * 8], BF16).ap()
        self.gbb = sbt("gbb", [128, 8], F32).ap()
        self.stat = sbt("stat", [128, 16], F32).ap()
        self.gsc = sbt("gsc", [128, 32], F32).ap()
        self.rw_tm = sbt("rw_tm", [128, NT * NEXP], F32).ap()
        self.rw_fm = sbt("rw_fm", [NEXP, T], BF16).ap()
        self.scr = sbt("scr", [128, 2048], F32).ap()
        self.px = sbt("px", [128, T + 1], F32).ap()
        self.xnt = sbt("xnt", [128, 2 * C], BF16).ap()
        self.xnb = sbt("xnb", [128, NC * T], BF16).ap()

        rem = nc.sbuf_bytes_remaining
        self.arena_bytes = (rem - 3 * KB) // 4 * 4
        assert self.arena_bytes >= 127 * KB, f"arena too small: {self.arena_bytes}"
        self.AR = sbt("arena", [128, self.arena_bytes], U8).ap()

        self.ps_s = nc.alloc_psum_tensor("ps_s", [128, 2048], F32)
        self.pbA = nc.alloc_psum_tensor("pbA", [128, 512], F32)
        self.pbB = nc.alloc_psum_tensor("pbB", [128, 512], F32)
        self.pbC = nc.alloc_psum_tensor("pbC", [128, 512], F32)
        self.pbD = nc.alloc_psum_tensor("pbD", [128, 512], F32)

    def av(self, off_b, parts, cols, dt):
        n = cols * _dt_size(dt)
        assert off_b % 4 == 0 and off_b + n <= self.arena_bytes, (off_b, n)
        return self.AR[0:parts, off_b:off_b + n].bitcast(dt)

    def psf(self, t, cols=512, base=0, parts=128, pbase=0):
        return t.ap()[pbase:pbase + parts, base:base + cols]

    def psb(self, t, cols, base=0, parts=128, pbase=0):
        return t.ap().bitcast(BF16)[pbase:pbase + parts, base:base + cols]

    def dbg_store(self, name, src_ap, shape, dt, waits):
        if name not in self.dbg:
            return
        od = self.nc.dram_tensor("dbg_" + name, list(shape), dt,
                                 kind="ExternalOutput").ap()
        P, st = self.P, self.P.sem("st")
        for s, n in waits:
            P.wait("sync", s, n)
        P.add("sync", lambda e, o=od, a=src_ap: e.dma_start(out=o, in_=a), inc=st, k=16)

    def _build(self):
        self._load_consts()
        self._ln(src_kind="dram", dram=self.x_in, ready=None)
        self.dbg_store("xnb0", self.xnb, (128, NC * T), BF16,
                       [(self.P.sem("tA"), self.P.sem("tA").n)])
        if self.stop == "ln":
            return self._finish()
        self._ffn()
        if self.stop == "ffn":
            return self._finish()
        self._attn()
        if self.stop == "attn":
            return self._finish()
        self._proj_resid()
        if self.stop == "resid":
            return self._finish()
        self._ln(src_kind="dram", dram=self.xo, ready=self.x1_tile_ready)
        self.dbg_store("xnb1", self.xnb, (128, NC * T), BF16,
                       [(self.P.sem("tA"), self.P.sem("tA").n)])
        if self.stop == "ln1":
            return self._finish()
        self._qkv1()
        if self.stop == "qkv1":
            return self._finish()
        self._gate()
        if self.stop == "gate":
            return self._finish()
        self._moe()
        return self._finish()

    def _finish(self):
        P = self.P
        for s in ("st", "qXo", "qQKV"):
            sem = P.sem(s)
            if sem.n:
                P.wait("sync", sem, sem.n)
        qM = P.sem("qM")
        if qM.n:
            P.wait("gpsimd", qM, qM.n)
        P.finalize()

    def _load_consts(self):
        P = self.P
        qS, qG, tT, tV = P.sem("qS"), P.sem("qG"), P.sem("tT"), P.sem("tV")
        for t, name in [
            (self.mask4t, "mask4t"), (self.ones_col, "ones_col"),
            (self.ident_bf, "ident_bf"), (self.ident_f32, "ident_f32"),
            (self.mask_bf, "mask_bf"), (self.ones_bf, "ones_bf"),
            (self.ones_f32, "ones_f32"), (self.bproj_pp, "b_proj_pp"),
            (self.mu_pp, "mu_pp"), (self.bqkv1_pp, "b_qkv1_pp"),
            (self.b1_pp, "b1_pp"), (self.b3_pp, "b3_pp"),
            (self.bgate_row, "b_gate_row"),
        ]:
            P.add("sync", lambda e, t=t, a=self.g[name]: e.dma_start(out=t, in_=a),
                  inc=qS, k=16)
        for t, name in [(self.bqkv_row, "b_qkv_row"), (self.b2cat, "b2cat"),
                        (self.wgate, "w_gate")]:
            P.add("gpsimd", lambda e, t=t, a=self.g[name]: e.dma_start(out=t, in_=a),
                  inc=qG, k=16)
        self.n_const_s = qS.n
        self.n_const_g = qG.n
        P.wait("tensor", qS, self.n_const_s)
        n = P.add("tensor", lambda e: e.matmul(
            self.psf(self.pbD, 8), self.ones_f32[0:1, :], self.bgate_row,
            start=True, stop=True), inc=tT)
        P.wait("vector", tT, n)
        self.gbb_done = P.add("vector", lambda e: e.tensor_copy(
            self.gbb, self.psf(self.pbD, 8)), inc=tV)

    def _ln(self, src_kind, dram, ready):
        P = self.P
        qS, tV, tA, tT = P.sem("qS"), P.sem("tV"), P.sem("tA"), P.sem("tT")
        P.wait("tensor", tV, tV.n)
        P.wait("tensor", tA, tA.n)
        g_ = getattr(self, "_xnb_guard", None)
        if g_:
            P.wait("scalar", g_[0], g_[1])
            self._xnb_guard = None
        st = self.stat
        ln_x = [self.scr[:, 0:1024], self.scr[:, 1024:2048]]
        jk = self.px[:, 0:1024]
        packs = [self.pbA, self.pbB]
        evac_done = [0, 0]
        xnt_free = [0, 0]
        dve_read = [tV.n, tV.n]
        for n in range(NT):
            par = n % 2
            so = 8 * par
            xt = ln_x[par]
            if dve_read[par]:
                P.wait("sync", tV, dve_read[par])
            if ready is not None:
                P.wait("sync", ready[n][0], ready[n][1])
            ldn = P.add("sync", lambda e, n=n, xt=xt, dram=dram: e.dma_start(
                out=xt, in_=dram[n * 128:(n + 1) * 128, :]), inc=qS, k=16)
            P.wait("vector", qS, ldn)
            P.add("vector", lambda e, xt=xt, so=so: e.scalar_tensor_tensor(
                jk, xt, 1.0, xt, ALU.mult, ALU.mult, accum_out=st[:, so + 1:so + 2]))
            P.addf("vector", lambda e, xt=xt, so=so: e.tensor_scalar(
                jk, xt, 1.0, 0.0, ALU.mult, ALU.add, accum_out=st[:, so:so + 1]))
            P.add("vector", lambda e, so=so: e.tensor_scalar(
                st[:, so + 2:so + 3], st[:, so:so + 1], 1.0 / C, None, ALU.mult))
            P.addf("vector", lambda e, so=so: e.tensor_scalar(
                st[:, so + 3:so + 4], st[:, so + 1:so + 2], 1.0 / C, 1e-5,
                ALU.mult, ALU.add))
            P.addf("vector", lambda e, so=so: e.scalar_tensor_tensor(
                st[:, so + 5:so + 6], st[:, so + 2:so + 3], -1.0,
                st[:, so + 2:so + 3], ALU.mult, ALU.mult))
            nvar = P.add("vector", lambda e, so=so: e.tensor_add(
                st[:, so + 3:so + 4], st[:, so + 3:so + 4], st[:, so + 5:so + 6]),
                inc=tV)
            P.wait("scalar", tV, nvar)
            P.addf("scalar", lambda e, so=so: e.activation(
                st[:, so + 5:so + 6], st[:, so + 3:so + 4], AF.Ln))
            nrstd = P.add("scalar", lambda e, so=so: e.activation(
                st[:, so + 4:so + 5], st[:, so + 5:so + 6], AF.Exp, scale=-0.5),
                inc=tA)
            P.wait("vector", tA, nrstd)
            if xnt_free[par]:
                P.wait("vector", tA, xnt_free[par])
            P.addf("vector", lambda e, so=so: e.scalar_tensor_tensor(
                st[:, so + 5:so + 6], st[:, so + 2:so + 3], -1.0,
                st[:, so + 4:so + 5], ALU.mult, ALU.mult))
            xnt = self.xnt[:, par * C:(par + 1) * C]
            nxn = P.add("vector", lambda e, xt=xt, xnt=xnt, so=so: e.tensor_scalar(
                xnt, xt, st[:, so + 4:so + 5], st[:, so + 5:so + 6],
                ALU.mult, ALU.add), inc=tV)
            dve_read[par] = nxn
            P.wait("tensor", tV, nxn)
            if evac_done[par]:
                P.wait("tensor", tA, evac_done[par])
            pk = packs[par]
            for ck in range(NC):
                P.add("tensor", lambda e, ck=ck, pk=pk, xnt=xnt: e.transpose(
                    self.psb(pk, 128, base=ck * 128), xnt[:, ck * 128:(ck + 1) * 128],
                    self.ident_bf), inc=(tT if ck == NC - 1 else None))
            P.wait("scalar", tT, tT.n)
            dst = self.xnb.rearrange("p (ck t) -> p ck t", ck=NC)[:, :, n * 128:(n + 1) * 128]
            srcv = self.psb(pk, 1024).rearrange("p (ck t) -> p ck t", ck=NC)
            ne = P.add("scalar", lambda e, dst=dst, srcv=srcv: e.activation(
                dst, srcv, AF.Copy), inc=tA)
            evac_done[par] = ne
            xnt_free[par] = ne
        self.ln_tA = tA.n

    def _ffn(self):
        P = self.P
        qG, tV, tA, tT = P.sem("qG"), P.sem("tV"), P.sem("tA"), P.sem("tT")
        xr = self.av(0, 128, NC * T, BF16)
        xk = self.av(32 * KB, 128, NC * T, BF16)
        kact = self.av(64 * KB, 128, NC * T, BF16)
        r_bf = self.av(32 * KB, 128, NC * T, BF16)
        ffn_C = self.av(0, 128, NC * T, BF16)
        wbuf = self.av(96 * KB, 128, NC * 1024, BF16)
        self.ffn_C = ffn_C

        P.addf("vector", lambda e: e.memset(self.px[:, 0:1], 0.0))
        P.wait("vector", P.sem("tA"), self.ln_tA)
        d = self.scr[:, 0:2048]
        xr_v = xr.rearrange("p (ck t) -> p ck t", ck=NC)
        xk_v = xk.rearrange("p (ck t) -> p ck t", ck=NC)
        for ck in range(NC):
            src = self.xnb[:, ck * T:(ck + 1) * T]
            P.addf("vector", lambda e, src=src: e.tensor_copy(self.px[:, 1:T + 1], src))
            P.addf("vector", lambda e: e.tensor_sub(d, self.px[:, 1:T + 1],
                                                    self.px[:, 0:T]))
            P.add("vector", lambda e, ck=ck: e.scalar_tensor_tensor(
                xr_v[:, ck], d, self.mu_pp[:, ck:ck + 1], self.px[:, 0:T],
                ALU.mult, ALU.add))
            self.mix_done = P.add("vector", lambda e, ck=ck: e.scalar_tensor_tensor(
                xk_v[:, ck], d, self.mu_pp[:, 8 + ck:9 + ck], self.px[:, 0:T],
                ALU.mult, ALU.add), inc=tV)

        def load_w(name, wait_t):
            P.wait("gpsimd", tT, wait_t)
            return P.add("gpsimd", lambda e, a=self.g[name]: e.dma_start(
                out=wbuf, in_=a), inc=qG, k=16)

        banks8 = [self.pbA, self.pbB, self.pbC, self.pbD] + [self.ps_s] * 4
        bank_base = [0, 0, 0, 0, 0, 512, 1024, 1536]

        def bank(i):
            return banks8[i], bank_base[i]

        nwk = load_w("w_k", 0)
        P.wait("tensor", qG, nwk)
        P.wait("tensor", tV, self.mix_done)
        kact_v = kact.rearrange("p (dk t) -> p dk t", dk=NC)
        tmp = [self.scr[:, 0:512], self.scr[:, 512:1024]]
        tmp_tA = [0, 0]
        bank_free = [0] * 8
        for dk in range(NC):
            for ns in range(4):
                bi = (dk % 2) * 4 + ns
                bt, bb = bank(bi)
                P.wait("tensor", tV, bank_free[bi])
                for ck in range(NC):
                    P.add("tensor", lambda e, dk=dk, ns=ns, ck=ck, bt=bt, bb=bb:
                          e.matmul(self.psf(bt, 512, base=bb),
                                   wbuf[:, ck * 1024 + dk * 128:ck * 1024 + (dk + 1) * 128],
                                   xk[:, ck * T + ns * 512:ck * T + (ns + 1) * 512],
                                   start=(ck == 0), stop=(ck == NC - 1)),
                          inc=(tT if ck == NC - 1 else None))
                nmm = tT.n
                wpar = (dk * 4 + ns) % 2
                P.wait("vector", tT, nmm)
                P.wait("vector", tA, tmp_tA[wpar])
                nrelu = P.add("vector", lambda e, bt=bt, bb=bb, wpar=wpar:
                              e.tensor_scalar(tmp[wpar], self.psf(bt, 512, base=bb),
                                              0.0, None, ALU.max), inc=tV)
                bank_free[bi] = nrelu
                P.wait("scalar", tV, nrelu)
                tmp_tA[wpar] = P.add("scalar", lambda e, dk=dk, ns=ns, wpar=wpar:
                                     e.activation(kact_v[:, dk, ns * 512:(ns + 1) * 512],
                                                  tmp[wpar], AF.Square), inc=tA)
        k_mm_done = tT.n
        kact_tA = tA.n

        nwr = load_w("w_r", k_mm_done)
        P.wait("tensor", qG, nwr)
        rbf_v = r_bf.rearrange("p (dk t) -> p dk t", dk=NC)
        r_bank_tA = [0] * 4
        for dk in range(NC):
            for ns in range(4):
                bi = ns
                bt, bb = bank(bi)
                P.wait("tensor", tV, bank_free[bi])
                P.wait("tensor", tA, r_bank_tA[bi])
                for ck in range(NC):
                    P.add("tensor", lambda e, dk=dk, ns=ns, ck=ck, bt=bt, bb=bb:
                          e.matmul(self.psf(bt, 512, base=bb),
                                   wbuf[:, ck * 1024 + dk * 128:ck * 1024 + (dk + 1) * 128],
                                   xr[:, ck * T + ns * 512:ck * T + (ns + 1) * 512],
                                   start=(ck == 0), stop=(ck == NC - 1)),
                          inc=(tT if ck == NC - 1 else None))
                nmm = tT.n
                P.wait("scalar", tT, nmm)
                r_bank_tA[bi] = P.add("scalar", lambda e, dk=dk, ns=ns, bt=bt, bb=bb:
                                      e.activation(rbf_v[:, dk, ns * 512:(ns + 1) * 512],
                                                   self.psf(bt, 512, base=bb),
                                                   AF.Sigmoid), inc=tA)
        r_mm_done = tT.n
        sig_all = tA.n

        nwv = load_w("w_v", r_mm_done)
        P.wait("tensor", qG, nwv)
        P.wait("tensor", tA, kact_tA)
        ffnC_v = ffn_C.rearrange("p (dk t) -> p dk t", dk=NC)
        v_bank_tV = [bank_free[4 + i] for i in range(4)]
        v_bank_tV = [0, 0, 0, 0]
        for i in range(4):
            v_bank_tV[i] = bank_free[4 + i]
        for dk in range(NC):
            for ns in range(4):
                bi = 4 + ns
                bt, bb = bank(bi)
                P.wait("tensor", tV, v_bank_tV[ns])
                for ck in range(NC):
                    P.add("tensor", lambda e, dk=dk, ns=ns, ck=ck, bt=bt, bb=bb:
                          e.matmul(self.psf(bt, 512, base=bb),
                                   wbuf[:, ck * 1024 + dk * 128:ck * 1024 + (dk + 1) * 128],
                                   kact[:, ck * T + ns * 512:ck * T + (ns + 1) * 512],
                                   start=(ck == 0), stop=(ck == NC - 1)),
                          inc=(tT if ck == NC - 1 else None))
                nmm = tT.n
                P.wait("vector", tT, nmm)
                P.wait("vector", tA, sig_all if (dk == 0 and ns == 0) else 0)
                v_bank_tV[ns] = P.add("vector", lambda e, dk=dk, ns=ns, bt=bt, bb=bb:
                                      e.tensor_mul(ffnC_v[:, dk, ns * 512:(ns + 1) * 512],
                                                   self.psf(bt, 512, base=bb),
                                                   rbf_v[:, dk, ns * 512:(ns + 1) * 512]),
                                      inc=tV)
        self.ffn_tV = tV.n
        self.ffn_v_mm_done = tT.n
        self.dbg_store("ffn", ffn_C, (128, NC * T), BF16, [(tV, tV.n)])

    def _attn(self):
        P = self.P
        qG, tV, tA, tT = P.sem("qG"), P.sem("tV"), P.sem("tA"), P.sem("tT")
        wqkv = self.av(32 * KB, 128, NC * 3 * C, BF16)
        yD = self.av(80 * KB, 128, NC * T, BF16)
        QT = self.av(112 * KB, 64, T, BF16)
        KT = self.av(116 * KB, 64, T, BF16)
        VTP = self.av(125 * KB, 128, T, BF16)
        VTm = self.av(120 * KB, 128, NT * (HD + 1), BF16)
        PTc = [self.av(122 * KB, 128, 512, BF16),
               self.av(123 * KB, 128, 512, BF16)]
        ytmp = [self.av(124 * KB, 128, HD, BF16),
                self.av(124 * KB + 256, 128, HD, BF16)]
        tm_h = self.px[:, 0:1536].bitcast(BF16)
        self.yD = yD

        nwq = None
        P.wait("gpsimd", tT, self.ffn_v_mm_done)
        nwq = P.add("gpsimd", lambda e: e.dma_start(out=wqkv, in_=self.g["w_qkv"]),
                    inc=qG, k=16)
        P.wait("tensor", tV, tV.n)
        P.wait("tensor", tA, tA.n)
        kvg = getattr(self, "_kv_share_done", None)
        if kvg:
            for s_, n_ in kvg:
                P.wait("vector", s_, n_)
                P.wait("scalar", s_, n_)
            self._kv_share_done = None
        P.wait("tensor", qG, nwq)
        P.wait("tensor", qG, self.n_const_g)
        st8 = self.stat
        nvone = P.add("vector", lambda e: e.memset(
            VTm.rearrange("p (n c) -> p n c", c=HD + 1)[:, :, HD], 1.0),
            inc=tV)
        P.wait("scalar", tV, nvone)

        qkv_banks = [self.pbA, self.pbB]
        qkv_free_tA = [0, 0]
        pack_banks = [self.pbC, self.pbD]
        pack_free = [0, 0]
        pack_free_eng = [None, None]

        for h in range(NHEAD):
            P.wait("tensor", tV, tV.n)
            P.wait("tensor", tA, tA.n)
            for slab in range(6):
                par = slab % 2
                bt = qkv_banks[par]
                P.wait("tensor", tA, qkv_free_tA[par])
                for ck in range(NC):
                    P.add("tensor", lambda e, h=h, slab=slab, ck=ck, bt=bt: e.matmul(
                        self.psf(bt, 512),
                        self.xnb[:, ck * T + h * 128:ck * T + (h + 1) * 128],
                        wqkv[:, ck * 3072 + slab * 512:ck * 3072 + (slab + 1) * 512],
                        start=(ck == 0), stop=False))
                nmm = P.add("tensor", lambda e, slab=slab, bt=bt: e.matmul(
                    self.psf(bt, 512), self.ones_bf[0:1, 0:128],
                    self.bqkv_row[0:1, slab * 512:(slab + 1) * 512],
                    start=False, stop=True), inc=tT)
                P.wait("scalar", tT, nmm)
                qkv_free_tA[par] = P.add("scalar", lambda e, slab=slab, bt=bt:
                                         e.activation(tm_h[:, slab * 512:(slab + 1) * 512],
                                                      self.psf(bt, 512), AF.Copy),
                                         inc=tA)
            qkv_done_tA = tA.n

            def emit_pack(src_col0, dst_fm, s_lo, pk_i, on_act):
                pk = pack_banks[pk_i]
                P.wait("tensor", tA, qkv_done_tA)
                if pack_free[pk_i]:
                    P.wait("tensor", P.sem(pack_free_eng[pk_i]), pack_free[pk_i])
                for i in range(8):
                    s = s_lo + i
                    P.add("tensor", lambda e, s=s, i=i, pk=pk, src_col0=src_col0:
                          e.transpose(self.psb(pk, 128, base=i * 128, parts=64),
                                      tm_h[:, src_col0 + s * 64:src_col0 + (s + 1) * 64],
                                      self.ident_bf),
                          inc=(tT if i == 7 else None))
                ntr = tT.n
                eng = "scalar" if on_act else "vector"
                P.wait(eng, tT, ntr)
                dstv = dst_fm.rearrange("d (j s) -> d s j", s=16)[:, s_lo:s_lo + 8, :]
                srcv = self.psb(pk, 1024, parts=64).rearrange("d (s j) -> d s j", s=8)
                if on_act:
                    nn = P.add("scalar", lambda e, dstv=dstv, srcv=srcv:
                               e.activation(dstv, srcv, AF.Copy), inc=tA)
                else:
                    nn = P.add("vector", lambda e, dstv=dstv, srcv=srcv:
                               e.tensor_copy(dstv, srcv), inc=tV)
                pack_free[pk_i] = nn
                pack_free_eng[pk_i] = "tA" if on_act else "tV"
                return nn

            emit_pack(0, QT, 0, 0, False)
            emit_pack(0, QT, 8, 1, False)
            emit_pack(C, KT, 0, 0, False)
            nk = emit_pack(C, KT, 8, 1, False)
            emit_pack(2 * C, VTP[0:64, :], 0, 0, True)
            nvt = emit_pack(2 * C, VTP[0:64, :], 8, 1, True)
            pk = pack_banks[0]
            P.wait("tensor", P.sem(pack_free_eng[0]), pack_free[0])
            P.wait("tensor", tA, nvt)
            for gph in range(16):
                P.add("tensor", lambda e, gph=gph, pk=pk: e.transpose(
                    self.psb(pk, 64, base=gph * 64),
                    VTP[0:64, gph * 128:(gph + 1) * 128],
                    self.ident_bf[0:64, 0:64]),
                    inc=(tT if gph == 15 else None))
            nvtm_tr = tT.n
            P.wait("scalar", tT, nvtm_tr)
            nvtm = P.add("scalar", lambda e, pk=pk: e.activation(
                VTm.rearrange("p (n c) -> p n c", c=HD + 1)[:, :, 0:HD],
                self.psb(pk, 1024).rearrange("p (n c) -> p n c", c=HD),
                AF.Copy), inc=tA)
            pack_free[0] = nvtm
            pack_free_eng[0] = "tA"

            P.wait("tensor", tV, nk)
            P.wait("tensor", tA, nvtm)
            lrow = self.scr[0:1, 0:512]
            rrow = self.scr[0:1, 512:1024]
            lbank = [self.pbC, self.pbD]
            ybase = [1024, 1536]
            prev_y_ev = [0, 0]
            prev_l_ev = [0, 0]
            prev_rc_tV = 0
            prev_yev_tA = 0
            ytr_tT = [0, 0]
            for qg in range(4):
                nch = 4 * qg + 4
                gp = qg % 2
                qc0 = qg * 512
                nst = [0] * nch
                nexpc = [0] * nch
                npvc = [0] * nch
                P.wait("tensor", tV, prev_y_ev[gp])
                P.wait("tensor", tV, prev_l_ev[gp])
                for kt in range(nch + 1):
                    sp = kt % 2
                    if kt < nch:
                        if kt >= 2:
                            P.wait("tensor", tA, nexpc[kt - 2])
                        masked = kt >= 4 * qg
                        P.add("tensor", lambda e, kt=kt, sp=sp, qc0=qc0,
                              masked=masked: e.matmul(
                            self.psf(self.ps_s, 512, base=sp * 512),
                            KT[:, kt * 128:(kt + 1) * 128], QT[:, qc0:qc0 + 512],
                            start=True, stop=not masked),
                            inc=(None if masked else tT))
                        if masked:
                            pp = kt - 4 * qg
                            P.add("tensor", lambda e, sp=sp, pp=pp: e.matmul(
                                self.psf(self.ps_s, 512, base=sp * 512),
                                self.ident_bf,
                                self.mask4t[:, pp * 512:(pp + 1) * 512],
                                start=False, stop=True), inc=tT)
                        nst[kt] = tT.n
                        P.wait("scalar", tT, nst[kt])
                        if kt >= 2:
                            P.wait("scalar", tT, npvc[kt - 2])
                        nexpc[kt] = P.add("scalar", lambda e, sp=sp: e.activation(
                            PTc[sp], self.psf(self.ps_s, 512, base=sp * 512),
                            AF.Exp, scale=0.125), inc=tA)
                    if kt >= 1:
                        cj = kt - 1
                        cp = cj % 2
                        P.wait("tensor", tA, nexpc[cj])
                        for qtl in range(4):
                            P.add("tensor", lambda e, cp=cp, gp=gp, cj=cj,
                                  qtl=qtl, nch=nch: e.matmul(
                                self.psf(self.ps_s, HD + 1,
                                         base=ybase[gp] + qtl * (HD + 1)),
                                PTc[cp][:, qtl * 128:(qtl + 1) * 128],
                                VTm[:, cj * (HD + 1):(cj + 1) * (HD + 1)],
                                start=(cj == 0 and qtl == 0),
                                stop=(cj == nch - 1)),
                                inc=(tT if qtl == 3 else None))
                        npvc[cj] = tT.n
                P.wait("vector", tT, npvc[nch - 1])
                last_yts = 0
                for qtl in range(4):
                    yp = qtl % 2
                    P.addf("vector", lambda e, qtl=qtl, gp=gp: e.reciprocal(
                        st8[:, 8 + qtl:9 + qtl],
                        self.psf(self.ps_s, 1,
                                 base=ybase[gp] + qtl * (HD + 1) + HD)))
                    P.wait("vector", tT, ytr_tT[yp])
                    nyt = P.add("vector", lambda e, qtl=qtl, gp=gp, yp=yp:
                                e.tensor_scalar(
                        ytmp[yp], self.psf(self.ps_s, HD,
                                           base=ybase[gp] + qtl * (HD + 1)),
                        st8[:, 8 + qtl:9 + qtl], None, ALU.mult), inc=tV)
                    last_yts = nyt
                    P.wait("tensor", tV, nyt)
                    if qtl == 0:
                        P.wait("tensor", tA, prev_yev_tA)
                    ytr_tT[yp] = P.add("tensor", lambda e, qtl=qtl, yp=yp:
                                       e.transpose(
                        self.psb(self.pbA, 128, base=qtl * 128, parts=64),
                        ytmp[yp], self.ident_bf), inc=tT)
                prev_y_ev[gp] = last_yts
                ntr4 = tT.n
                P.wait("scalar", tT, ntr4)
                dst = yD[(h % 2) * 64:(h % 2) * 64 + 64,
                         (h // 2) * T + qg * 512:(h // 2) * T + (qg + 1) * 512]
                prev_yev_tA = P.add("scalar", lambda e, dst=dst: e.activation(
                    dst, self.psb(self.pbA, 512, parts=64), AF.Copy), inc=tA)
        self.attn_tA = tA.n
        self.attn_mm_done = tT.n
        self.dbg_store("yd", yD, (128, NC * T), BF16, [(tA, tA.n)])

    def _Pv(self, L):
        return self.av(120 * KB, 128, L, BF16)

    def _proj_resid(self, x_dram=None, xo_dram=None):
        if x_dram is None:
            x_dram = self.x_in
        if xo_dram is None:
            xo_dram = self.xo
        P = self.P
        qG, qS, tV, tA, tT = (P.sem(s) for s in ("qG", "qS", "tV", "tA", "tT"))
        qXo = P.sem("qXo")
        wproj = self.av(32 * KB, 128, NC * C, BF16)
        x1s = [self.av(48 * KB, 128, C, F32), self.av(52 * KB, 128, C, F32)]
        ffn_C = self.ffn_C
        yD = self.yD

        P.wait("gpsimd", tT, self.attn_mm_done)
        nwp = P.add("gpsimd", lambda e: e.dma_start(out=wproj, in_=self.g["w_proj"]),
                    inc=qG, k=16)
        P.wait("tensor", tV, tV.n)
        P.wait("tensor", tA, tA.n)
        P.wait("tensor", qG, nwp)
        banks = [self.pbA, self.pbB]
        bank_free_tV = [0, 0]
        comb_tV = 0
        for cko in range(NC):
            for ns in range(4):
                par = (cko * 4 + ns) % 2
                bt = banks[par]
                P.wait("tensor", tV, bank_free_tV[par])
                for ck in range(NC):
                    P.add("tensor", lambda e, cko=cko, ns=ns, ck=ck, bt=bt: e.matmul(
                        self.psf(bt, 512),
                        wproj[:, ck * C + cko * 128:ck * C + (cko + 1) * 128],
                        yD[:, ck * T + ns * 512:ck * T + (ns + 1) * 512],
                        start=(ck == 0), stop=(ck == NC - 1)),
                        inc=(tT if ck == NC - 1 else None))
                nmm = tT.n
                P.wait("vector", tT, nmm)
                cv = ffn_C.rearrange("p (ck t) -> p ck t", ck=NC)[:, cko,
                                                                  ns * 512:(ns + 1) * 512]
                bank_free_tV[par] = P.add("vector", lambda e, cko=cko, cv=cv, bt=bt:
                                          e.scalar_tensor_tensor(
                                              cv, self.psf(bt, 512),
                                              self.bproj_pp[:, cko:cko + 1], cv,
                                              ALU.add, ALU.add), inc=tV)
        comb_tV = tV.n

        self.x1_tile_ready = []
        packs = [self.pbC, self.pbD]
        pack_free_tV = [0, 0]
        x1s_free = [0, 0]
        xt_slots = [self.scr[:, 0:1024], self.scr[:, 1024:2048]]
        xt_read_tV = [0, 0]
        P.wait("tensor", tV, comb_tV)
        P.wait("sync", tV, comb_tV)
        P.wait("sync", tA, tA.n)
        for n in range(NT):
            par = n % 2
            P.wait("sync", tV, xt_read_tV[par])
            ldn = P.add("sync", lambda e, n=n, par=par, x_dram=x_dram: e.dma_start(
                out=xt_slots[par], in_=x_dram[n * 128:(n + 1) * 128, :]),
                inc=qS, k=16)
            pk = packs[par]
            P.wait("tensor", tV, pack_free_tV[par])
            for ck in range(NC):
                P.add("tensor", lambda e, n=n, ck=ck, pk=pk: e.transpose(
                    self.psb(pk, 128, base=ck * 128),
                    ffn_C[:, ck * T + n * 128:ck * T + (n + 1) * 128],
                    self.ident_bf), inc=(tT if ck == NC - 1 else None))
            ntr = tT.n
            P.wait("vector", tT, ntr)
            P.wait("vector", qS, ldn)
            P.wait("vector", qXo, x1s_free[par])
            nadd = P.add("vector", lambda e, par=par, pk=pk: e.tensor_add(
                x1s[par], xt_slots[par], self.psb(pk, 1024)), inc=tV)
            pack_free_tV[par] = nadd
            xt_read_tV[par] = nadd
            P.wait("sync", tV, nadd)
            nst = P.add("sync", lambda e, n=n, par=par, xo_dram=xo_dram: e.dma_start(
                out=xo_dram[n * 128:(n + 1) * 128, :], in_=x1s[par]), inc=qXo, k=16)
            x1s_free[par] = nst
            self.x1_tile_ready.append((qXo, nst))

    def _qkv1(self):
        P = self.P
        qG, qS, tV, tA, tT = (P.sem(s) for s in ("qG", "qS", "tV", "tA", "tT"))
        qQKV = P.sem("qQKV")
        wq1 = self.av(112 * KB, 128, NC * HD, BF16)
        wk1 = self.av(113 * KB, 128, NC * HD, BF16)
        wv1 = self.av(114 * KB, 128, NC * HD, BF16)
        v1t = self.av(115 * KB, 128, NT * HD, BF16)
        q1 = self.av(117 * KB, 64, T, BF16)
        k1 = self.av(121 * KB, 64, T, BF16)
        v1f = self.av(125 * KB, 64, T, BF16)

        P.wait("gpsimd", tT, self.attn_mm_done)
        nl = None
        for w, name in ((wq1, "w_q1"), (wk1, "w_k1"), (wv1, "w_v1")):
            nl = P.add("gpsimd", lambda e, w=w, a=self.g[name]: e.dma_start(
                out=w, in_=a), inc=qG, k=16)
        P.wait("tensor", tV, P.sem("tV").n)
        P.wait("tensor", tA, self.ln_tA)
        P.wait("tensor", qG, nl)
        banks = [self.pbA, self.pbB]
        bank_free_tV = [0, 0]
        outs = [(wq1, q1, 0), (wk1, k1, 1), (wv1, v1f, 2)]
        for wi, (w, dstf, bcol) in enumerate(outs):
            for ns in range(4):
                par = ns % 2
                bt = banks[par]
                P.wait("tensor", tV, bank_free_tV[par])
                for ck in range(NC):
                    P.add("tensor", lambda e, w=w, ck=ck, ns=ns, bt=bt: e.matmul(
                        self.psf(bt, 512, parts=64),
                        w[:, ck * HD:(ck + 1) * HD],
                        self.xnb[:, ck * T + ns * 512:ck * T + (ns + 1) * 512],
                        start=(ck == 0), stop=(ck == NC - 1)),
                        inc=(tT if ck == NC - 1 else None))
                nmm = tT.n
                P.wait("vector", tT, nmm)
                bank_free_tV[par] = P.add("vector", lambda e, dstf=dstf, ns=ns,
                                          bcol=bcol, bt=bt: e.tensor_scalar(
                    dstf[:, ns * 512:(ns + 1) * 512], self.psf(bt, 512, parts=64),
                    self.bqkv1_pp[:, bcol:bcol + 1], None, ALU.add), inc=tV)
        qkv_tV = tV.n
        P.wait("tensor", tV, qkv_tV)
        for gph in range(16):
            P.add("tensor", lambda e, gph=gph: e.transpose(
                self.psb(self.pbC, 64, base=gph * 64),
                v1f[:, gph * 128:(gph + 1) * 128], self.ident_bf[0:64, 0:64]),
                inc=(tT if gph == 15 else None))
        nvt = tT.n
        P.wait("scalar", tT, nvt)
        nvta = P.add("scalar", lambda e: e.activation(v1t, self.psb(self.pbC, 1024),
                                                      AF.Copy), inc=tA)
        P.wait("sync", tV, qkv_tV)
        P.add("sync", lambda e: e.dma_start(out=self.qo, in_=q1), inc=qQKV, k=16)
        P.add("sync", lambda e: e.dma_start(out=self.ko, in_=k1), inc=qQKV, k=16)
        P.wait("sync", tA, nvta)
        P.add("sync", lambda e: e.dma_start(out=self.vo, in_=v1t), inc=qQKV, k=16)
        self.qkv1_pbc_tA = nvta

    def _gate(self):
        P = self.P
        tV, tA, tT, qG = P.sem("tV"), P.sem("tA"), P.sem("tT"), P.sem("qG")
        gs = self.gsc
        P.add("vector", lambda e: e.memset(
            self.rw_tm.rearrange("p (n e) -> p n e", e=NEXP)[:, :, 8], 1.0))
        P.wait("vector", tV, self.gbb_done)
        P.wait("tensor", qG, self.n_const_g)
        exp_tA = 0
        rwfm_tV = 0
        for n in range(NT):
            P.wait("tensor", tA, exp_tA)
            P.wait("tensor", tA, self.qkv1_pbc_tA if n == 0 else 0)
            for ck in range(NC):
                P.add("tensor", lambda e, n=n, ck=ck: e.matmul(
                    self.psf(self.pbD, 8),
                    self.xnb[:, ck * T + n * 128:ck * T + (n + 1) * 128],
                    self.wgate[:, ck * 8:(ck + 1) * 8],
                    start=(ck == 0), stop=(ck == NC - 1)),
                    inc=(tT if ck == NC - 1 else None))
            nmm = tT.n
            P.wait("vector", tT, nmm)
            nng = P.add("vector", lambda e: e.reduce_max(
                gs[:, 24:25], self.psf(self.pbD, 8), axis=AX.X, negate=True), inc=tV)
            P.wait("scalar", tV, nng)
            exp_tA = P.add("scalar", lambda e: e.activation(
                gs[:, 0:8], self.psf(self.pbD, 8), AF.Exp, bias=gs[:, 24:25],
                accum_out=gs[:, 27:28]), inc=tA)
            P.wait("vector", tA, exp_tA)
            P.addf("vector", lambda e: e.reciprocal(gs[:, 28:29], gs[:, 27:28]))
            P.addf("vector", lambda e: e.tensor_scalar(
                gs[:, 0:8], gs[:, 0:8], gs[:, 28:29], None, ALU.mult))
            P.addf("vector", lambda e: e.tensor_add(gs[:, 8:16], gs[:, 0:8], self.gbb))
            P.addf("vector", lambda e: e.reduce_max(gs[:, 24:25], gs[:, 8:16], axis=AX.X))
            P.addf("vector", lambda e: e.tensor_scalar(
                gs[:, 16:24], gs[:, 8:16], gs[:, 24:25], None, ALU.is_ge))
            P.addf("vector", lambda e: e.scalar_tensor_tensor(
                gs[:, 16:24], gs[:, 16:24], -1e30, gs[:, 8:16], ALU.mult, ALU.add))
            P.addf("vector", lambda e: e.reduce_max(gs[:, 25:26], gs[:, 16:24], axis=AX.X))
            P.addf("vector", lambda e: e.tensor_scalar(
                gs[:, 16:24], gs[:, 8:16], gs[:, 25:26], None, ALU.is_ge))
            rw = self.rw_tm[:, n * NEXP:n * NEXP + 8]
            nrw = P.add("vector", lambda e, rw=rw: e.tensor_mul(
                rw, gs[:, 0:8], gs[:, 16:24]), inc=tV)
            P.wait("tensor", tV, nrw)
            P.wait("tensor", tV, rwfm_tV)
            ntr = P.add("tensor", lambda e, n=n: e.transpose(
                self.psf(self.pbC, 128, parts=NEXP),
                self.rw_tm[:, n * NEXP:(n + 1) * NEXP], self.ident_f32), inc=tT)
            P.wait("vector", tT, ntr)
            rwfm_tV = P.add("vector", lambda e, n=n: e.tensor_copy(
                self.rw_fm[:, n * 128:(n + 1) * 128],
                self.psf(self.pbC, 128, parts=NEXP)), inc=tV)
        self.gate_tV = tV.n
        self.gate_mm_tT = tT.n
        self.dbg_store("rw", self.rw_tm, (128, NT * NEXP), F32, [(tV, tV.n)])

    def _moe(self):
        P = self.P
        qG, tV, tA, tT = P.sem("qG"), P.sem("tV"), P.sem("tA"), P.sem("tT")
        qM, qXo = P.sem("qM"), P.sem("qXo")
        h_buf = self.av(0, 128, NHK * T, BF16)
        blk = [[self.av(44 * KB + (i * 2 + p) * 2 * KB, 128, 1024, BF16)
                for p in range(2)] for i in range(2)]
        w2 = self.av(52 * KB, 128, NHK * C, BF16)
        stg = [self.av(74 * KB, 128, C, F32), self.av(78 * KB, 128, C, F32)]

        tile_chain = [self.x1_tile_ready[n] for n in range(NT)]
        P.wait("gpsimd", qXo, P.sem("qXo").n)

        P.wait("tensor", tV, self.gate_tV)
        P.wait("tensor", qG, self.n_const_g)
        stg_free = [0, 0]
        psq_free = [0, 0, 0, 0]
        for n in range(NT):
            par = n % 2
            for cs in range(2):
                qi = par * 2 + cs
                P.wait("tensor", tV, psq_free[qi])
                P.add("tensor", lambda e, n=n, cs=cs, qi=qi: e.matmul(
                    self.psf(self.ps_s, 512, base=qi * 512),
                    self.rw_fm[:, n * 128:(n + 1) * 128],
                    self.b2cat[:, cs * 512:(cs + 1) * 512],
                    start=True, stop=True), inc=(tT if cs == 1 else None))
            nmm = tT.n
            P.wait("vector", tT, nmm)
            P.wait("vector", qM, stg_free[par])
            ncp = None
            for cs in range(2):
                qi = par * 2 + cs
                psq_free[qi] = P.add("vector", lambda e, par=par, cs=cs, qi=qi:
                                     e.tensor_copy(stg[par][:, cs * 512:(cs + 1) * 512],
                                                   self.psf(self.ps_s, 512, base=qi * 512)),
                                     inc=tV)
            ncp = psq_free[par * 2 + 1]
            P.wait("gpsimd", tV, ncp)
            cs_, cn_ = tile_chain[n]
            P.wait("gpsimd", cs_, cn_)
            nadd = P.add("gpsimd", lambda e, n=n, par=par: e.dma_start(
                out=self.xo[n * 128:(n + 1) * 128, :], in_=stg[par],
                accum_op=ALU.add), inc=qM, k=16)
            stg_free[par] = nadd
            tile_chain[n] = (qM, nadd)

        hv = h_buf.rearrange("p (hk t) -> p hk t", hk=NHK)
        w2v = w2.rearrange("p (hk c) -> p hk c", hk=NHK)
        esA = [self.scr[:, 0:512], self.scr[:, 512:1024]]
        esB = [self.scr[:, 1024:1536], self.scr[:, 1536:2048]]
        blk_free = [[0, 0], [0, 0]]
        w2_free = 0
        h_free = 0
        bankA = [(self.pbA, self.pbB), (self.pbC, self.pbD)]
        h1_free = [0, 0]
        h3_free = [0, 0]
        es_free = [0, 0]
        psh_free = [0, 0]
        P.wait("tensor", tV, tV.n)
        P.wait("tensor", tA, tA.n)
        for e_i in range(NEXP):
            P.wait("gpsimd", tT, w2_free)
            nw2 = None
            for hk in range(NHK):
                nw2 = P.add("gpsimd", lambda e, e_i=e_i, hk=hk: e.dma_start(
                    out=w2v[:, hk], in_=self.g["w2b"][e_i * NHK + hk]),
                    inc=qG, k=16)
            w2_loaded = nw2
            P.wait("scalar", tT, h_free)
            for hk in range(NHK):
                par = hk % 2
                P.wait("gpsimd", tT, blk_free[0][par])
                n1 = P.add("gpsimd", lambda e, e_i=e_i, hk=hk, par=par: e.dma_start(
                    out=blk[0][par], in_=self.g["w1b"][e_i * NHK + hk]), inc=qG, k=16)
                P.wait("gpsimd", tT, blk_free[1][par])
                n3 = P.add("gpsimd", lambda e, e_i=e_i, hk=hk, par=par: e.dma_start(
                    out=blk[1][par], in_=self.g["w3b"][e_i * NHK + hk]), inc=qG, k=16)
                bt1, bt3 = bankA[par]
                P.wait("tensor", qG, n3)
                for ns in range(4):
                    P.wait("tensor", tV, h1_free[par])
                    for ck in range(NC):
                        P.add("tensor", lambda e, hk=hk, ns=ns, ck=ck, par=par, bt1=bt1:
                              e.matmul(self.psf(bt1, 512),
                                       blk[0][par][:, ck * 128:(ck + 1) * 128],
                                       self.xnb[:, ck * T + ns * 512:ck * T + (ns + 1) * 512],
                                       start=(ck == 0), stop=(ck == NC - 1)),
                              inc=(tT if ck == NC - 1 else None))
                    n1mm = tT.n
                    P.wait("tensor", tA, h3_free[par])
                    for ck in range(NC):
                        P.add("tensor", lambda e, hk=hk, ns=ns, ck=ck, par=par, bt3=bt3:
                              e.matmul(self.psf(bt3, 512),
                                       blk[1][par][:, ck * 128:(ck + 1) * 128],
                                       self.xnb[:, ck * T + ns * 512:ck * T + (ns + 1) * 512],
                                       start=(ck == 0), stop=(ck == NC - 1)),
                              inc=(tT if ck == NC - 1 else None))
                    n3mm = tT.n
                    bcol = e_i * NHK + hk
                    P.wait("vector", tT, n1mm)
                    P.wait("vector", tA, es_free[par])
                    nt1 = P.add("vector", lambda e, par=par, bcol=bcol, bt1=bt1:
                                e.tensor_scalar(esA[par], self.psf(bt1, 512),
                                                self.b1_pp[:, bcol:bcol + 1], None,
                                                ALU.add), inc=tV)
                    P.wait("vector", tV, nt1)
                    h1_free[par] = nt1
                    P.wait("scalar", tT, n3mm)
                    nt2 = P.add("scalar", lambda e, par=par, bcol=bcol, bt3=bt3:
                                e.activation(esB[par], self.psf(bt3, 512), AF.Identity,
                                             bias=self.b3_pp[:, bcol:bcol + 1]),
                                inc=tA)
                    h3_free[par] = nt2
                    P.wait("vector", tA, nt2)
                    nt3 = P.add("vector", lambda e, par=par: e.tensor_mul(
                        esA[par], esA[par], esB[par]), inc=tV)
                    P.wait("scalar", tV, nt3)
                    nsl = P.add("scalar", lambda e, hk=hk, ns=ns, par=par:
                                e.activation(hv[:, hk, ns * 512:(ns + 1) * 512],
                                             esA[par], AF.Silu), inc=tA)
                    es_free[par] = nsl
                blk_free[0][par] = n1mm
                blk_free[1][par] = n3mm
            silu_all = P.sem("tA").n
            P.wait("tensor", tA, silu_all)
            P.wait("tensor", qG, w2_loaded)
            for tq in range(NT):
                par = tq % 2
                P.wait("tensor", tV, psh_free[par])
                for hk in range(NHK):
                    for cs in range(2):
                        P.add("tensor", lambda e, tq=tq, hk=hk, cs=cs, par=par:
                              e.matmul(self.psf(self.ps_s, 512, base=par * 1024 + cs * 512),
                                       hv[:, hk, tq * 128:(tq + 1) * 128],
                                       w2v[:, hk, cs * 512:(cs + 1) * 512],
                                       start=(hk == 0), stop=(hk == NHK - 1)),
                              inc=(tT if hk == NHK - 1 and cs == 1 else None))
                nmm = tT.n
                P.wait("vector", tT, nmm)
                P.wait("vector", qM, stg_free[par])
                rwcol = self.rw_tm[:, tq * NEXP + e_i:tq * NEXP + e_i + 1]
                nev = None
                for cs in range(2):
                    nev = P.add("vector", lambda e, par=par, cs=cs, rwcol=rwcol:
                                e.tensor_scalar(stg[par][:, cs * 512:(cs + 1) * 512],
                                                self.psf(self.ps_s, 512,
                                                         base=par * 1024 + cs * 512),
                                                rwcol, None, ALU.mult), inc=tV)
                psh_free[par] = nev
                P.wait("gpsimd", tV, nev)
                cs_, cn_ = tile_chain[tq]
                P.wait("gpsimd", cs_, cn_)
                nadd = P.add("gpsimd", lambda e, tq=tq, par=par: e.dma_start(
                    out=self.xo[tq * 128:(tq + 1) * 128, :], in_=stg[par],
                    accum_op=ALU.add), inc=qM, k=16)
                stg_free[par] = nadd
                tile_chain[tq] = (qM, nadd)
            w2_free = tT.n
            h_free = tT.n
        self.x1_tile_ready = tile_chain



class BuildB:

    def __init__(self):
        nc = bass.Bass("TRN2", target_bir_lowering=False)
        self.nc = nc
        P = Prog(nc)
        self.P = P
        xin = nc.dram_tensor("xin", [T, C], F32, kind="ExternalInput").ap()
        qin = nc.dram_tensor("q", [HD, T], BF16, kind="ExternalInput").ap()
        kall = nc.dram_tensor("kall", [HD, GTOK], BF16, kind="ExternalInput").ap()
        vall = nc.dram_tensor("vall", [128, (GTOK // 128) * HD], BF16,
                              kind="ExternalInput").ap()
        identd = nc.dram_tensor("ident_bf", [128, 128], BF16,
                                kind="ExternalInput").ap()
        out = nc.dram_tensor("out", [T, C], F32, kind="ExternalOutput").ap()

        sbt = nc.alloc_sbuf_tensor
        ident = sbt("ident", [128, 128], BF16).ap()
        q1 = sbt("q1", [HD, T], BF16).ap()
        ka = sbt("ka", [HD, GTOK], BF16).ap()
        va = sbt("va", [128, (GTOK // 128) * HD], BF16).ap()
        Pt = sbt("Pt", [128, 2 * 512], BF16).ap()
        pt_sb = sbt("pt_sb", [128, 2 * 512], BF16).ap()
        acc = sbt("acc", [128, HD], F32).ap()
        yt = sbt("yt", [128, HD], F32).ap()
        stt = sbt("stt", [128, 24], F32).ap()
        xts = sbt("xts", [128, 2 * C], F32).ap()
        ots = sbt("ots", [128, 2 * C], F32).ap()

        ps_sc = nc.alloc_psum_tensor("ps_sc", [128, 1024], F32)
        pv = nc.alloc_psum_tensor("pv", [128, 512], F32)
        ptp = [nc.alloc_psum_tensor("ptp0", [128, 512], F32),
               nc.alloc_psum_tensor("ptp1", [128, 512], F32)]

        qS, qO, tV, tA, tT = (P.sem(s) for s in ("qS", "qO", "tV", "tA", "tT"))

        for t, a in ((ident, identd), (q1, qin), (ka, kall), (va, vall)):
            P.add("sync", lambda e, t=t, a=a: e.dma_start(out=t, in_=a), inc=qS, k=16)
        n_ld = qS.n
        P.wait("tensor", qS, n_ld)

        NCH = GTOK // 512
        xt_read = [0, 0]
        ot_free = [0, 0]
        for qt in range(NT):
            par_t = qt % 2
            P.wait("sync", tV, xt_read[par_t])
            ldx = P.add("sync", lambda e, qt=qt, par_t=par_t: e.dma_start(
                out=xts[:, par_t * C:(par_t + 1) * C],
                in_=xin[qt * 128:(qt + 1) * 128, :]), inc=qS, k=16)

            nsc = [0] * (NCH + 1)
            ntr = [0] * (NCH + 1)
            npv = [0] * (NCH + 1)
            nexp = [0] * (NCH + 1)
            nev = [0] * (NCH + 1)
            nevS = [None] * (NCH + 1)
            nacc = [0] * (NCH + 1)
            nb = [0] * (NCH + 1)

            def emit_scores(ci):
                par = ci % 2
                nsc[ci] = P.add("tensor", lambda e, qt=qt, ci=ci, par=par: e.matmul(
                    ps_sc.ap()[:, par * 512:(par + 1) * 512],
                    q1[:, qt * 128:(qt + 1) * 128],
                    ka[:, ci * 512:(ci + 1) * 512], start=True, stop=True), inc=tT)

            P.wait("tensor", tA, tA.n)
            P.wait("tensor", tV, tV.n)
            emit_scores(0)
            for ci in range(NCH + 1):
                par = ci % 2
                so = 8 + 8 * par
                if ci < NCH:
                    P.wait("vector", tT, nsc[ci])
                    P.addf("vector", lambda e, par=par, so=so: e.reduce_max(
                        stt[:, so:so + 1], ps_sc.ap()[:, par * 512:(par + 1) * 512],
                        axis=AX.X, negate=True))
                    if ci == 0:
                        P.addf("vector", lambda e, so=so: e.tensor_copy(
                            stt[:, 0:1], stt[:, so:so + 1]))
                        nb[ci] = P.add("vector", lambda e, so=so: e.tensor_scalar(
                            stt[:, so + 2:so + 3], stt[:, 0:1], 1.0 / 32.0, None,
                            ALU.mult), inc=tV)
                    else:
                        P.addf("vector", lambda e, so=so: e.tensor_scalar(
                            stt[:, so + 1:so + 2], stt[:, so:so + 1], stt[:, 0:1],
                            0.0, ALU.subtract, ALU.min))
                        nb[ci] = P.add("vector", lambda e, so=so: e.tensor_scalar(
                            stt[:, so + 2:so + 3], stt[:, so + 1:so + 2],
                            stt[:, 0:1], 1.0 / 32.0, ALU.add, ALU.mult), inc=tV)
                        P.addf("vector", lambda e, so=so: e.tensor_scalar(
                            stt[:, 0:1], stt[:, so + 1:so + 2], stt[:, 0:1], None,
                            ALU.add))
                if ci < NCH:
                    P.wait("scalar", tV, nb[ci])
                    if ci > 0:
                        P.add("scalar", lambda e, so=so: e.activation(
                            stt[:, so + 3:so + 4], stt[:, so + 1:so + 2], AF.Exp,
                            scale=1.0 / 32.0))
                    if ci >= 2:
                        P.wait("scalar", tT, ntr[ci - 2])
                    nexp[ci] = P.add("scalar", lambda e, par=par, so=so: e.activation(
                        Pt[:, par * 512:(par + 1) * 512],
                        ps_sc.ap()[:, par * 512:(par + 1) * 512], AF.Exp,
                        scale=1.0 / 32.0, bias=stt[:, so + 2:so + 3],
                        accum_out=stt[:, so + 4:so + 5]), inc=tA)
                if ci + 1 < NCH:
                    if ci >= 1:
                        P.wait("tensor", tA, nexp[ci - 1])
                    emit_scores(ci + 1)
                if ci < NCH:
                    P.wait("tensor", tA, nexp[ci])
                    if ci >= 2:
                        P.wait("tensor", nevS[ci - 2], nev[ci - 2])
                    for kt in range(4):
                        P.add("tensor", lambda e, kt=kt, par=par: e.transpose(
                            ptp[par].ap().bitcast(BF16)[:, kt * 128:(kt + 1) * 128],
                            Pt[:, par * 512 + kt * 128:par * 512 + (kt + 1) * 128],
                            ident), inc=(tT if kt == 3 else None))
                    ntr[ci] = tT.n
                if ci >= 1:
                    cj = ci - 1
                    pj = cj % 2
                    P.wait("tensor", nevS[cj], nev[cj])
                    P.wait("tensor", tV, nacc[cj - 1] if cj >= 1 else 0)
                    for kt in range(4):
                        gk = cj * 4 + kt
                        P.add("tensor", lambda e, gk=gk, kt=kt, pj=pj: e.matmul(
                            pv.ap()[:, 0:HD],
                            pt_sb[:, pj * 512 + kt * 128:pj * 512 + (kt + 1) * 128],
                            va[:, gk * HD:(gk + 1) * HD],
                            start=(kt == 0), stop=(kt == 3)),
                            inc=(tT if kt == 3 else None))
                    npv[cj] = tT.n
                if ci < NCH:
                    if par == 0:
                        P.wait("vector", tT, ntr[ci])
                        if ci >= 2:
                            P.wait("vector", tT, npv[ci - 2])
                        nev[ci] = P.add("vector", lambda e, par=par: e.tensor_copy(
                            pt_sb[:, par * 512:(par + 1) * 512],
                            ptp[par].ap().bitcast(BF16)[:, 0:512]), inc=tV)
                        nevS[ci] = tV
                    else:
                        P.wait("scalar", tT, ntr[ci])
                        if ci >= 2:
                            P.wait("scalar", tT, npv[ci - 2])
                        nev[ci] = P.add("scalar", lambda e, par=par: e.activation(
                            pt_sb[:, par * 512:(par + 1) * 512],
                            ptp[par].ap().bitcast(BF16)[:, 0:512], AF.Copy),
                            inc=tA)
                        nevS[ci] = tA
                if ci < NCH:
                    P.wait("vector", tA, nexp[ci])
                    if ci == 0:
                        P.addf("vector", lambda e, so=so: e.tensor_copy(
                            stt[:, 1:2], stt[:, so + 4:so + 5]))
                    else:
                        P.addf("vector", lambda e, so=so: e.scalar_tensor_tensor(
                            stt[:, 1:2], stt[:, 1:2], stt[:, so + 3:so + 4],
                            stt[:, so + 4:so + 5], ALU.mult, ALU.add))
                if ci >= 1:
                    cj = ci - 1
                    soj = 8 + 8 * (cj % 2)
                    P.wait("vector", tT, npv[cj])
                    if cj == 0:
                        nacc[cj] = P.add("vector", lambda e: e.tensor_copy(
                            acc, pv.ap()[:, 0:HD]), inc=tV)
                    else:
                        nacc[cj] = P.add("vector", lambda e, soj=soj:
                                         e.scalar_tensor_tensor(
                                             acc, acc, stt[:, soj + 3:soj + 4],
                                             pv.ap()[:, 0:HD], ALU.mult, ALU.add),
                                         inc=tV)
                    P.wait("vector", tV, nacc[cj])
            P.addf("vector", lambda e: e.reciprocal(stt[:, 2:3], stt[:, 1:2]))
            nyt = P.addf("vector", lambda e: e.tensor_scalar(
                yt, acc, stt[:, 2:3], None, ALU.mult))
            P.wait("vector", qS, ldx)
            P.wait("vector", qO, ot_free[par_t])
            ot = ots[:, par_t * C:(par_t + 1) * C]
            not_ = P.add("vector", lambda e, par_t=par_t, ot=ot: e.tensor_add(
                ot.rearrange("p (r d) -> p r d", d=HD),
                xts[:, par_t * C:(par_t + 1) * C].rearrange("p (r d) -> p r d", d=HD),
                yt.unsqueeze(1).broadcast_to([128, NHEAD, HD])), inc=tV)
            xt_read[par_t] = not_
            P.wait("sync", tV, not_)
            nst = P.add("sync", lambda e, qt=qt, ot=ot: e.dma_start(
                out=out[qt * 128:(qt + 1) * 128, :], in_=ot), inc=qO, k=16)
            ot_free[par_t] = nst
        P.wait("sync", qO, P.sem("qO").n)
        P.finalize()



I32 = mybir.dt.int32
MAGIC = 777777
MAXPOLL = 100000


class BuildF(BuildA):

    def __init__(self, dbgF=False):
        self.stop = "all"
        self.dbg = set()
        self.dbgF = dbgF
        nc = bass.Bass("TRN2", target_bir_lowering=False)
        self.nc = nc
        self.P = Prog(nc)
        self.xmine = nc.dram_tensor("xmine", [4 * T, C], F32,
                                    kind="ExternalInput").ap()
        self.g = {}
        for name, shape, dt in GLOBAL_SPECS:
            self.g[name] = nc.dram_tensor(name, list(shape), dt,
                                          kind="ExternalInput").ap()
        self.out_d = nc.dram_tensor("out", [T, C], F32, kind="ExternalOutput").ap()
        self.xo = nc.dram_tensor("xo", [T, C], F32, kind="Internal").ap()
        self.x1tmp = nc.dram_tensor("x1tmp", [T, C], F32, kind="Internal").ap()
        self.xn0_d = nc.dram_tensor("xn0_d", [128, NC * T], BF16,
                                    kind="Internal").ap()
        self.q1_d = nc.dram_tensor("q1_d", [HD, T], BF16, kind="Internal").ap()
        self.sh_k = nc.dram_tensor("sh_k", [2, 4, HD, T], BF16, kind="Internal",
                                   addr_space="Shared")
        self.sh_v = nc.dram_tensor("sh_v", [2, 4, 128, NT * HD], BF16,
                                   kind="Internal", addr_space="Shared")
        self.sh_f = nc.dram_tensor("sh_f", [2, 4, 128], I32, kind="Internal",
                                   addr_space="Shared")
        self.fwm = nc.alloc_sbuf_tensor("fwm", [1, 128], I32).ap()
        self.fpoll = nc.alloc_sbuf_tensor("fpoll", [1, 128], I32).ap()
        self.fwz = nc.alloc_sbuf_tensor("fwz", [4, 128], I32).ap()
        self._setup_sbuf()
        self._buildF()

    def _x_slice(self, s):
        return self.xmine[s * T:(s + 1) * T, :]

    def _buildF(self):
        P = self.P
        self._load_consts()
        self.fwm_ready = P.add("vector", lambda e: e.memset(self.fwm, MAGIC),
                               inc=P.sem("tV"))
        nz = P.add("vector", lambda e: e.memset(self.fwz, 0), inc=P.sem("tV"))
        P.wait("gpsimd", P.sem("tV"), nz)

        def zero_flags(e):
            engs_ = OrderedEngineSet([mybir.EngineType.Pool])
            if not hasattr(self, "_regs"):
                nc_ = self.nc
                r = {}
                for nmq in ("half", "phalf", "koff", "voff", "foff"):
                    r[nmq] = nc_.alloc_registers("r_" + nmq,
                                                 engines=engs_).handles[0]
                pid = e.partition_id()
                e.reg_mov(r["half"], pid)
                e.reg_alu(r["half"], r["half"], 1, ALU.bitwise_and)
                e.reg_mov(r["phalf"], r["half"])
                e.reg_alu(r["phalf"], r["phalf"], 1, ALU.bitwise_xor)
                self._regs = r
            r = self._regs
            e.reg_mov(r["foff"], r["half"])
            e.reg_alu(r["foff"], r["foff"], 4 * 128, ALU.mult)
            fdst = bass.AP(self.sh_f, r["foff"], [[128, 4], [1, 128]])
            return e.dma_start(out=fdst, in_=self.fwz)

        self.flags_zeroed = P.add("gpsimd", lambda e: zero_flags(e),
                                  inc=P.sem("qSh"), k=16)
        self._kv_share_done = None
        self._xnb_guard = None
        for s in range(4):
            self._ln(src_kind="dram", dram=self._x_slice(s), ready=None)
            self._ffn()
            self._attn()
            xdst = self.xo if s == 0 else self.x1tmp
            self._proj_resid(x_dram=self._x_slice(s), xo_dram=xdst)
            self._ln(src_kind="dram", dram=xdst, ready=self.x1_tile_ready)
            if s == 0:
                self._x1ready_slot0 = list(self.x1_tile_ready)
                self._q_proj_store()
                self.qkv1_pbc_tA = P.sem("tA").n
                self._gate()
                self._store_xn0()
            self._kv_share(s)
        self._reload_xn0()
        self.x1_tile_ready = self._x1ready_slot0
        self._moe()
        self._poll_and_load()
        self._flash_final()
        if self.dbgF:
            st = P.sem("st")
            for name, src, shape, dt in [
                ("kall", self.K_all, (HD, NCORE * T), BF16),
                ("vall", self.V_all, (128, (NCORE * T // 128) * HD), BF16),
                ("q1s", self.q1s, (HD, T), BF16),
                ("xnbf", self.xnb, (128, NC * T), BF16),
            ]:
                od = self.nc.dram_tensor("dbg_" + name, list(shape), dt,
                                         kind="ExternalOutput").ap()
                P.wait("sync", P.sem("tV"), P.sem("tV").n)
                P.add("sync", lambda e, o=od, a=src: e.dma_start(out=o, in_=a),
                      inc=st, k=16)
            if True:
                odf = self.nc.dram_tensor("dbg_flags", [2, 4, 128], I32,
                                          kind="ExternalOutput").ap()
                P.add("sync", lambda e, o=odf: e.dma_start(
                    out=o, in_=bass.AP(self.sh_f, 0, [[512, 1], [1, 1024]]).rearrange(
                        "a (h s t) -> (a h) s t", h=2, s=4)), inc=st, k=16)
                odp = self.nc.dram_tensor("dbg_poll", [1, 128], I32,
                                          kind="ExternalOutput").ap()
                P.add("sync", lambda e, o=odp: e.dma_start(out=o, in_=self.fpoll),
                      inc=st, k=16)
            if hasattr(self, "_fdump"):
                od = self.nc.dram_tensor("dbg_fstats", [128, NT * 4], F32,
                                         kind="ExternalOutput").ap()
                P.wait("sync", P.sem("tV"), P.sem("tV").n)
                P.add("sync", lambda e, o=od: e.dma_start(out=o, in_=self._fdump),
                      inc=st, k=16)
            od = self.nc.dram_tensor("dbg_xo", [T, C], F32,
                                     kind="ExternalOutput").ap()
            qM_ = P.sem("qM")
            P.wait("sync", qM_, qM_.n)
            P.add("sync", lambda e, o=od: e.dma_start(out=o, in_=self.xo),
                  inc=st, k=16)
            P.wait("sync", st, st.n)
        qM = P.sem("qM")
        if qM.n:
            P.wait("gpsimd", qM, qM.n)
        P.finalize()

    def _q_proj_store(self):
        P = self.P
        qG, qS, tV, tT, tA = (P.sem(s) for s in ("qG", "qS", "tV", "tT", "tA"))
        wq1 = self.av(112 * KB, 128, NC * HD, BF16)
        q1 = self.av(117 * KB, 64, T, BF16)
        P.wait("gpsimd", tT, self.attn_mm_done)
        nl = P.add("gpsimd", lambda e: e.dma_start(out=wq1, in_=self.g["w_q1"]),
                   inc=qG, k=16)
        P.wait("tensor", tV, P.sem("tV").n)
        P.wait("tensor", tA, self.ln_tA)
        P.wait("tensor", qG, nl)
        banks = [self.pbA, self.pbB]
        bank_free_tV = [0, 0]
        for ns in range(4):
            par = ns % 2
            bt = banks[par]
            P.wait("tensor", tV, bank_free_tV[par])
            for ck in range(NC):
                P.add("tensor", lambda e, ck=ck, ns=ns, bt=bt: e.matmul(
                    self.psf(bt, 512, parts=64),
                    wq1[:, ck * HD:(ck + 1) * HD],
                    self.xnb[:, ck * T + ns * 512:ck * T + (ns + 1) * 512],
                    start=(ck == 0), stop=(ck == NC - 1)),
                    inc=(tT if ck == NC - 1 else None))
            nmm = tT.n
            P.wait("vector", tT, nmm)
            bank_free_tV[par] = P.add("vector", lambda e, ns=ns, bt=bt:
                                      e.tensor_scalar(
                q1[:, ns * 512:(ns + 1) * 512], self.psf(bt, 512, parts=64),
                self.bqkv1_pp[:, 0:1], None, ALU.add), inc=tV)
        nq = bank_free_tV[1]
        P.wait("sync", tV, nq)
        self.q1_store = P.add("sync", lambda e, q1=q1: e.dma_start(
            out=self.q1_d, in_=q1), inc=P.sem("qQ"), k=16)

    def _store_xn0(self):
        P = self.P
        qG, tA = P.sem("qG"), P.sem("tA")
        P.wait("gpsimd", tA, self.ln_tA)
        self.xn0_store = P.add("gpsimd", lambda e: e.dma_start(
            out=self.xn0_d, in_=self.xnb), inc=qG, k=16)
        self._xnb_guard = (qG, self.xn0_store)

    def _kv_share(self, s):
        P = self.P
        qG, qS, tV, tA, tT = (P.sem(x) for x in ("qG", "qS", "tV", "tA", "tT"))
        qSh = P.sem("qSh")
        wk1 = self.av(113 * KB, 128, NC * HD, BF16)
        wv1 = self.av(114 * KB, 128, NC * HD, BF16)
        kst = self.av(121 * KB, 64, T, BF16)
        vfst = self.av(125 * KB, 64, T, BF16)
        vtst = self.av(115 * KB, 128, NT * HD, BF16)
        P.wait("gpsimd", tT, self.attn_mm_done)
        nl = None
        for w, name in ((wk1, "w_k1"), (wv1, "w_v1")):
            nl = P.add("gpsimd", lambda e, w=w, a=self.g[name]: e.dma_start(
                out=w, in_=a), inc=qG, k=16)
        P.wait("tensor", tV, P.sem("tV").n)
        P.wait("tensor", tA, self.ln_tA)
        P.wait("tensor", qG, nl)
        banks = [self.pbA, self.pbB]
        bank_free_tV = [0, 0]
        for wi, (w, dstf, bcol) in enumerate(((wk1, kst, 1), (wv1, vfst, 2))):
            for ns in range(4):
                par = ns % 2
                bt = banks[par]
                P.wait("tensor", tV, bank_free_tV[par])
                for ck in range(NC):
                    P.add("tensor", lambda e, w=w, ck=ck, ns=ns, bt=bt: e.matmul(
                        self.psf(bt, 512, parts=64),
                        w[:, ck * HD:(ck + 1) * HD],
                        self.xnb[:, ck * T + ns * 512:ck * T + (ns + 1) * 512],
                        start=(ck == 0), stop=(ck == NC - 1)),
                        inc=(tT if ck == NC - 1 else None))
                nmm = tT.n
                P.wait("vector", tT, nmm)
                bank_free_tV[par] = P.add("vector", lambda e, dstf=dstf, ns=ns,
                                          bcol=bcol, bt=bt: e.tensor_scalar(
                    dstf[:, ns * 512:(ns + 1) * 512], self.psf(bt, 512, parts=64),
                    self.bqkv1_pp[:, bcol:bcol + 1], None, ALU.add), inc=tV)
        kv_tV = tV.n
        P.wait("tensor", tV, kv_tV)
        for gph in range(16):
            P.add("tensor", lambda e, gph=gph: e.transpose(
                self.psb(self.pbC, 64, base=gph * 64),
                vfst[:, gph * 128:(gph + 1) * 128], self.ident_bf[0:64, 0:64]),
                inc=(tT if gph == 15 else None))
        nvt = tT.n
        P.wait("scalar", tT, nvt)
        nvta = P.add("scalar", lambda e, vtst=vtst: e.activation(
            vtst, self.psb(self.pbC, 1024), AF.Copy), inc=tA)
        self.qkv1_pbc_tA = nvta
        engs = OrderedEngineSet([mybir.EngineType.Pool])

        def gp_regs(e):
            if not hasattr(self, "_regs"):
                nc = self.nc
                r = {}
                for nmq in ("half", "phalf", "koff", "voff", "foff"):
                    r[nmq] = nc.alloc_registers("r_" + nmq, engines=engs).handles[0]
                pid = e.partition_id()
                e.reg_mov(r["half"], pid)
                e.reg_alu(r["half"], r["half"], 1, ALU.bitwise_and)
                e.reg_mov(r["phalf"], r["half"])
                e.reg_alu(r["phalf"], r["phalf"], 1, ALU.bitwise_xor)
                self._regs = r
            return self._regs

        KHALF = 4 * HD * T
        VHALF = 4 * 128 * NT * HD
        FHALF = 4 * 128

        def share_writes(e, s=s):
            r = gp_regs(e)
            e.reg_mov(r["koff"], r["half"])
            e.reg_alu(r["koff"], r["koff"], KHALF, ALU.mult)
            e.reg_alu(r["koff"], r["koff"], s * HD * T, ALU.add)
            e.reg_mov(r["voff"], r["half"])
            e.reg_alu(r["voff"], r["voff"], VHALF, ALU.mult)
            e.reg_alu(r["voff"], r["voff"], s * 128 * NT * HD, ALU.add)
            kdst = bass.AP(self.sh_k, r["koff"], [[T, HD], [1, T]])
            vdst = bass.AP(self.sh_v, r["voff"], [[NT * HD, 128], [1, NT * HD]])
            return kdst, vdst

        P.wait("gpsimd", tV, kv_tV)
        P.wait("gpsimd", tA, nvta)
        nk = P.add("gpsimd", lambda e, s=s, kst=kst: e.dma_start(
            out=share_writes(e)[0], in_=kst), inc=qSh, k=16)
        nv = P.add("gpsimd", lambda e, s=s, vtst=vtst: e.dma_start(
            out=share_writes(e)[1], in_=vtst), inc=qSh, k=16)
        P.wait("gpsimd", qSh, nv)

        def flag_write(e, s=s):
            r = gp_regs(e)
            e.reg_mov(r["foff"], r["half"])
            e.reg_alu(r["foff"], r["foff"], FHALF, ALU.mult)
            e.reg_alu(r["foff"], r["foff"], s * 128, ALU.add)
            fdst = bass.AP(self.sh_f, r["foff"], [[128, 1], [1, 128]])
            return e.dma_start(out=fdst, in_=self.fwm)

        P.wait("gpsimd", P.sem("tV"), self.fwm_ready)
        nf = P.add("gpsimd", lambda e, s=s: flag_write(e, s), inc=qSh, k=16)
        guards = [(qSh, nf)]
        if s == 0:
            guards.append((P.sem("qQ"), self.q1_store))
        self._kv_share_done = guards

    def _reload_xn0(self):
        P = self.P
        qG, tT = P.sem("qG"), P.sem("tT")
        P.wait("gpsimd", tT, tT.n)
        self.xn0_reload = P.add("gpsimd", lambda e: e.dma_start(
            out=self.xnb, in_=self.xn0_d), inc=qG, k=16)
        P.wait("tensor", qG, self.xn0_reload)

    def _poll_and_load(self):
        P = self.P
        nc = self.nc
        qG, tT = P.sem("qG"), P.sem("tT")
        qP = P.sem("qP")
        qKV = P.sem("qKV")
        engs = OrderedEngineSet([mybir.EngineType.Pool])
        KHALF = 4 * HD * T
        VHALF = 4 * 128 * NT * HD
        FHALF = 4 * 128
        K_all = self.av(0, 64, NCORE * T, BF16)
        V_all = self.av(32 * KB, 128, (NCORE * T // 128) * HD, BF16)
        self.K_all, self.V_all = K_all, V_all
        moe_end_tT = tT.n
        self.moe_end_tT = moe_end_tT

        def poll(e):
            r = self._regs
            rv, re_ = r["koff"], r["voff"]
            rc_c = nc.alloc_registers("pollc", engines=engs)
            rt_c = nc.alloc_registers("pollt", engines=engs)
            rc, rt = rc_c.handles[0], rt_c.handles[0]
            rv_c = bass.RegisterHandles(rv)
            del rv_c
            e.reg_mov(rv, 0)
            e.reg_mov(re_, 0)
            e.reg_mov(rc, 0)
            e.reg_mov(r["foff"], r["phalf"])
            e.reg_alu(r["foff"], r["foff"], FHALF, ALU.mult)
            e.reg_alu(r["foff"], r["foff"], 3 * 128, ALU.add)
            fsrc = bass.AP(self.sh_f, r["foff"], [[128, 1], [1, 128]])
            lbl, done = "fpollL", "fpollD"
            nc.br(lbl, engines=engs)
            with nc.body(lbl, valid_engines=engs):
                e.reg_alu(re_, re_, 16, ALU.add)
                e.dma_start(out=self.fpoll, in_=fsrc).then_inc(qP.h, 16)
                e.wait_ge(qP.h, re_)
                e.reg_load(rv, self.fpoll[0:1, 0:1])
                e.reg_alu(rc, rc, 1, ALU.add)
                e.reg_mov(rt, rc)
                e.reg_alu(rt, rt, MAXPOLL, ALU.divide)
                e.reg_alu(rt, rt, MAGIC, ALU.mult)
                e.reg_alu(rt, rt, rv, ALU.add)
                nc.br_lt(rt_c, MAGIC, on_true=lbl, on_false=done, engines=engs)
            nc.switch_bb(done)
            e.reg_save(self.fpoll[0:1, 64:65], rc)
            e.reg_save(self.fpoll[0:1, 65:66], rv)
            nc.free_register(rc)
            nc.free_register(rt)

        P.wait("gpsimd", tT, moe_end_tT)
        P.add("gpsimd", lambda e: poll(e))
        def kv_loads(e):
            r = self._regs
            insts = []
            for iw, (hreg) in enumerate(("half", "phalf")):
                e.reg_mov(r["koff"], r[hreg])
                e.reg_alu(r["koff"], r["koff"], KHALF, ALU.mult)
                ksrc = bass.AP(self.sh_k, r["koff"],
                               [[T, HD], [HD * T, 4], [1, T]])
                kdst = K_all.rearrange("d (s t) -> d s t", s=NCORE)[
                    :, iw * 4:(iw + 1) * 4, :]
                insts.append(e.dma_start(out=kdst, in_=ksrc))
                e.reg_mov(r["voff"], r[hreg])
                e.reg_alu(r["voff"], r["voff"], VHALF, ALU.mult)
                vsrc = bass.AP(self.sh_v, r["voff"],
                               [[NT * HD, 128], [128 * NT * HD, 4], [1, NT * HD]])
                vdst = V_all.rearrange("p (s c) -> p s c", s=NCORE)[
                    :, iw * 4:(iw + 1) * 4, :]
                insts.append(e.dma_start(out=vdst, in_=vsrc))
            return insts

        def kv_loads_emit(e):
            for ins in kv_loads(e):
                ins.then_inc(qKV.h, 16)

        P.add("gpsimd", lambda e: kv_loads_emit(e))
        qKV.n += 64
        self.kv_loaded = qKV.n
        q1s = self.av(84 * KB, 64, T, BF16)
        self.q1s = q1s
        P.wait("sync", P.sem("qQ"), self.q1_store)
        self.q1_loaded = P.add("sync", lambda e, q1s=q1s: e.dma_start(
            out=q1s, in_=self.q1_d), inc=P.sem("qS"), k=16)

    def _flash_final(self):
        P = self.P
        qS, qO, tV, tA, tT = (P.sem(s) for s in ("qS", "qO", "tV", "tA", "tT"))
        qKV, qM = P.sem("qKV"), P.sem("qM")
        K_all, V_all, q1 = self.K_all, self.V_all, self.q1s
        Pt = self.av(48 * KB, 128, 2 * 512, BF16)
        pt_sb = self.av(50 * KB, 128, 2 * 512, BF16)
        xts_a = [self.av(52 * KB, 128, C, F32), self.av(56 * KB, 128, C, F32)]
        ots_a = [self.av(60 * KB, 128, C, F32), self.av(64 * KB, 128, C, F32)]
        accv = self.av(72 * KB, 128, HD, F32)
        ytv = self.av(72 * KB + 512, 128, HD, F32)
        stt = self.gsc
        ps_sc = self.ps_s
        pv = self.pbA
        ptp = [self.pbB, self.pbC]
        ident = self.ident_bf

        P.wait("tensor", tV, tV.n)
        P.wait("tensor", tA, tA.n)
        P.wait("tensor", qKV, self.kv_loaded)
        P.wait("tensor", qS, self.q1_loaded)
        P.wait("sync", tT, self.moe_end_tT)
        tile_chain = self.x1_tile_ready

        NCH = GTOK // 512
        xt_read = [0, 0]
        ot_free = [0, 0]
        for qt in range(NT):
            par_t = qt % 2
            P.wait("sync", tV, xt_read[par_t])
            cs_, cn_ = tile_chain[qt]
            P.wait("sync", cs_, cn_)
            ldx = P.add("sync", lambda e, qt=qt, par_t=par_t: e.dma_start(
                out=xts_a[par_t], in_=self.xo[qt * 128:(qt + 1) * 128, :]),
                inc=qS, k=16)

            nsc = [0] * (NCH + 1)
            ntr = [0] * (NCH + 1)
            npv = [0] * (NCH + 1)
            nexp = [0] * (NCH + 1)
            nev = [0] * (NCH + 1)
            nevS = [None] * (NCH + 1)
            nacc = [0] * (NCH + 1)
            nb = [0] * (NCH + 1)

            def emit_scores(ci, qt=qt):
                par = ci % 2
                nsc[ci] = P.add("tensor", lambda e, qt=qt, ci=ci, par=par: e.matmul(
                    self.psf(ps_sc, 512, base=par * 512),
                    q1[:, qt * 128:(qt + 1) * 128],
                    K_all[:, ci * 512:(ci + 1) * 512], start=True, stop=True),
                    inc=tT)

            P.wait("tensor", tA, tA.n)
            P.wait("tensor", tV, tV.n)
            emit_scores(0)
            for ci in range(NCH + 1):
                par = ci % 2
                so = 8 + 8 * par
                if ci < NCH:
                    P.wait("vector", tT, nsc[ci])
                    P.addf("vector", lambda e, par=par, so=so: e.reduce_max(
                        stt[:, so:so + 1], self.psf(ps_sc, 512, base=par * 512),
                        axis=AX.X, negate=True))
                    if ci == 0:
                        P.addf("vector", lambda e, so=so: e.tensor_copy(
                            stt[:, 0:1], stt[:, so:so + 1]))
                        nb[ci] = P.add("vector", lambda e, so=so: e.tensor_scalar(
                            stt[:, so + 2:so + 3], stt[:, 0:1], 1.0 / 32.0, None,
                            ALU.mult), inc=tV)
                    else:
                        P.addf("vector", lambda e, so=so: e.tensor_scalar(
                            stt[:, so + 1:so + 2], stt[:, so:so + 1], stt[:, 0:1],
                            0.0, ALU.subtract, ALU.min))
                        nb[ci] = P.add("vector", lambda e, so=so: e.tensor_scalar(
                            stt[:, so + 2:so + 3], stt[:, so + 1:so + 2],
                            stt[:, 0:1], 1.0 / 32.0, ALU.add, ALU.mult), inc=tV)
                        P.addf("vector", lambda e, so=so: e.tensor_scalar(
                            stt[:, 0:1], stt[:, so + 1:so + 2], stt[:, 0:1], None,
                            ALU.add))
                if ci < NCH:
                    P.wait("scalar", tV, nb[ci])
                    if ci > 0:
                        P.add("scalar", lambda e, so=so: e.activation(
                            stt[:, so + 3:so + 4], stt[:, so + 1:so + 2], AF.Exp,
                            scale=1.0 / 32.0))
                    if ci >= 2:
                        P.wait("scalar", tT, ntr[ci - 2])
                    nexp[ci] = P.add("scalar", lambda e, par=par, so=so: e.activation(
                        Pt[:, par * 512:(par + 1) * 512],
                        self.psf(ps_sc, 512, base=par * 512), AF.Exp,
                        scale=1.0 / 32.0, bias=stt[:, so + 2:so + 3],
                        accum_out=stt[:, so + 4:so + 5]), inc=tA)
                if ci + 1 < NCH:
                    if ci >= 1:
                        P.wait("tensor", tA, nexp[ci - 1])
                    emit_scores(ci + 1)
                if ci < NCH:
                    P.wait("tensor", tA, nexp[ci])
                    if ci >= 2:
                        P.wait("tensor", nevS[ci - 2], nev[ci - 2])
                    for kt in range(4):
                        P.add("tensor", lambda e, kt=kt, par=par: e.transpose(
                            ptp[par].ap().bitcast(BF16)[:, kt * 128:(kt + 1) * 128],
                            Pt[:, par * 512 + kt * 128:par * 512 + (kt + 1) * 128],
                            ident), inc=(tT if kt == 3 else None))
                    ntr[ci] = tT.n
                if ci >= 1:
                    cj = ci - 1
                    pj = cj % 2
                    P.wait("tensor", nevS[cj], nev[cj])
                    P.wait("tensor", tV, nacc[cj - 1] if cj >= 1 else 0)
                    for kt in range(4):
                        gk = cj * 4 + kt
                        P.add("tensor", lambda e, gk=gk, kt=kt, pj=pj: e.matmul(
                            pv.ap()[:, 0:HD],
                            pt_sb[:, pj * 512 + kt * 128:pj * 512 + (kt + 1) * 128],
                            V_all[:, gk * HD:(gk + 1) * HD],
                            start=(kt == 0), stop=(kt == 3)),
                            inc=(tT if kt == 3 else None))
                    npv[cj] = tT.n
                if ci < NCH:
                    if par == 0:
                        P.wait("vector", tT, ntr[ci])
                        if ci >= 2:
                            P.wait("vector", tT, npv[ci - 2])
                        nev[ci] = P.add("vector", lambda e, par=par: e.tensor_copy(
                            pt_sb[:, par * 512:(par + 1) * 512],
                            ptp[par].ap().bitcast(BF16)[:, 0:512]), inc=tV)
                        nevS[ci] = tV
                    else:
                        P.wait("scalar", tT, ntr[ci])
                        if ci >= 2:
                            P.wait("scalar", tT, npv[ci - 2])
                        nev[ci] = P.add("scalar", lambda e, par=par: e.activation(
                            pt_sb[:, par * 512:(par + 1) * 512],
                            ptp[par].ap().bitcast(BF16)[:, 0:512], AF.Copy),
                            inc=tA)
                        nevS[ci] = tA
                if ci < NCH:
                    P.wait("vector", tA, nexp[ci])
                    if ci == 0:
                        P.addf("vector", lambda e, so=so: e.tensor_copy(
                            stt[:, 1:2], stt[:, so + 4:so + 5]))
                    else:
                        P.addf("vector", lambda e, so=so: e.scalar_tensor_tensor(
                            stt[:, 1:2], stt[:, 1:2], stt[:, so + 3:so + 4],
                            stt[:, so + 4:so + 5], ALU.mult, ALU.add))
                if ci >= 1:
                    cj = ci - 1
                    soj = 8 + 8 * (cj % 2)
                    P.wait("vector", tT, npv[cj])
                    if cj == 0:
                        nacc[cj] = P.add("vector", lambda e: e.tensor_copy(
                            accv, pv.ap()[:, 0:HD]), inc=tV)
                    else:
                        nacc[cj] = P.add("vector", lambda e, soj=soj:
                                         e.scalar_tensor_tensor(
                                             accv, accv, stt[:, soj + 3:soj + 4],
                                             pv.ap()[:, 0:HD], ALU.mult, ALU.add),
                                         inc=tV)
                    P.wait("vector", tV, nacc[cj])
            P.addf("vector", lambda e: e.reciprocal(stt[:, 2:3], stt[:, 1:2]))
            P.addf("vector", lambda e: e.tensor_scalar(
                ytv, accv, stt[:, 2:3], None, ALU.mult))
            if self.dbgF:
                dv = self.av(88 * KB, 128, NT * 4, F32)
                self._fdump = dv
                P.addf("vector", lambda e, qt=qt: e.tensor_copy(
                    dv[:, qt * 4:qt * 4 + 3], stt[:, 0:3]))
            P.wait("vector", qS, ldx)
            P.wait("vector", qO, ot_free[par_t])
            not_ = P.add("vector", lambda e, par_t=par_t: e.tensor_add(
                ots_a[par_t].rearrange("p (r d) -> p r d", d=HD),
                xts_a[par_t].rearrange("p (r d) -> p r d", d=HD),
                ytv.unsqueeze(1).broadcast_to([128, NHEAD, HD])), inc=tV)
            xt_read[par_t] = not_
            P.wait("sync", tV, not_)
            nst = P.add("sync", lambda e, qt=qt, par_t=par_t: e.dma_start(
                out=self.out_d[qt * 128:(qt + 1) * 128, :], in_=ots_a[par_t]),
                inc=qO, k=16)
            ot_free[par_t] = nst
        P.wait("sync", qO, P.sem("qO").n)



_BUILD_CACHE = {}

FUSED = True


def kernel(**inputs):
    g = make_host_globals(inputs)
    x = np.asarray(inputs["L0_attn_x"], np.float32)

    if FUSED:
        if "F" not in _BUILD_CACHE:
            _BUILD_CACHE["F"] = BuildF()
        F = _BUILD_CACHE["F"]
        in_maps = []
        for c in range(NCORE):
            m = dict(g)
            m["xmine"] = np.ascontiguousarray(np.concatenate(
                [x[(c + 2 * s) % NCORE] for s in range(4)], axis=0))
            in_maps.append(m)
        res = run_bass_kernel_spmd(F.nc, in_maps, core_ids=list(range(NCORE)))
        return np.stack([res.results[c]["out"] for c in range(NCORE)], axis=0)

    if "A" not in _BUILD_CACHE:
        _BUILD_CACHE["A"] = BuildA()
        _BUILD_CACHE["B"] = BuildB()
    A, B = _BUILD_CACHE["A"], _BUILD_CACHE["B"]

    in_maps = []
    for b in range(NCORE):
        m = dict(g)
        m["x"] = np.ascontiguousarray(x[b])
        in_maps.append(m)
    resA = run_bass_kernel_spmd(A.nc, in_maps, core_ids=list(range(NCORE)))

    K_all = np.concatenate([resA.results[c]["ko"] for c in range(NCORE)], axis=1)
    vparts = [resA.results[c]["vo"].reshape(128, NT, HD) for c in range(NCORE)]
    V_all = np.ascontiguousarray(
        np.stack(vparts, axis=1).reshape(128, NCORE * NT * HD))
    in_mapsB = []
    for c in range(NCORE):
        in_mapsB.append({
            "xin": resA.results[c]["xo"],
            "q": resA.results[c]["qo"],
            "kall": np.ascontiguousarray(K_all),
            "vall": V_all,
            "ident_bf": g["ident_bf"],
        })
    resB = run_bass_kernel_spmd(B.nc, in_mapsB, core_ids=list(range(NCORE)))
    out = np.stack([resB.results[c]["out"] for c in range(NCORE)], axis=0)
    return out


# version 09
# speedup vs baseline: 1.1237x; 1.1237x over previous
import numpy as np
import ml_dtypes
import concourse.bass as bass
import concourse.mybir as mybir
from concourse.bass_utils import run_bass_kernel_spmd
from concourse.ordered_engine_set import OrderedEngineSet

F32 = mybir.dt.float32
BF16 = mybir.dt.bfloat16
U8 = mybir.dt.uint8
AF = mybir.ActivationFunctionType
ALU = mybir.AluOpType
AX = mybir.AxisListType

ENGINES = ("sync", "gpsimd", "tensor", "vector", "scalar")

T = 2048
C = 1024
NT = T // 128
NC = C // 128
HD = 64
NHEAD = 16
HMOE = 1408
NHK = HMOE // 128
NEXP = 9
NCORE = 8
GTOK = NCORE * T
KB = 1024


class Sem:
    __slots__ = ("h", "n", "name")

    def __init__(self, nc, name):
        self.h = nc.alloc_semaphore(name)
        self.n = 0
        self.name = name


class Prog:
    def __init__(self, nc):
        self.nc = nc
        self.q = {e: [] for e in ENGINES}
        self._sems = {}

    def sem(self, name):
        if name not in self._sems:
            self._sems[name] = Sem(self.nc, name)
        return self._sems[name]

    def add(self, eng, fn, inc=None, k=1):
        if inc is not None:
            inc.n += k
            self.q[eng].append(lambda e, fn=fn, s=inc.h, k=k: fn(e).then_inc(s, k))
            return inc.n
        self.q[eng].append(fn)
        return None

    def wait(self, eng, sem, n):
        if n and n > 0:
            self.q[eng].append(lambda e, s=sem.h, n=n: e.wait_ge(s, n))

    def addf(self, eng, fn):
        sem = self.sem("flush_" + eng)
        n = self.add(eng, fn, inc=sem)
        self.wait(eng, sem, n)
        return n

    def finalize(self):
        with self.nc.Block() as block:
            block.sync(lambda e: [f(e) for f in self.q["sync"]])
            block.gpsimd(lambda e: [f(e) for f in self.q["gpsimd"]])
            block.tensor(lambda e: [f(e) for f in self.q["tensor"]])
            block.vector(lambda e: [f(e) for f in self.q["vector"]])
            block.scalar(lambda e: [f(e) for f in self.q["scalar"]])


def _dt_size(dt):
    return {F32: 4, BF16: 2, U8: 1}[dt]



def _cm(W):
    Cin, Cout = np.asarray(W).shape
    return np.ascontiguousarray(
        np.asarray(W, np.float32).reshape(Cin // 128, 128, Cout)
        .transpose(1, 0, 2).reshape(128, -1))


def _pp(v):
    v = np.asarray(v, dtype=np.float32)
    K = v.shape[0] // 128
    return np.ascontiguousarray(v.reshape(K, 128).T)


def make_host_globals(inputs):
    g = {}
    g["w_qkv"] = _cm(inputs["L0_attn_Wqkv"])
    g["b_qkv_row"] = np.asarray(inputs["L0_attn_bqkv"], np.float32).reshape(1, 3 * C)
    g["w_proj"] = _cm(inputs["L0_attn_Wproj"])
    g["b_proj_pp"] = _pp(inputs["L0_attn_bproj"])
    g["w_r"] = _cm(np.asarray(inputs["L0_ffn_W_r"], np.float32).T)
    g["w_k"] = _cm(np.asarray(inputs["L0_ffn_W_k"], np.float32).T)
    g["w_v"] = _cm(np.asarray(inputs["L0_ffn_W_v"], np.float32).T)
    g["mu_pp"] = np.ascontiguousarray(
        np.concatenate([_pp(inputs["L0_ffn_mu_r"]), _pp(inputs["L0_ffn_mu_k"])],
                       axis=1))
    g["w_q1"] = _cm(np.asarray(inputs["L1_attn_Wq"], np.float32).T)
    g["w_k1"] = _cm(np.asarray(inputs["L1_attn_Wk"], np.float32).T)
    g["w_v1"] = _cm(np.asarray(inputs["L1_attn_Wv"], np.float32).T)
    g["b_qkv1_pp"] = np.ascontiguousarray(
        np.stack([np.asarray(inputs["L1_attn_bq"], np.float32),
                  np.asarray(inputs["L1_attn_bk"], np.float32),
                  np.asarray(inputs["L1_attn_bv"], np.float32)], axis=1))
    g["w_gate"] = _cm(np.asarray(inputs["L1_ffn_gate_w"], np.float32).T)
    g["b_gate_row"] = np.asarray(inputs["L1_ffn_gate_b"], np.float32).reshape(1, 8)

    w1 = np.asarray(inputs["L1_ffn_w1"], np.float32)
    w3 = np.asarray(inputs["L1_ffn_w3"], np.float32)
    w2 = np.asarray(inputs["L1_ffn_w2"], np.float32)
    sw1 = np.asarray(inputs["L1_ffn_sw1"], np.float32)
    sw3 = np.asarray(inputs["L1_ffn_sw3"], np.float32)
    sw2 = np.asarray(inputs["L1_ffn_sw2"], np.float32)

    def blocks13(w_all, sw):
        out = np.empty((NEXP * NHK, 128, NC * 128), np.float32)
        for e in range(NEXP):
            w = sw if e == 8 else w_all[e]
            wr = w.reshape(NC, 128, NHK, 128)
            for hk in range(NHK):
                out[e * NHK + hk] = wr[:, :, hk, :].transpose(1, 0, 2).reshape(128, -1)
        return np.ascontiguousarray(out)

    g["w1b"] = blocks13(w1, sw1)
    g["w3b"] = blocks13(w3, sw3)
    w2b = np.empty((NEXP * NHK, 128, C), np.float32)
    for e in range(NEXP):
        w2b[e * NHK:(e + 1) * NHK] = (sw2 if e == 8 else w2[e]).reshape(NHK, 128, C)
    g["w2b"] = np.ascontiguousarray(w2b)

    def b13(b_all, sb):
        cols = [(sb if e == 8 else b_all[e]).reshape(NHK, 128).T for e in range(NEXP)]
        return np.ascontiguousarray(np.concatenate(cols, axis=1), dtype=np.float32)

    g["b1_pp"] = b13(np.asarray(inputs["L1_ffn_b1"], np.float32),
                     np.asarray(inputs["L1_ffn_sb1"], np.float32))
    g["b3_pp"] = b13(np.asarray(inputs["L1_ffn_b3"], np.float32),
                     np.asarray(inputs["L1_ffn_sb3"], np.float32))
    g["b2cat"] = np.ascontiguousarray(
        np.concatenate([np.asarray(inputs["L1_ffn_b2"], np.float32),
                        np.asarray(inputs["L1_ffn_sb2"], np.float32).reshape(1, C)],
                       axis=0))

    m4 = np.zeros((128, 4 * 512), np.float32)
    for pp in range(4):
        for qtl in range(4):
            blk = m4[:, pp * 512 + qtl * 128:pp * 512 + (qtl + 1) * 128]
            if qtl < pp:
                blk[:] = -1e9
            elif qtl == pp:
                ii = np.arange(128)
                blk[:] = np.where(ii[:, None] <= ii[None, :], 0.0, -1e9)
    g["mask4t"] = m4.astype(ml_dtypes.bfloat16)
    g["ones_col"] = np.ones((128, 1), dtype=ml_dtypes.bfloat16)
    g["ident_bf"] = np.eye(128, dtype=ml_dtypes.bfloat16)
    g["ident_f32"] = np.eye(128, dtype=np.float32)
    g["mask_bf"] = np.where(np.tril(np.ones((128, 128), bool)), 0.0, -1e9
                            ).astype(ml_dtypes.bfloat16)
    g["ones_bf"] = np.ones((1, 512), dtype=ml_dtypes.bfloat16)
    g["ones_f32"] = np.ones((1, 128), dtype=np.float32)
    return g


GLOBAL_SPECS = [
    ("w_qkv", (128, NC * 3 * C), F32), ("b_qkv_row", (1, 3 * C), F32),
    ("w_proj", (128, NC * C), F32), ("b_proj_pp", (128, NC), F32),
    ("w_r", (128, NC * C), F32), ("w_k", (128, NC * C), F32),
    ("w_v", (128, NC * C), F32), ("mu_pp", (128, 16), F32),
    ("w_q1", (128, NC * HD), F32), ("w_k1", (128, NC * HD), F32),
    ("w_v1", (128, NC * HD), F32), ("b_qkv1_pp", (64, 3), F32),
    ("w_gate", (128, NC * 8), F32), ("b_gate_row", (1, 8), F32),
    ("w1b", (NEXP * NHK, 128, NC * 128), F32),
    ("w3b", (NEXP * NHK, 128, NC * 128), F32),
    ("w2b", (NEXP * NHK, 128, C), F32),
    ("b1_pp", (128, NEXP * NHK), F32), ("b3_pp", (128, NEXP * NHK), F32),
    ("b2cat", (NEXP, C), F32),
    ("mask4t", (128, 4 * 512), BF16), ("ones_col", (128, 1), BF16),
    ("ident_bf", (128, 128), BF16), ("ident_f32", (128, 128), F32),
    ("mask_bf", (128, 128), BF16), ("ones_bf", (1, 512), BF16),
    ("ones_f32", (1, 128), F32),
]



class BuildA:
    def __init__(self, stop="all", dbg=()):
        self.stop = stop
        self.dbg = set(dbg)
        nc = bass.Bass("TRN2", target_bir_lowering=False)
        self.nc = nc
        self.P = Prog(nc)
        self.x_in = nc.dram_tensor("x", [T, C], F32, kind="ExternalInput").ap()
        self.g = {}
        for name, shape, dt in GLOBAL_SPECS:
            self.g[name] = nc.dram_tensor(name, list(shape), dt,
                                          kind="ExternalInput").ap()
        self.xo = nc.dram_tensor("xo", [T, C], F32, kind="ExternalOutput").ap()
        self.qo = nc.dram_tensor("qo", [HD, T], BF16, kind="ExternalOutput").ap()
        self.ko = nc.dram_tensor("ko", [HD, T], BF16, kind="ExternalOutput").ap()
        self.vo = nc.dram_tensor("vo", [128, NT * HD], BF16, kind="ExternalOutput").ap()
        self._setup_sbuf()
        self._build()

    def _setup_sbuf(self):
        nc = self.nc
        sbt = nc.alloc_sbuf_tensor
        self.mask4t = sbt("s_mask4t", [128, 4 * 512], BF16).ap()
        self.ones_col = sbt("s_ones_col", [128, 1], BF16).ap()
        self.ident_bf = sbt("s_ident_bf", [128, 128], BF16).ap()
        self.ident_f32 = sbt("s_ident_f32", [128, 128], F32).ap()
        self.mask_bf = sbt("s_mask_bf", [128, 128], BF16).ap()
        self.ones_bf = sbt("s_ones_bf", [1, 512], BF16).ap()
        self.ones_f32 = sbt("s_ones_f32", [1, 128], F32).ap()
        self.bqkv_row = sbt("bqkv_row", [1, 3 * C], BF16).ap()
        self.bproj_pp = sbt("bproj_pp", [128, NC], F32).ap()
        self.mu_pp = sbt("s_mu_pp", [128, 16], F32).ap()
        self.bqkv1_pp = sbt("bqkv1_pp", [64, 3], F32).ap()
        self.bgate_row = sbt("bgate_row", [1, 8], F32).ap()
        self.b1_pp = sbt("s_b1_pp", [128, NEXP * NHK], F32).ap()
        self.b3_pp = sbt("s_b3_pp", [128, NEXP * NHK], F32).ap()
        self.b2cat = sbt("s_b2cat", [NEXP, C], BF16).ap()
        self.wgate = sbt("wgate", [128, NC * 8], BF16).ap()
        self.gbb = sbt("gbb", [128, 8], F32).ap()
        self.stat = sbt("stat", [128, 16], F32).ap()
        self.gsc = sbt("gsc", [128, 32], F32).ap()
        self.rw_tm = sbt("rw_tm", [128, NT * NEXP], F32).ap()
        self.rw_fm = sbt("rw_fm", [NEXP, T], BF16).ap()
        self.scr = sbt("scr", [128, 2048], F32).ap()
        self.px = sbt("px", [128, T + 1], F32).ap()
        self.xnt = sbt("xnt", [128, 2 * C], BF16).ap()
        self.xnb = sbt("xnb", [128, NC * T], BF16).ap()

        rem = nc.sbuf_bytes_remaining
        self.arena_bytes = (rem - 3 * KB) // 4 * 4
        assert self.arena_bytes >= 127 * KB, f"arena too small: {self.arena_bytes}"
        self.AR = sbt("arena", [128, self.arena_bytes], U8).ap()

        self.ps_s = nc.alloc_psum_tensor("ps_s", [128, 2048], F32)
        self.pbA = nc.alloc_psum_tensor("pbA", [128, 512], F32)
        self.pbB = nc.alloc_psum_tensor("pbB", [128, 512], F32)
        self.pbC = nc.alloc_psum_tensor("pbC", [128, 512], F32)
        self.pbD = nc.alloc_psum_tensor("pbD", [128, 512], F32)

    def av(self, off_b, parts, cols, dt):
        n = cols * _dt_size(dt)
        assert off_b % 4 == 0 and off_b + n <= self.arena_bytes, (off_b, n)
        return self.AR[0:parts, off_b:off_b + n].bitcast(dt)

    def psf(self, t, cols=512, base=0, parts=128, pbase=0):
        return t.ap()[pbase:pbase + parts, base:base + cols]

    def psb(self, t, cols, base=0, parts=128, pbase=0):
        return t.ap().bitcast(BF16)[pbase:pbase + parts, base:base + cols]

    def dbg_store(self, name, src_ap, shape, dt, waits):
        if name not in self.dbg:
            return
        od = self.nc.dram_tensor("dbg_" + name, list(shape), dt,
                                 kind="ExternalOutput").ap()
        P, st = self.P, self.P.sem("st")
        for s, n in waits:
            P.wait("sync", s, n)
        P.add("sync", lambda e, o=od, a=src_ap: e.dma_start(out=o, in_=a), inc=st, k=16)

    def _build(self):
        self._load_consts()
        self._ln(src_kind="dram", dram=self.x_in, ready=None)
        self.dbg_store("xnb0", self.xnb, (128, NC * T), BF16,
                       [(self.P.sem("tA"), self.P.sem("tA").n)])
        if self.stop == "ln":
            return self._finish()
        self._ffn()
        if self.stop == "ffn":
            return self._finish()
        self._attn()
        if self.stop == "attn":
            return self._finish()
        self._proj_resid()
        if self.stop == "resid":
            return self._finish()
        self._ln(src_kind="dram", dram=self.xo, ready=self.x1_tile_ready)
        self.dbg_store("xnb1", self.xnb, (128, NC * T), BF16,
                       [(self.P.sem("tA"), self.P.sem("tA").n)])
        if self.stop == "ln1":
            return self._finish()
        self._qkv1()
        if self.stop == "qkv1":
            return self._finish()
        self._gate()
        if self.stop == "gate":
            return self._finish()
        self._moe()
        return self._finish()

    def _finish(self):
        P = self.P
        for s in ("st", "qXo", "qQKV"):
            sem = P.sem(s)
            if sem.n:
                P.wait("sync", sem, sem.n)
        qM = P.sem("qM")
        if qM.n:
            P.wait("gpsimd", qM, qM.n)
        P.finalize()

    def _load_consts(self):
        P = self.P
        qS, qG, tT, tV = P.sem("qS"), P.sem("qG"), P.sem("tT"), P.sem("tV")
        for t, name in [
            (self.mask4t, "mask4t"), (self.ones_col, "ones_col"),
            (self.ident_bf, "ident_bf"), (self.ident_f32, "ident_f32"),
            (self.mask_bf, "mask_bf"), (self.ones_bf, "ones_bf"),
            (self.ones_f32, "ones_f32"), (self.bproj_pp, "b_proj_pp"),
            (self.mu_pp, "mu_pp"), (self.bqkv1_pp, "b_qkv1_pp"),
            (self.b1_pp, "b1_pp"), (self.b3_pp, "b3_pp"),
            (self.bgate_row, "b_gate_row"),
        ]:
            P.add("sync", lambda e, t=t, a=self.g[name]: e.dma_start(out=t, in_=a),
                  inc=qS, k=16)
        for t, name in [(self.bqkv_row, "b_qkv_row"), (self.b2cat, "b2cat"),
                        (self.wgate, "w_gate")]:
            P.add("gpsimd", lambda e, t=t, a=self.g[name]: e.dma_start(out=t, in_=a),
                  inc=qG, k=16)
        self.n_const_s = qS.n
        self.n_const_g = qG.n
        P.wait("tensor", qS, self.n_const_s)
        n = P.add("tensor", lambda e: e.matmul(
            self.psf(self.pbD, 8), self.ones_f32[0:1, :], self.bgate_row,
            start=True, stop=True), inc=tT)
        P.wait("vector", tT, n)
        self.gbb_done = P.add("vector", lambda e: e.tensor_copy(
            self.gbb, self.psf(self.pbD, 8)), inc=tV)

    def _ln(self, src_kind, dram, ready):
        P = self.P
        qS, tV, tA, tT = P.sem("qS"), P.sem("tV"), P.sem("tA"), P.sem("tT")
        P.wait("tensor", tV, tV.n)
        P.wait("tensor", tA, tA.n)
        g_ = getattr(self, "_xnb_guard", None)
        if g_:
            P.wait("scalar", g_[0], g_[1])
            self._xnb_guard = None
        st = self.stat
        ln_x = [self.scr[:, 0:1024], self.scr[:, 1024:2048]]
        jk = self.px[:, 0:1024]
        packs = [self.pbA, self.pbB]
        evac_done = [0, 0]
        xnt_free = [0, 0]
        dve_read = [tV.n, tV.n]
        for n in range(NT):
            par = n % 2
            so = 8 * par
            xt = ln_x[par]
            if dve_read[par]:
                P.wait("sync", tV, dve_read[par])
            if ready is not None:
                P.wait("sync", ready[n][0], ready[n][1])
            ldn = P.add("sync", lambda e, n=n, xt=xt, dram=dram: e.dma_start(
                out=xt, in_=dram[n * 128:(n + 1) * 128, :]), inc=qS, k=16)
            P.wait("vector", qS, ldn)
            P.add("vector", lambda e, xt=xt, so=so: e.scalar_tensor_tensor(
                jk, xt, 1.0, xt, ALU.mult, ALU.mult, accum_out=st[:, so + 1:so + 2]))
            P.addf("vector", lambda e, xt=xt, so=so: e.tensor_scalar(
                jk, xt, 1.0, 0.0, ALU.mult, ALU.add, accum_out=st[:, so:so + 1]))
            P.add("vector", lambda e, so=so: e.tensor_scalar(
                st[:, so + 2:so + 3], st[:, so:so + 1], 1.0 / C, None, ALU.mult))
            P.addf("vector", lambda e, so=so: e.tensor_scalar(
                st[:, so + 3:so + 4], st[:, so + 1:so + 2], 1.0 / C, 1e-5,
                ALU.mult, ALU.add))
            P.addf("vector", lambda e, so=so: e.scalar_tensor_tensor(
                st[:, so + 5:so + 6], st[:, so + 2:so + 3], -1.0,
                st[:, so + 2:so + 3], ALU.mult, ALU.mult))
            nvar = P.add("vector", lambda e, so=so: e.tensor_add(
                st[:, so + 3:so + 4], st[:, so + 3:so + 4], st[:, so + 5:so + 6]),
                inc=tV)
            P.wait("scalar", tV, nvar)
            P.addf("scalar", lambda e, so=so: e.activation(
                st[:, so + 5:so + 6], st[:, so + 3:so + 4], AF.Ln))
            nrstd = P.add("scalar", lambda e, so=so: e.activation(
                st[:, so + 4:so + 5], st[:, so + 5:so + 6], AF.Exp, scale=-0.5),
                inc=tA)
            P.wait("vector", tA, nrstd)
            if xnt_free[par]:
                P.wait("vector", tA, xnt_free[par])
            P.addf("vector", lambda e, so=so: e.scalar_tensor_tensor(
                st[:, so + 5:so + 6], st[:, so + 2:so + 3], -1.0,
                st[:, so + 4:so + 5], ALU.mult, ALU.mult))
            xnt = self.xnt[:, par * C:(par + 1) * C]
            nxn = P.add("vector", lambda e, xt=xt, xnt=xnt, so=so: e.tensor_scalar(
                xnt, xt, st[:, so + 4:so + 5], st[:, so + 5:so + 6],
                ALU.mult, ALU.add), inc=tV)
            dve_read[par] = nxn
            P.wait("tensor", tV, nxn)
            if evac_done[par]:
                P.wait("tensor", tA, evac_done[par])
            pk = packs[par]
            for ck in range(NC):
                P.add("tensor", lambda e, ck=ck, pk=pk, xnt=xnt: e.transpose(
                    self.psb(pk, 128, base=ck * 128), xnt[:, ck * 128:(ck + 1) * 128],
                    self.ident_bf), inc=(tT if ck == NC - 1 else None))
            P.wait("scalar", tT, tT.n)
            dst = self.xnb.rearrange("p (ck t) -> p ck t", ck=NC)[:, :, n * 128:(n + 1) * 128]
            srcv = self.psb(pk, 1024).rearrange("p (ck t) -> p ck t", ck=NC)
            ne = P.add("scalar", lambda e, dst=dst, srcv=srcv: e.activation(
                dst, srcv, AF.Copy), inc=tA)
            evac_done[par] = ne
            xnt_free[par] = ne
        self.ln_tA = tA.n

    def _ffn(self):
        P = self.P
        qG, tV, tA, tT = P.sem("qG"), P.sem("tV"), P.sem("tA"), P.sem("tT")
        xr = self.av(0, 128, NC * T, BF16)
        xk = self.av(32 * KB, 128, NC * T, BF16)
        kact = self.av(64 * KB, 128, NC * T, BF16)
        r_bf = self.av(32 * KB, 128, NC * T, BF16)
        ffn_C = self.av(0, 128, NC * T, BF16)
        wbuf = self.av(96 * KB, 128, NC * 1024, BF16)
        self.ffn_C = ffn_C

        P.addf("vector", lambda e: e.memset(self.px[:, 0:1], 0.0))
        P.wait("vector", P.sem("tA"), self.ln_tA)
        d = self.scr[:, 0:2048]
        xr_v = xr.rearrange("p (ck t) -> p ck t", ck=NC)
        xk_v = xk.rearrange("p (ck t) -> p ck t", ck=NC)
        for ck in range(NC):
            src = self.xnb[:, ck * T:(ck + 1) * T]
            P.addf("vector", lambda e, src=src: e.tensor_copy(self.px[:, 1:T + 1], src))
            P.addf("vector", lambda e: e.tensor_sub(d, self.px[:, 1:T + 1],
                                                    self.px[:, 0:T]))
            P.add("vector", lambda e, ck=ck: e.scalar_tensor_tensor(
                xr_v[:, ck], d, self.mu_pp[:, ck:ck + 1], self.px[:, 0:T],
                ALU.mult, ALU.add))
            self.mix_done = P.add("vector", lambda e, ck=ck: e.scalar_tensor_tensor(
                xk_v[:, ck], d, self.mu_pp[:, 8 + ck:9 + ck], self.px[:, 0:T],
                ALU.mult, ALU.add), inc=tV)

        def load_w(name, wait_t):
            P.wait("gpsimd", tT, wait_t)
            return P.add("gpsimd", lambda e, a=self.g[name]: e.dma_start(
                out=wbuf, in_=a), inc=qG, k=16)

        banks8 = [self.pbA, self.pbB, self.pbC, self.pbD] + [self.ps_s] * 4
        bank_base = [0, 0, 0, 0, 0, 512, 1024, 1536]

        def bank(i):
            return banks8[i], bank_base[i]

        nwk = load_w("w_k", 0)
        P.wait("tensor", qG, nwk)
        P.wait("tensor", tV, self.mix_done)
        kact_v = kact.rearrange("p (dk t) -> p dk t", dk=NC)
        tmp = [self.scr[:, 0:512], self.scr[:, 512:1024]]
        tmp_tA = [0, 0]
        bank_free = [0] * 8
        for dk in range(NC):
            for ns in range(4):
                bi = (dk % 2) * 4 + ns
                bt, bb = bank(bi)
                P.wait("tensor", tV, bank_free[bi])
                for ck in range(NC):
                    P.add("tensor", lambda e, dk=dk, ns=ns, ck=ck, bt=bt, bb=bb:
                          e.matmul(self.psf(bt, 512, base=bb),
                                   wbuf[:, ck * 1024 + dk * 128:ck * 1024 + (dk + 1) * 128],
                                   xk[:, ck * T + ns * 512:ck * T + (ns + 1) * 512],
                                   start=(ck == 0), stop=(ck == NC - 1)),
                          inc=(tT if ck == NC - 1 else None))
                nmm = tT.n
                wpar = (dk * 4 + ns) % 2
                P.wait("vector", tT, nmm)
                P.wait("vector", tA, tmp_tA[wpar])
                nrelu = P.add("vector", lambda e, bt=bt, bb=bb, wpar=wpar:
                              e.tensor_scalar(tmp[wpar], self.psf(bt, 512, base=bb),
                                              0.0, None, ALU.max), inc=tV)
                bank_free[bi] = nrelu
                P.wait("scalar", tV, nrelu)
                tmp_tA[wpar] = P.add("scalar", lambda e, dk=dk, ns=ns, wpar=wpar:
                                     e.activation(kact_v[:, dk, ns * 512:(ns + 1) * 512],
                                                  tmp[wpar], AF.Square), inc=tA)
        k_mm_done = tT.n
        kact_tA = tA.n

        nwr = load_w("w_r", k_mm_done)
        P.wait("tensor", qG, nwr)
        rbf_v = r_bf.rearrange("p (dk t) -> p dk t", dk=NC)
        r_bank_tA = [0] * 4
        for dk in range(NC):
            for ns in range(4):
                bi = ns
                bt, bb = bank(bi)
                P.wait("tensor", tV, bank_free[bi])
                P.wait("tensor", tA, r_bank_tA[bi])
                for ck in range(NC):
                    P.add("tensor", lambda e, dk=dk, ns=ns, ck=ck, bt=bt, bb=bb:
                          e.matmul(self.psf(bt, 512, base=bb),
                                   wbuf[:, ck * 1024 + dk * 128:ck * 1024 + (dk + 1) * 128],
                                   xr[:, ck * T + ns * 512:ck * T + (ns + 1) * 512],
                                   start=(ck == 0), stop=(ck == NC - 1)),
                          inc=(tT if ck == NC - 1 else None))
                nmm = tT.n
                P.wait("scalar", tT, nmm)
                r_bank_tA[bi] = P.add("scalar", lambda e, dk=dk, ns=ns, bt=bt, bb=bb:
                                      e.activation(rbf_v[:, dk, ns * 512:(ns + 1) * 512],
                                                   self.psf(bt, 512, base=bb),
                                                   AF.Sigmoid), inc=tA)
        r_mm_done = tT.n
        sig_all = tA.n

        nwv = load_w("w_v", r_mm_done)
        P.wait("tensor", qG, nwv)
        P.wait("tensor", tA, kact_tA)
        ffnC_v = ffn_C.rearrange("p (dk t) -> p dk t", dk=NC)
        v_bank_tV = [bank_free[4 + i] for i in range(4)]
        v_bank_tV = [0, 0, 0, 0]
        for i in range(4):
            v_bank_tV[i] = bank_free[4 + i]
        for dk in range(NC):
            for ns in range(4):
                bi = 4 + ns
                bt, bb = bank(bi)
                P.wait("tensor", tV, v_bank_tV[ns])
                for ck in range(NC):
                    P.add("tensor", lambda e, dk=dk, ns=ns, ck=ck, bt=bt, bb=bb:
                          e.matmul(self.psf(bt, 512, base=bb),
                                   wbuf[:, ck * 1024 + dk * 128:ck * 1024 + (dk + 1) * 128],
                                   kact[:, ck * T + ns * 512:ck * T + (ns + 1) * 512],
                                   start=(ck == 0), stop=(ck == NC - 1)),
                          inc=(tT if ck == NC - 1 else None))
                nmm = tT.n
                P.wait("vector", tT, nmm)
                P.wait("vector", tA, sig_all if (dk == 0 and ns == 0) else 0)
                v_bank_tV[ns] = P.add("vector", lambda e, dk=dk, ns=ns, bt=bt, bb=bb:
                                      e.tensor_mul(ffnC_v[:, dk, ns * 512:(ns + 1) * 512],
                                                   self.psf(bt, 512, base=bb),
                                                   rbf_v[:, dk, ns * 512:(ns + 1) * 512]),
                                      inc=tV)
        self.ffn_tV = tV.n
        self.ffn_v_mm_done = tT.n
        self.dbg_store("ffn", ffn_C, (128, NC * T), BF16, [(tV, tV.n)])

    def _attn(self):
        P = self.P
        qG, tV, tA, tT = P.sem("qG"), P.sem("tV"), P.sem("tA"), P.sem("tT")
        wqkv = self.av(32 * KB, 128, NC * 3 * C, BF16)
        yD = self.av(80 * KB, 128, NC * T, BF16)
        QT = self.av(112 * KB, 64, T, BF16)
        KT = self.av(116 * KB, 64, T, BF16)
        VTP = self.av(125 * KB, 128, T, BF16)
        VTm = self.av(120 * KB, 128, NT * (HD + 1), BF16)
        PTc = [self.av(122 * KB + 256, 128, 512, BF16),
               self.av(123 * KB + 256, 128, 512, BF16)]
        scrb = self.scr.bitcast(BF16)
        rinv_bf = scrb[0:1, 0:512]
        rbc_sb = self.scr[0:64, 512:1024]
        tm_h = self.px[:, 0:1536].bitcast(BF16)
        self.yD = yD

        nwq = None
        P.wait("gpsimd", tT, self.ffn_v_mm_done)
        nwq = P.add("gpsimd", lambda e: e.dma_start(out=wqkv, in_=self.g["w_qkv"]),
                    inc=qG, k=16)
        P.wait("tensor", tV, tV.n)
        P.wait("tensor", tA, tA.n)
        kvg = getattr(self, "_kv_share_done", None)
        if kvg:
            for s_, n_ in kvg:
                P.wait("vector", s_, n_)
                P.wait("scalar", s_, n_)
            self._kv_share_done = None
        P.wait("tensor", qG, nwq)
        P.wait("tensor", qG, self.n_const_g)
        st8 = self.stat
        nvone = P.add("vector", lambda e: e.memset(
            VTm.rearrange("p (n c) -> p n c", c=HD + 1)[:, :, HD], 1.0),
            inc=tV)
        P.wait("scalar", tV, nvone)

        qkv_banks = [self.pbA, self.pbB]
        qkv_free_tA = [0, 0]
        pack_banks = [self.pbC, self.pbD]
        pack_free = [0, 0]
        pack_free_eng = [None, None]

        for h in range(NHEAD):
            P.wait("tensor", tV, tV.n)
            P.wait("tensor", tA, tA.n)
            for slab in range(6):
                par = slab % 2
                bt = qkv_banks[par]
                P.wait("tensor", tA, qkv_free_tA[par])
                for ck in range(NC):
                    P.add("tensor", lambda e, h=h, slab=slab, ck=ck, bt=bt: e.matmul(
                        self.psf(bt, 512),
                        self.xnb[:, ck * T + h * 128:ck * T + (h + 1) * 128],
                        wqkv[:, ck * 3072 + slab * 512:ck * 3072 + (slab + 1) * 512],
                        start=(ck == 0), stop=False))
                nmm = P.add("tensor", lambda e, slab=slab, bt=bt: e.matmul(
                    self.psf(bt, 512), self.ones_bf[0:1, 0:128],
                    self.bqkv_row[0:1, slab * 512:(slab + 1) * 512],
                    start=False, stop=True), inc=tT)
                P.wait("scalar", tT, nmm)
                qkv_free_tA[par] = P.add("scalar", lambda e, slab=slab, bt=bt:
                                         e.activation(tm_h[:, slab * 512:(slab + 1) * 512],
                                                      self.psf(bt, 512), AF.Copy),
                                         inc=tA)
            qkv_done_tA = tA.n

            def emit_pack(src_col0, dst_fm, s_lo, pk_i, on_act):
                pk = pack_banks[pk_i]
                P.wait("tensor", tA, qkv_done_tA)
                if pack_free[pk_i]:
                    P.wait("tensor", P.sem(pack_free_eng[pk_i]), pack_free[pk_i])
                for i in range(8):
                    s = s_lo + i
                    P.add("tensor", lambda e, s=s, i=i, pk=pk, src_col0=src_col0:
                          e.transpose(self.psb(pk, 128, base=i * 128, parts=64),
                                      tm_h[:, src_col0 + s * 64:src_col0 + (s + 1) * 64],
                                      self.ident_bf),
                          inc=(tT if i == 7 else None))
                ntr = tT.n
                eng = "scalar" if on_act else "vector"
                P.wait(eng, tT, ntr)
                dstv = dst_fm.rearrange("d (j s) -> d s j", s=16)[:, s_lo:s_lo + 8, :]
                srcv = self.psb(pk, 1024, parts=64).rearrange("d (s j) -> d s j", s=8)
                if on_act:
                    nn = P.add("scalar", lambda e, dstv=dstv, srcv=srcv:
                               e.activation(dstv, srcv, AF.Copy), inc=tA)
                else:
                    nn = P.add("vector", lambda e, dstv=dstv, srcv=srcv:
                               e.tensor_copy(dstv, srcv), inc=tV)
                pack_free[pk_i] = nn
                pack_free_eng[pk_i] = "tA" if on_act else "tV"
                return nn

            emit_pack(0, QT, 0, 0, False)
            emit_pack(0, QT, 8, 1, False)
            emit_pack(C, KT, 0, 0, False)
            nk = emit_pack(C, KT, 8, 1, False)
            emit_pack(2 * C, VTP[0:64, :], 0, 0, True)
            nvt = emit_pack(2 * C, VTP[0:64, :], 8, 1, True)
            pk = pack_banks[0]
            P.wait("tensor", P.sem(pack_free_eng[0]), pack_free[0])
            P.wait("tensor", tA, nvt)
            for gph in range(16):
                P.add("tensor", lambda e, gph=gph, pk=pk: e.transpose(
                    self.psb(pk, 64, base=gph * 64),
                    VTP[0:64, gph * 128:(gph + 1) * 128],
                    self.ident_bf[0:64, 0:64]),
                    inc=(tT if gph == 15 else None))
            nvtm_tr = tT.n
            P.wait("scalar", tT, nvtm_tr)
            nvtm = P.add("scalar", lambda e, pk=pk: e.activation(
                VTm.rearrange("p (n c) -> p n c", c=HD + 1)[:, :, 0:HD],
                self.psb(pk, 1024).rearrange("p (n c) -> p n c", c=HD),
                AF.Copy), inc=tA)
            pack_free[0] = nvtm
            pack_free_eng[0] = "tA"

            P.wait("tensor", tV, nk)
            P.wait("tensor", tA, nvtm)
            ybase = [1024, 1536]
            prev_y_ev = [0, 0]
            prev_rbc_tA = 0
            for qg in range(4):
                nch = 4 * qg + 4
                gp = qg % 2
                qc0 = qg * 512
                nst = [0] * nch
                nexpc = [0] * nch
                npvc = [0] * nch
                P.wait("tensor", tV, prev_y_ev[gp])
                for kt in range(nch + 1):
                    sp = kt % 2
                    if kt < nch:
                        if kt >= 2:
                            P.wait("tensor", tA, nexpc[kt - 2])
                        masked = kt >= 4 * qg
                        P.add("tensor", lambda e, kt=kt, sp=sp, qc0=qc0,
                              masked=masked: e.matmul(
                            self.psf(self.ps_s, 512, base=sp * 512),
                            KT[:, kt * 128:(kt + 1) * 128], QT[:, qc0:qc0 + 512],
                            start=True, stop=not masked),
                            inc=(None if masked else tT))
                        if masked:
                            pp = kt - 4 * qg
                            P.add("tensor", lambda e, sp=sp, pp=pp: e.matmul(
                                self.psf(self.ps_s, 512, base=sp * 512),
                                self.ident_bf,
                                self.mask4t[:, pp * 512:(pp + 1) * 512],
                                start=False, stop=True), inc=tT)
                        nst[kt] = tT.n
                        P.wait("scalar", tT, nst[kt])
                        if kt >= 2:
                            P.wait("scalar", tT, npvc[kt - 2])
                        nexpc[kt] = P.add("scalar", lambda e, sp=sp: e.activation(
                            PTc[sp], self.psf(self.ps_s, 512, base=sp * 512),
                            AF.Exp, scale=0.125), inc=tA)
                    if kt >= 1:
                        cj = kt - 1
                        cp = cj % 2
                        P.wait("tensor", tA, nexpc[cj])
                        npvc[cj] = P.add("tensor", lambda e, cp=cp, gp=gp,
                                         cj=cj, nch=nch: e.matmul(
                            self.psf(self.ps_s, 512, base=ybase[gp], parts=65),
                            VTm[:, cj * (HD + 1):(cj + 1) * (HD + 1)],
                            PTc[cp],
                            start=(cj == 0), stop=(cj == nch - 1)), inc=tT)
                P.wait("vector", tT, npvc[nch - 1])
                def _recip(e, gp=gp):
                    with self.nc.allow_low_precision(
                            reason="rinv rounded to bf16 matmul operand"):
                        return e.reciprocal(
                            rinv_bf, self.psf(self.ps_s, 512, base=ybase[gp],
                                              parts=1, pbase=64))
                nrec = P.add("vector", _recip, inc=tV)
                P.wait("tensor", tV, nrec)
                P.wait("tensor", tA, prev_rbc_tA)
                nbc = P.add("tensor", lambda e: e.matmul(
                    self.psf(self.pbD, 512, parts=64),
                    self.ones_bf[0:1, 0:64], rinv_bf,
                    start=True, stop=True), inc=tT)
                P.wait("scalar", tT, nbc)
                prev_rbc_tA = P.add("scalar", lambda e: e.activation(
                    rbc_sb, self.psf(self.pbD, 512, parts=64), AF.Copy),
                    inc=tA)
                P.wait("vector", tA, prev_rbc_tA)
                dst = yD[(h % 2) * 64:(h % 2) * 64 + 64,
                         (h // 2) * T + qg * 512:(h // 2) * T + (qg + 1) * 512]
                prev_y_ev[gp] = P.add("vector", lambda e, dst=dst, gp=gp:
                                      e.tensor_mul(
                    dst, self.psf(self.ps_s, 512, base=ybase[gp], parts=64),
                    rbc_sb), inc=tV)
        self.attn_tA = tA.n
        self.attn_mm_done = tT.n
        self.dbg_store("yd", yD, (128, NC * T), BF16, [(tA, tA.n), (tV, tV.n)])

    def _Pv(self, L):
        return self.av(120 * KB, 128, L, BF16)

    def _proj_resid(self, x_dram=None, xo_dram=None):
        if x_dram is None:
            x_dram = self.x_in
        if xo_dram is None:
            xo_dram = self.xo
        P = self.P
        qG, qS, tV, tA, tT = (P.sem(s) for s in ("qG", "qS", "tV", "tA", "tT"))
        qXo = P.sem("qXo")
        wproj = self.av(32 * KB, 128, NC * C, BF16)
        x1s = [self.av(48 * KB, 128, C, F32), self.av(52 * KB, 128, C, F32)]
        ffn_C = self.ffn_C
        yD = self.yD

        P.wait("gpsimd", tT, self.attn_mm_done)
        nwp = P.add("gpsimd", lambda e: e.dma_start(out=wproj, in_=self.g["w_proj"]),
                    inc=qG, k=16)
        P.wait("tensor", tV, tV.n)
        P.wait("tensor", tA, tA.n)
        P.wait("tensor", qG, nwp)
        banks = [self.pbA, self.pbB]
        bank_free_tV = [0, 0]
        comb_tV = 0
        for cko in range(NC):
            for ns in range(4):
                par = (cko * 4 + ns) % 2
                bt = banks[par]
                P.wait("tensor", tV, bank_free_tV[par])
                for ck in range(NC):
                    P.add("tensor", lambda e, cko=cko, ns=ns, ck=ck, bt=bt: e.matmul(
                        self.psf(bt, 512),
                        wproj[:, ck * C + cko * 128:ck * C + (cko + 1) * 128],
                        yD[:, ck * T + ns * 512:ck * T + (ns + 1) * 512],
                        start=(ck == 0), stop=(ck == NC - 1)),
                        inc=(tT if ck == NC - 1 else None))
                nmm = tT.n
                P.wait("vector", tT, nmm)
                cv = ffn_C.rearrange("p (ck t) -> p ck t", ck=NC)[:, cko,
                                                                  ns * 512:(ns + 1) * 512]
                bank_free_tV[par] = P.add("vector", lambda e, cko=cko, cv=cv, bt=bt:
                                          e.scalar_tensor_tensor(
                                              cv, self.psf(bt, 512),
                                              self.bproj_pp[:, cko:cko + 1], cv,
                                              ALU.add, ALU.add), inc=tV)
        comb_tV = tV.n

        self.x1_tile_ready = []
        packs = [self.pbC, self.pbD]
        pack_free_tV = [0, 0]
        x1s_free = [0, 0]
        xt_slots = [self.scr[:, 0:1024], self.scr[:, 1024:2048]]
        xt_read_tV = [0, 0]
        P.wait("tensor", tV, comb_tV)
        P.wait("sync", tV, comb_tV)
        P.wait("sync", tA, tA.n)
        for n in range(NT):
            par = n % 2
            P.wait("sync", tV, xt_read_tV[par])
            ldn = P.add("sync", lambda e, n=n, par=par, x_dram=x_dram: e.dma_start(
                out=xt_slots[par], in_=x_dram[n * 128:(n + 1) * 128, :]),
                inc=qS, k=16)
            pk = packs[par]
            P.wait("tensor", tV, pack_free_tV[par])
            for ck in range(NC):
                P.add("tensor", lambda e, n=n, ck=ck, pk=pk: e.transpose(
                    self.psb(pk, 128, base=ck * 128),
                    ffn_C[:, ck * T + n * 128:ck * T + (n + 1) * 128],
                    self.ident_bf), inc=(tT if ck == NC - 1 else None))
            ntr = tT.n
            P.wait("vector", tT, ntr)
            P.wait("vector", qS, ldn)
            P.wait("vector", qXo, x1s_free[par])
            nadd = P.add("vector", lambda e, par=par, pk=pk: e.tensor_add(
                x1s[par], xt_slots[par], self.psb(pk, 1024)), inc=tV)
            pack_free_tV[par] = nadd
            xt_read_tV[par] = nadd
            P.wait("sync", tV, nadd)
            nst = P.add("sync", lambda e, n=n, par=par, xo_dram=xo_dram: e.dma_start(
                out=xo_dram[n * 128:(n + 1) * 128, :], in_=x1s[par]), inc=qXo, k=16)
            x1s_free[par] = nst
            self.x1_tile_ready.append((qXo, nst))

    def _qkv1(self):
        P = self.P
        qG, qS, tV, tA, tT = (P.sem(s) for s in ("qG", "qS", "tV", "tA", "tT"))
        qQKV = P.sem("qQKV")
        wq1 = self.av(112 * KB, 128, NC * HD, BF16)
        wk1 = self.av(113 * KB, 128, NC * HD, BF16)
        wv1 = self.av(114 * KB, 128, NC * HD, BF16)
        v1t = self.av(115 * KB, 128, NT * HD, BF16)
        q1 = self.av(117 * KB, 64, T, BF16)
        k1 = self.av(121 * KB, 64, T, BF16)
        v1f = self.av(125 * KB, 64, T, BF16)

        P.wait("gpsimd", tT, self.attn_mm_done)
        nl = None
        for w, name in ((wq1, "w_q1"), (wk1, "w_k1"), (wv1, "w_v1")):
            nl = P.add("gpsimd", lambda e, w=w, a=self.g[name]: e.dma_start(
                out=w, in_=a), inc=qG, k=16)
        P.wait("tensor", tV, P.sem("tV").n)
        P.wait("tensor", tA, self.ln_tA)
        P.wait("tensor", qG, nl)
        banks = [self.pbA, self.pbB]
        bank_free_tV = [0, 0]
        outs = [(wq1, q1, 0), (wk1, k1, 1), (wv1, v1f, 2)]
        for wi, (w, dstf, bcol) in enumerate(outs):
            for ns in range(4):
                par = ns % 2
                bt = banks[par]
                P.wait("tensor", tV, bank_free_tV[par])
                for ck in range(NC):
                    P.add("tensor", lambda e, w=w, ck=ck, ns=ns, bt=bt: e.matmul(
                        self.psf(bt, 512, parts=64),
                        w[:, ck * HD:(ck + 1) * HD],
                        self.xnb[:, ck * T + ns * 512:ck * T + (ns + 1) * 512],
                        start=(ck == 0), stop=(ck == NC - 1)),
                        inc=(tT if ck == NC - 1 else None))
                nmm = tT.n
                P.wait("vector", tT, nmm)
                bank_free_tV[par] = P.add("vector", lambda e, dstf=dstf, ns=ns,
                                          bcol=bcol, bt=bt: e.tensor_scalar(
                    dstf[:, ns * 512:(ns + 1) * 512], self.psf(bt, 512, parts=64),
                    self.bqkv1_pp[:, bcol:bcol + 1], None, ALU.add), inc=tV)
        qkv_tV = tV.n
        P.wait("tensor", tV, qkv_tV)
        for gph in range(16):
            P.add("tensor", lambda e, gph=gph: e.transpose(
                self.psb(self.pbC, 64, base=gph * 64),
                v1f[:, gph * 128:(gph + 1) * 128], self.ident_bf[0:64, 0:64]),
                inc=(tT if gph == 15 else None))
        nvt = tT.n
        P.wait("scalar", tT, nvt)
        nvta = P.add("scalar", lambda e: e.activation(v1t, self.psb(self.pbC, 1024),
                                                      AF.Copy), inc=tA)
        P.wait("sync", tV, qkv_tV)
        P.add("sync", lambda e: e.dma_start(out=self.qo, in_=q1), inc=qQKV, k=16)
        P.add("sync", lambda e: e.dma_start(out=self.ko, in_=k1), inc=qQKV, k=16)
        P.wait("sync", tA, nvta)
        P.add("sync", lambda e: e.dma_start(out=self.vo, in_=v1t), inc=qQKV, k=16)
        self.qkv1_pbc_tA = nvta

    def _gate(self):
        P = self.P
        tV, tA, tT, qG = P.sem("tV"), P.sem("tA"), P.sem("tT"), P.sem("qG")
        gs = self.gsc
        P.add("vector", lambda e: e.memset(
            self.rw_tm.rearrange("p (n e) -> p n e", e=NEXP)[:, :, 8], 1.0))
        P.wait("vector", tV, self.gbb_done)
        P.wait("tensor", qG, self.n_const_g)
        exp_tA = 0
        rwfm_tV = 0
        for n in range(NT):
            P.wait("tensor", tA, exp_tA)
            P.wait("tensor", tA, self.qkv1_pbc_tA if n == 0 else 0)
            for ck in range(NC):
                P.add("tensor", lambda e, n=n, ck=ck: e.matmul(
                    self.psf(self.pbD, 8),
                    self.xnb[:, ck * T + n * 128:ck * T + (n + 1) * 128],
                    self.wgate[:, ck * 8:(ck + 1) * 8],
                    start=(ck == 0), stop=(ck == NC - 1)),
                    inc=(tT if ck == NC - 1 else None))
            nmm = tT.n
            P.wait("vector", tT, nmm)
            nng = P.add("vector", lambda e: e.reduce_max(
                gs[:, 24:25], self.psf(self.pbD, 8), axis=AX.X, negate=True), inc=tV)
            P.wait("scalar", tV, nng)
            exp_tA = P.add("scalar", lambda e: e.activation(
                gs[:, 0:8], self.psf(self.pbD, 8), AF.Exp, bias=gs[:, 24:25],
                accum_out=gs[:, 27:28]), inc=tA)
            P.wait("vector", tA, exp_tA)
            P.addf("vector", lambda e: e.reciprocal(gs[:, 28:29], gs[:, 27:28]))
            P.addf("vector", lambda e: e.tensor_scalar(
                gs[:, 0:8], gs[:, 0:8], gs[:, 28:29], None, ALU.mult))
            P.addf("vector", lambda e: e.tensor_add(gs[:, 8:16], gs[:, 0:8], self.gbb))
            P.addf("vector", lambda e: e.reduce_max(gs[:, 24:25], gs[:, 8:16], axis=AX.X))
            P.addf("vector", lambda e: e.tensor_scalar(
                gs[:, 16:24], gs[:, 8:16], gs[:, 24:25], None, ALU.is_ge))
            P.addf("vector", lambda e: e.scalar_tensor_tensor(
                gs[:, 16:24], gs[:, 16:24], -1e30, gs[:, 8:16], ALU.mult, ALU.add))
            P.addf("vector", lambda e: e.reduce_max(gs[:, 25:26], gs[:, 16:24], axis=AX.X))
            P.addf("vector", lambda e: e.tensor_scalar(
                gs[:, 16:24], gs[:, 8:16], gs[:, 25:26], None, ALU.is_ge))
            rw = self.rw_tm[:, n * NEXP:n * NEXP + 8]
            nrw = P.add("vector", lambda e, rw=rw: e.tensor_mul(
                rw, gs[:, 0:8], gs[:, 16:24]), inc=tV)
            P.wait("tensor", tV, nrw)
            P.wait("tensor", tV, rwfm_tV)
            ntr = P.add("tensor", lambda e, n=n: e.transpose(
                self.psf(self.pbC, 128, parts=NEXP),
                self.rw_tm[:, n * NEXP:(n + 1) * NEXP], self.ident_f32), inc=tT)
            P.wait("vector", tT, ntr)
            rwfm_tV = P.add("vector", lambda e, n=n: e.tensor_copy(
                self.rw_fm[:, n * 128:(n + 1) * 128],
                self.psf(self.pbC, 128, parts=NEXP)), inc=tV)
        self.gate_tV = tV.n
        self.gate_mm_tT = tT.n
        self.dbg_store("rw", self.rw_tm, (128, NT * NEXP), F32, [(tV, tV.n)])

    def _moe(self):
        P = self.P
        qG, tV, tA, tT = P.sem("qG"), P.sem("tV"), P.sem("tA"), P.sem("tT")
        qM, qXo = P.sem("qM"), P.sem("qXo")
        h_buf = self.av(0, 128, NHK * T, BF16)
        blk = [[self.av(44 * KB + (i * 2 + p) * 2 * KB, 128, 1024, BF16)
                for p in range(2)] for i in range(2)]
        w2 = self.av(52 * KB, 128, NHK * C, BF16)
        stg = [self.av(74 * KB, 128, C, F32), self.av(78 * KB, 128, C, F32)]

        tile_chain = [self.x1_tile_ready[n] for n in range(NT)]
        P.wait("gpsimd", qXo, P.sem("qXo").n)

        P.wait("tensor", tV, self.gate_tV)
        P.wait("tensor", qG, self.n_const_g)
        stg_free = [0, 0]
        psq_free = [0, 0, 0, 0]
        for n in range(NT):
            par = n % 2
            for cs in range(2):
                qi = par * 2 + cs
                P.wait("tensor", tV, psq_free[qi])
                P.add("tensor", lambda e, n=n, cs=cs, qi=qi: e.matmul(
                    self.psf(self.ps_s, 512, base=qi * 512),
                    self.rw_fm[:, n * 128:(n + 1) * 128],
                    self.b2cat[:, cs * 512:(cs + 1) * 512],
                    start=True, stop=True), inc=(tT if cs == 1 else None))
            nmm = tT.n
            P.wait("vector", tT, nmm)
            P.wait("vector", qM, stg_free[par])
            ncp = None
            for cs in range(2):
                qi = par * 2 + cs
                psq_free[qi] = P.add("vector", lambda e, par=par, cs=cs, qi=qi:
                                     e.tensor_copy(stg[par][:, cs * 512:(cs + 1) * 512],
                                                   self.psf(self.ps_s, 512, base=qi * 512)),
                                     inc=tV)
            ncp = psq_free[par * 2 + 1]
            P.wait("gpsimd", tV, ncp)
            cs_, cn_ = tile_chain[n]
            P.wait("gpsimd", cs_, cn_)
            nadd = P.add("gpsimd", lambda e, n=n, par=par: e.dma_start(
                out=self.xo[n * 128:(n + 1) * 128, :], in_=stg[par],
                accum_op=ALU.add), inc=qM, k=16)
            stg_free[par] = nadd
            tile_chain[n] = (qM, nadd)

        hv = h_buf.rearrange("p (hk t) -> p hk t", hk=NHK)
        w2v = w2.rearrange("p (hk c) -> p hk c", hk=NHK)
        esA = [self.scr[:, 0:512], self.scr[:, 512:1024]]
        esB = [self.scr[:, 1024:1536], self.scr[:, 1536:2048]]
        blk_free = [[0, 0], [0, 0]]
        w2_free = 0
        h_free = 0
        bankA = [(self.pbA, self.pbB), (self.pbC, self.pbD)]
        h1_free = [0, 0]
        h3_free = [0, 0]
        es_free = [0, 0]
        psh_free = [0, 0]
        P.wait("tensor", tV, tV.n)
        P.wait("tensor", tA, tA.n)
        for e_i in range(NEXP):
            P.wait("gpsimd", tT, w2_free)
            nw2 = None
            for hk in range(NHK):
                nw2 = P.add("gpsimd", lambda e, e_i=e_i, hk=hk: e.dma_start(
                    out=w2v[:, hk], in_=self.g["w2b"][e_i * NHK + hk]),
                    inc=qG, k=16)
            w2_loaded = nw2
            P.wait("scalar", tT, h_free)
            for hk in range(NHK):
                par = hk % 2
                P.wait("gpsimd", tT, blk_free[0][par])
                n1 = P.add("gpsimd", lambda e, e_i=e_i, hk=hk, par=par: e.dma_start(
                    out=blk[0][par], in_=self.g["w1b"][e_i * NHK + hk]), inc=qG, k=16)
                P.wait("gpsimd", tT, blk_free[1][par])
                n3 = P.add("gpsimd", lambda e, e_i=e_i, hk=hk, par=par: e.dma_start(
                    out=blk[1][par], in_=self.g["w3b"][e_i * NHK + hk]), inc=qG, k=16)
                bt1, bt3 = bankA[par]
                P.wait("tensor", qG, n3)
                for ns in range(4):
                    P.wait("tensor", tV, h1_free[par])
                    for ck in range(NC):
                        P.add("tensor", lambda e, hk=hk, ns=ns, ck=ck, par=par, bt1=bt1:
                              e.matmul(self.psf(bt1, 512),
                                       blk[0][par][:, ck * 128:(ck + 1) * 128],
                                       self.xnb[:, ck * T + ns * 512:ck * T + (ns + 1) * 512],
                                       start=(ck == 0), stop=(ck == NC - 1)),
                              inc=(tT if ck == NC - 1 else None))
                    n1mm = tT.n
                    P.wait("tensor", tA, h3_free[par])
                    for ck in range(NC):
                        P.add("tensor", lambda e, hk=hk, ns=ns, ck=ck, par=par, bt3=bt3:
                              e.matmul(self.psf(bt3, 512),
                                       blk[1][par][:, ck * 128:(ck + 1) * 128],
                                       self.xnb[:, ck * T + ns * 512:ck * T + (ns + 1) * 512],
                                       start=(ck == 0), stop=(ck == NC - 1)),
                              inc=(tT if ck == NC - 1 else None))
                    n3mm = tT.n
                    bcol = e_i * NHK + hk
                    P.wait("vector", tT, n1mm)
                    P.wait("vector", tA, es_free[par])
                    nt1 = P.add("vector", lambda e, par=par, bcol=bcol, bt1=bt1:
                                e.tensor_scalar(esA[par], self.psf(bt1, 512),
                                                self.b1_pp[:, bcol:bcol + 1], None,
                                                ALU.add), inc=tV)
                    P.wait("vector", tV, nt1)
                    h1_free[par] = nt1
                    P.wait("scalar", tT, n3mm)
                    nt2 = P.add("scalar", lambda e, par=par, bcol=bcol, bt3=bt3:
                                e.activation(esB[par], self.psf(bt3, 512), AF.Identity,
                                             bias=self.b3_pp[:, bcol:bcol + 1]),
                                inc=tA)
                    h3_free[par] = nt2
                    P.wait("vector", tA, nt2)
                    nt3 = P.add("vector", lambda e, par=par: e.tensor_mul(
                        esA[par], esA[par], esB[par]), inc=tV)
                    P.wait("scalar", tV, nt3)
                    nsl = P.add("scalar", lambda e, hk=hk, ns=ns, par=par:
                                e.activation(hv[:, hk, ns * 512:(ns + 1) * 512],
                                             esA[par], AF.Silu), inc=tA)
                    es_free[par] = nsl
                blk_free[0][par] = n1mm
                blk_free[1][par] = n3mm
            silu_all = P.sem("tA").n
            P.wait("tensor", tA, silu_all)
            P.wait("tensor", qG, w2_loaded)
            for tq in range(NT):
                par = tq % 2
                P.wait("tensor", tV, psh_free[par])
                for hk in range(NHK):
                    for cs in range(2):
                        P.add("tensor", lambda e, tq=tq, hk=hk, cs=cs, par=par:
                              e.matmul(self.psf(self.ps_s, 512, base=par * 1024 + cs * 512),
                                       hv[:, hk, tq * 128:(tq + 1) * 128],
                                       w2v[:, hk, cs * 512:(cs + 1) * 512],
                                       start=(hk == 0), stop=(hk == NHK - 1)),
                              inc=(tT if hk == NHK - 1 and cs == 1 else None))
                nmm = tT.n
                P.wait("vector", tT, nmm)
                P.wait("vector", qM, stg_free[par])
                rwcol = self.rw_tm[:, tq * NEXP + e_i:tq * NEXP + e_i + 1]
                nev = None
                for cs in range(2):
                    nev = P.add("vector", lambda e, par=par, cs=cs, rwcol=rwcol:
                                e.tensor_scalar(stg[par][:, cs * 512:(cs + 1) * 512],
                                                self.psf(self.ps_s, 512,
                                                         base=par * 1024 + cs * 512),
                                                rwcol, None, ALU.mult), inc=tV)
                psh_free[par] = nev
                P.wait("gpsimd", tV, nev)
                cs_, cn_ = tile_chain[tq]
                P.wait("gpsimd", cs_, cn_)
                nadd = P.add("gpsimd", lambda e, tq=tq, par=par: e.dma_start(
                    out=self.xo[tq * 128:(tq + 1) * 128, :], in_=stg[par],
                    accum_op=ALU.add), inc=qM, k=16)
                stg_free[par] = nadd
                tile_chain[tq] = (qM, nadd)
            w2_free = tT.n
            h_free = tT.n
        self.x1_tile_ready = tile_chain



class BuildB:

    def __init__(self):
        nc = bass.Bass("TRN2", target_bir_lowering=False)
        self.nc = nc
        P = Prog(nc)
        self.P = P
        xin = nc.dram_tensor("xin", [T, C], F32, kind="ExternalInput").ap()
        qin = nc.dram_tensor("q", [HD, T], BF16, kind="ExternalInput").ap()
        kall = nc.dram_tensor("kall", [HD, GTOK], BF16, kind="ExternalInput").ap()
        vall = nc.dram_tensor("vall", [128, (GTOK // 128) * HD], BF16,
                              kind="ExternalInput").ap()
        identd = nc.dram_tensor("ident_bf", [128, 128], BF16,
                                kind="ExternalInput").ap()
        out = nc.dram_tensor("out", [T, C], F32, kind="ExternalOutput").ap()

        sbt = nc.alloc_sbuf_tensor
        ident = sbt("ident", [128, 128], BF16).ap()
        q1 = sbt("q1", [HD, T], BF16).ap()
        ka = sbt("ka", [HD, GTOK], BF16).ap()
        va = sbt("va", [128, (GTOK // 128) * HD], BF16).ap()
        Pt = sbt("Pt", [128, 2 * 512], BF16).ap()
        pt_sb = sbt("pt_sb", [128, 2 * 512], BF16).ap()
        acc = sbt("acc", [128, HD], F32).ap()
        yt = sbt("yt", [128, HD], F32).ap()
        stt = sbt("stt", [128, 24], F32).ap()
        xts = sbt("xts", [128, 2 * C], F32).ap()
        ots = sbt("ots", [128, 2 * C], F32).ap()

        ps_sc = nc.alloc_psum_tensor("ps_sc", [128, 1024], F32)
        pv = nc.alloc_psum_tensor("pv", [128, 512], F32)
        ptp = [nc.alloc_psum_tensor("ptp0", [128, 512], F32),
               nc.alloc_psum_tensor("ptp1", [128, 512], F32)]

        qS, qO, tV, tA, tT = (P.sem(s) for s in ("qS", "qO", "tV", "tA", "tT"))

        for t, a in ((ident, identd), (q1, qin), (ka, kall), (va, vall)):
            P.add("sync", lambda e, t=t, a=a: e.dma_start(out=t, in_=a), inc=qS, k=16)
        n_ld = qS.n
        P.wait("tensor", qS, n_ld)

        NCH = GTOK // 512
        xt_read = [0, 0]
        ot_free = [0, 0]
        for qt in range(NT):
            par_t = qt % 2
            P.wait("sync", tV, xt_read[par_t])
            ldx = P.add("sync", lambda e, qt=qt, par_t=par_t: e.dma_start(
                out=xts[:, par_t * C:(par_t + 1) * C],
                in_=xin[qt * 128:(qt + 1) * 128, :]), inc=qS, k=16)

            nsc = [0] * (NCH + 1)
            ntr = [0] * (NCH + 1)
            npv = [0] * (NCH + 1)
            nexp = [0] * (NCH + 1)
            nev = [0] * (NCH + 1)
            nevS = [None] * (NCH + 1)
            nacc = [0] * (NCH + 1)
            nb = [0] * (NCH + 1)

            def emit_scores(ci):
                par = ci % 2
                nsc[ci] = P.add("tensor", lambda e, qt=qt, ci=ci, par=par: e.matmul(
                    ps_sc.ap()[:, par * 512:(par + 1) * 512],
                    q1[:, qt * 128:(qt + 1) * 128],
                    ka[:, ci * 512:(ci + 1) * 512], start=True, stop=True), inc=tT)

            P.wait("tensor", tA, tA.n)
            P.wait("tensor", tV, tV.n)
            emit_scores(0)
            for ci in range(NCH + 1):
                par = ci % 2
                so = 8 + 8 * par
                if ci < NCH:
                    P.wait("vector", tT, nsc[ci])
                    P.addf("vector", lambda e, par=par, so=so: e.reduce_max(
                        stt[:, so:so + 1], ps_sc.ap()[:, par * 512:(par + 1) * 512],
                        axis=AX.X, negate=True))
                    if ci == 0:
                        P.addf("vector", lambda e, so=so: e.tensor_copy(
                            stt[:, 0:1], stt[:, so:so + 1]))
                        nb[ci] = P.add("vector", lambda e, so=so: e.tensor_scalar(
                            stt[:, so + 2:so + 3], stt[:, 0:1], 1.0 / 32.0, None,
                            ALU.mult), inc=tV)
                    else:
                        P.addf("vector", lambda e, so=so: e.tensor_scalar(
                            stt[:, so + 1:so + 2], stt[:, so:so + 1], stt[:, 0:1],
                            0.0, ALU.subtract, ALU.min))
                        nb[ci] = P.add("vector", lambda e, so=so: e.tensor_scalar(
                            stt[:, so + 2:so + 3], stt[:, so + 1:so + 2],
                            stt[:, 0:1], 1.0 / 32.0, ALU.add, ALU.mult), inc=tV)
                        P.addf("vector", lambda e, so=so: e.tensor_scalar(
                            stt[:, 0:1], stt[:, so + 1:so + 2], stt[:, 0:1], None,
                            ALU.add))
                if ci < NCH:
                    P.wait("scalar", tV, nb[ci])
                    if ci > 0:
                        P.add("scalar", lambda e, so=so: e.activation(
                            stt[:, so + 3:so + 4], stt[:, so + 1:so + 2], AF.Exp,
                            scale=1.0 / 32.0))
                    if ci >= 2:
                        P.wait("scalar", tT, ntr[ci - 2])
                    nexp[ci] = P.add("scalar", lambda e, par=par, so=so: e.activation(
                        Pt[:, par * 512:(par + 1) * 512],
                        ps_sc.ap()[:, par * 512:(par + 1) * 512], AF.Exp,
                        scale=1.0 / 32.0, bias=stt[:, so + 2:so + 3],
                        accum_out=stt[:, so + 4:so + 5]), inc=tA)
                if ci + 1 < NCH:
                    if ci >= 1:
                        P.wait("tensor", tA, nexp[ci - 1])
                    emit_scores(ci + 1)
                if ci < NCH:
                    P.wait("tensor", tA, nexp[ci])
                    if ci >= 2:
                        P.wait("tensor", nevS[ci - 2], nev[ci - 2])
                    for kt in range(4):
                        P.add("tensor", lambda e, kt=kt, par=par: e.transpose(
                            ptp[par].ap().bitcast(BF16)[:, kt * 128:(kt + 1) * 128],
                            Pt[:, par * 512 + kt * 128:par * 512 + (kt + 1) * 128],
                            ident), inc=(tT if kt == 3 else None))
                    ntr[ci] = tT.n
                if ci >= 1:
                    cj = ci - 1
                    pj = cj % 2
                    P.wait("tensor", nevS[cj], nev[cj])
                    P.wait("tensor", tV, nacc[cj - 1] if cj >= 1 else 0)
                    for kt in range(4):
                        gk = cj * 4 + kt
                        P.add("tensor", lambda e, gk=gk, kt=kt, pj=pj: e.matmul(
                            pv.ap()[:, 0:HD],
                            pt_sb[:, pj * 512 + kt * 128:pj * 512 + (kt + 1) * 128],
                            va[:, gk * HD:(gk + 1) * HD],
                            start=(kt == 0), stop=(kt == 3)),
                            inc=(tT if kt == 3 else None))
                    npv[cj] = tT.n
                if ci < NCH:
                    if par == 0:
                        P.wait("vector", tT, ntr[ci])
                        if ci >= 2:
                            P.wait("vector", tT, npv[ci - 2])
                        nev[ci] = P.add("vector", lambda e, par=par: e.tensor_copy(
                            pt_sb[:, par * 512:(par + 1) * 512],
                            ptp[par].ap().bitcast(BF16)[:, 0:512]), inc=tV)
                        nevS[ci] = tV
                    else:
                        P.wait("scalar", tT, ntr[ci])
                        if ci >= 2:
                            P.wait("scalar", tT, npv[ci - 2])
                        nev[ci] = P.add("scalar", lambda e, par=par: e.activation(
                            pt_sb[:, par * 512:(par + 1) * 512],
                            ptp[par].ap().bitcast(BF16)[:, 0:512], AF.Copy),
                            inc=tA)
                        nevS[ci] = tA
                if ci < NCH:
                    P.wait("vector", tA, nexp[ci])
                    if ci == 0:
                        P.addf("vector", lambda e, so=so: e.tensor_copy(
                            stt[:, 1:2], stt[:, so + 4:so + 5]))
                    else:
                        P.addf("vector", lambda e, so=so: e.scalar_tensor_tensor(
                            stt[:, 1:2], stt[:, 1:2], stt[:, so + 3:so + 4],
                            stt[:, so + 4:so + 5], ALU.mult, ALU.add))
                if ci >= 1:
                    cj = ci - 1
                    soj = 8 + 8 * (cj % 2)
                    P.wait("vector", tT, npv[cj])
                    if cj == 0:
                        nacc[cj] = P.add("vector", lambda e: e.tensor_copy(
                            acc, pv.ap()[:, 0:HD]), inc=tV)
                    else:
                        nacc[cj] = P.add("vector", lambda e, soj=soj:
                                         e.scalar_tensor_tensor(
                                             acc, acc, stt[:, soj + 3:soj + 4],
                                             pv.ap()[:, 0:HD], ALU.mult, ALU.add),
                                         inc=tV)
                    P.wait("vector", tV, nacc[cj])
            P.addf("vector", lambda e: e.reciprocal(stt[:, 2:3], stt[:, 1:2]))
            nyt = P.addf("vector", lambda e: e.tensor_scalar(
                yt, acc, stt[:, 2:3], None, ALU.mult))
            P.wait("vector", qS, ldx)
            P.wait("vector", qO, ot_free[par_t])
            ot = ots[:, par_t * C:(par_t + 1) * C]
            not_ = P.add("vector", lambda e, par_t=par_t, ot=ot: e.tensor_add(
                ot.rearrange("p (r d) -> p r d", d=HD),
                xts[:, par_t * C:(par_t + 1) * C].rearrange("p (r d) -> p r d", d=HD),
                yt.unsqueeze(1).broadcast_to([128, NHEAD, HD])), inc=tV)
            xt_read[par_t] = not_
            P.wait("sync", tV, not_)
            nst = P.add("sync", lambda e, qt=qt, ot=ot: e.dma_start(
                out=out[qt * 128:(qt + 1) * 128, :], in_=ot), inc=qO, k=16)
            ot_free[par_t] = nst
        P.wait("sync", qO, P.sem("qO").n)
        P.finalize()



I32 = mybir.dt.int32
MAGIC = 777777
MAXPOLL = 100000


class BuildF(BuildA):

    def __init__(self, dbgF=False):
        self.stop = "all"
        self.dbg = set()
        self.dbgF = dbgF
        nc = bass.Bass("TRN2", target_bir_lowering=False)
        self.nc = nc
        self.P = Prog(nc)
        self.xmine = nc.dram_tensor("xmine", [4 * T, C], F32,
                                    kind="ExternalInput").ap()
        self.g = {}
        for name, shape, dt in GLOBAL_SPECS:
            self.g[name] = nc.dram_tensor(name, list(shape), dt,
                                          kind="ExternalInput").ap()
        self.out_d = nc.dram_tensor("out", [T, C], F32, kind="ExternalOutput").ap()
        self.xo = nc.dram_tensor("xo", [T, C], F32, kind="Internal").ap()
        self.x1tmp = nc.dram_tensor("x1tmp", [T, C], F32, kind="Internal").ap()
        self.xn0_d = nc.dram_tensor("xn0_d", [128, NC * T], BF16,
                                    kind="Internal").ap()
        self.q1_d = nc.dram_tensor("q1_d", [HD, T], BF16, kind="Internal").ap()
        self.sh_k = nc.dram_tensor("sh_k", [2, 4, HD, T], BF16, kind="Internal",
                                   addr_space="Shared")
        self.sh_v = nc.dram_tensor("sh_v", [2, 4, 128, NT * HD], BF16,
                                   kind="Internal", addr_space="Shared")
        self.sh_f = nc.dram_tensor("sh_f", [2, 4, 128], I32, kind="Internal",
                                   addr_space="Shared")
        self.fwm = nc.alloc_sbuf_tensor("fwm", [1, 128], I32).ap()
        self.fpoll = nc.alloc_sbuf_tensor("fpoll", [1, 128], I32).ap()
        self.fwz = nc.alloc_sbuf_tensor("fwz", [4, 128], I32).ap()
        self._setup_sbuf()
        self._buildF()

    def _x_slice(self, s):
        return self.xmine[s * T:(s + 1) * T, :]

    def _buildF(self):
        P = self.P
        self._load_consts()
        self.fwm_ready = P.add("vector", lambda e: e.memset(self.fwm, MAGIC),
                               inc=P.sem("tV"))
        nz = P.add("vector", lambda e: e.memset(self.fwz, 0), inc=P.sem("tV"))
        P.wait("gpsimd", P.sem("tV"), nz)

        def zero_flags(e):
            engs_ = OrderedEngineSet([mybir.EngineType.Pool])
            if not hasattr(self, "_regs"):
                nc_ = self.nc
                r = {}
                for nmq in ("half", "phalf", "koff", "voff", "foff"):
                    r[nmq] = nc_.alloc_registers("r_" + nmq,
                                                 engines=engs_).handles[0]
                pid = e.partition_id()
                e.reg_mov(r["half"], pid)
                e.reg_alu(r["half"], r["half"], 1, ALU.bitwise_and)
                e.reg_mov(r["phalf"], r["half"])
                e.reg_alu(r["phalf"], r["phalf"], 1, ALU.bitwise_xor)
                self._regs = r
            r = self._regs
            e.reg_mov(r["foff"], r["half"])
            e.reg_alu(r["foff"], r["foff"], 4 * 128, ALU.mult)
            fdst = bass.AP(self.sh_f, r["foff"], [[128, 4], [1, 128]])
            return e.dma_start(out=fdst, in_=self.fwz)

        self.flags_zeroed = P.add("gpsimd", lambda e: zero_flags(e),
                                  inc=P.sem("qSh"), k=16)
        self._kv_share_done = None
        self._xnb_guard = None
        for s in range(4):
            self._ln(src_kind="dram", dram=self._x_slice(s), ready=None)
            self._ffn()
            self._attn()
            xdst = self.xo if s == 0 else self.x1tmp
            self._proj_resid(x_dram=self._x_slice(s), xo_dram=xdst)
            self._ln(src_kind="dram", dram=xdst, ready=self.x1_tile_ready)
            if s == 0:
                self._x1ready_slot0 = list(self.x1_tile_ready)
                self._q_proj_store()
                self.qkv1_pbc_tA = P.sem("tA").n
                self._gate()
                self._store_xn0()
            self._kv_share(s)
        self._reload_xn0()
        self.x1_tile_ready = self._x1ready_slot0
        self._moe()
        self._poll_and_load()
        self._flash_final()
        if self.dbgF:
            st = P.sem("st")
            for name, src, shape, dt in [
                ("kall", self.K_all, (HD, NCORE * T), BF16),
                ("vall", self.V_all, (128, (NCORE * T // 128) * HD), BF16),
                ("q1s", self.q1s, (HD, T), BF16),
                ("xnbf", self.xnb, (128, NC * T), BF16),
            ]:
                od = self.nc.dram_tensor("dbg_" + name, list(shape), dt,
                                         kind="ExternalOutput").ap()
                P.wait("sync", P.sem("tV"), P.sem("tV").n)
                P.add("sync", lambda e, o=od, a=src: e.dma_start(out=o, in_=a),
                      inc=st, k=16)
            if True:
                odf = self.nc.dram_tensor("dbg_flags", [2, 4, 128], I32,
                                          kind="ExternalOutput").ap()
                P.add("sync", lambda e, o=odf: e.dma_start(
                    out=o, in_=bass.AP(self.sh_f, 0, [[512, 1], [1, 1024]]).rearrange(
                        "a (h s t) -> (a h) s t", h=2, s=4)), inc=st, k=16)
                odp = self.nc.dram_tensor("dbg_poll", [1, 128], I32,
                                          kind="ExternalOutput").ap()
                P.add("sync", lambda e, o=odp: e.dma_start(out=o, in_=self.fpoll),
                      inc=st, k=16)
            if hasattr(self, "_fdump"):
                od = self.nc.dram_tensor("dbg_fstats", [128, NT * 4], F32,
                                         kind="ExternalOutput").ap()
                P.wait("sync", P.sem("tV"), P.sem("tV").n)
                P.add("sync", lambda e, o=od: e.dma_start(out=o, in_=self._fdump),
                      inc=st, k=16)
            od = self.nc.dram_tensor("dbg_xo", [T, C], F32,
                                     kind="ExternalOutput").ap()
            qM_ = P.sem("qM")
            P.wait("sync", qM_, qM_.n)
            P.add("sync", lambda e, o=od: e.dma_start(out=o, in_=self.xo),
                  inc=st, k=16)
            P.wait("sync", st, st.n)
        qM = P.sem("qM")
        if qM.n:
            P.wait("gpsimd", qM, qM.n)
        P.finalize()

    def _q_proj_store(self):
        P = self.P
        qG, qS, tV, tT, tA = (P.sem(s) for s in ("qG", "qS", "tV", "tT", "tA"))
        wq1 = self.av(112 * KB, 128, NC * HD, BF16)
        q1 = self.av(117 * KB, 64, T, BF16)
        P.wait("gpsimd", tT, self.attn_mm_done)
        nl = P.add("gpsimd", lambda e: e.dma_start(out=wq1, in_=self.g["w_q1"]),
                   inc=qG, k=16)
        P.wait("tensor", tV, P.sem("tV").n)
        P.wait("tensor", tA, self.ln_tA)
        P.wait("tensor", qG, nl)
        banks = [self.pbA, self.pbB]
        bank_free_tV = [0, 0]
        for ns in range(4):
            par = ns % 2
            bt = banks[par]
            P.wait("tensor", tV, bank_free_tV[par])
            for ck in range(NC):
                P.add("tensor", lambda e, ck=ck, ns=ns, bt=bt: e.matmul(
                    self.psf(bt, 512, parts=64),
                    wq1[:, ck * HD:(ck + 1) * HD],
                    self.xnb[:, ck * T + ns * 512:ck * T + (ns + 1) * 512],
                    start=(ck == 0), stop=(ck == NC - 1)),
                    inc=(tT if ck == NC - 1 else None))
            nmm = tT.n
            P.wait("vector", tT, nmm)
            bank_free_tV[par] = P.add("vector", lambda e, ns=ns, bt=bt:
                                      e.tensor_scalar(
                q1[:, ns * 512:(ns + 1) * 512], self.psf(bt, 512, parts=64),
                self.bqkv1_pp[:, 0:1], None, ALU.add), inc=tV)
        nq = bank_free_tV[1]
        P.wait("sync", tV, nq)
        self.q1_store = P.add("sync", lambda e, q1=q1: e.dma_start(
            out=self.q1_d, in_=q1), inc=P.sem("qQ"), k=16)

    def _store_xn0(self):
        P = self.P
        qG, tA = P.sem("qG"), P.sem("tA")
        P.wait("gpsimd", tA, self.ln_tA)
        self.xn0_store = P.add("gpsimd", lambda e: e.dma_start(
            out=self.xn0_d, in_=self.xnb), inc=qG, k=16)
        self._xnb_guard = (qG, self.xn0_store)

    def _kv_share(self, s):
        P = self.P
        qG, qS, tV, tA, tT = (P.sem(x) for x in ("qG", "qS", "tV", "tA", "tT"))
        qSh = P.sem("qSh")
        wk1 = self.av(113 * KB, 128, NC * HD, BF16)
        wv1 = self.av(114 * KB, 128, NC * HD, BF16)
        kst = self.av(121 * KB, 64, T, BF16)
        vfst = self.av(125 * KB, 64, T, BF16)
        vtst = self.av(115 * KB, 128, NT * HD, BF16)
        P.wait("gpsimd", tT, self.attn_mm_done)
        nl = None
        for w, name in ((wk1, "w_k1"), (wv1, "w_v1")):
            nl = P.add("gpsimd", lambda e, w=w, a=self.g[name]: e.dma_start(
                out=w, in_=a), inc=qG, k=16)
        P.wait("tensor", tV, P.sem("tV").n)
        P.wait("tensor", tA, self.ln_tA)
        P.wait("tensor", qG, nl)
        banks = [self.pbA, self.pbB]
        bank_free_tV = [0, 0]
        for wi, (w, dstf, bcol) in enumerate(((wk1, kst, 1), (wv1, vfst, 2))):
            for ns in range(4):
                par = ns % 2
                bt = banks[par]
                P.wait("tensor", tV, bank_free_tV[par])
                for ck in range(NC):
                    P.add("tensor", lambda e, w=w, ck=ck, ns=ns, bt=bt: e.matmul(
                        self.psf(bt, 512, parts=64),
                        w[:, ck * HD:(ck + 1) * HD],
                        self.xnb[:, ck * T + ns * 512:ck * T + (ns + 1) * 512],
                        start=(ck == 0), stop=(ck == NC - 1)),
                        inc=(tT if ck == NC - 1 else None))
                nmm = tT.n
                P.wait("vector", tT, nmm)
                bank_free_tV[par] = P.add("vector", lambda e, dstf=dstf, ns=ns,
                                          bcol=bcol, bt=bt: e.tensor_scalar(
                    dstf[:, ns * 512:(ns + 1) * 512], self.psf(bt, 512, parts=64),
                    self.bqkv1_pp[:, bcol:bcol + 1], None, ALU.add), inc=tV)
        kv_tV = tV.n
        P.wait("tensor", tV, kv_tV)
        for gph in range(16):
            P.add("tensor", lambda e, gph=gph: e.transpose(
                self.psb(self.pbC, 64, base=gph * 64),
                vfst[:, gph * 128:(gph + 1) * 128], self.ident_bf[0:64, 0:64]),
                inc=(tT if gph == 15 else None))
        nvt = tT.n
        P.wait("scalar", tT, nvt)
        nvta = P.add("scalar", lambda e, vtst=vtst: e.activation(
            vtst, self.psb(self.pbC, 1024), AF.Copy), inc=tA)
        self.qkv1_pbc_tA = nvta
        engs = OrderedEngineSet([mybir.EngineType.Pool])

        def gp_regs(e):
            if not hasattr(self, "_regs"):
                nc = self.nc
                r = {}
                for nmq in ("half", "phalf", "koff", "voff", "foff"):
                    r[nmq] = nc.alloc_registers("r_" + nmq, engines=engs).handles[0]
                pid = e.partition_id()
                e.reg_mov(r["half"], pid)
                e.reg_alu(r["half"], r["half"], 1, ALU.bitwise_and)
                e.reg_mov(r["phalf"], r["half"])
                e.reg_alu(r["phalf"], r["phalf"], 1, ALU.bitwise_xor)
                self._regs = r
            return self._regs

        KHALF = 4 * HD * T
        VHALF = 4 * 128 * NT * HD
        FHALF = 4 * 128

        def share_writes(e, s=s):
            r = gp_regs(e)
            e.reg_mov(r["koff"], r["half"])
            e.reg_alu(r["koff"], r["koff"], KHALF, ALU.mult)
            e.reg_alu(r["koff"], r["koff"], s * HD * T, ALU.add)
            e.reg_mov(r["voff"], r["half"])
            e.reg_alu(r["voff"], r["voff"], VHALF, ALU.mult)
            e.reg_alu(r["voff"], r["voff"], s * 128 * NT * HD, ALU.add)
            kdst = bass.AP(self.sh_k, r["koff"], [[T, HD], [1, T]])
            vdst = bass.AP(self.sh_v, r["voff"], [[NT * HD, 128], [1, NT * HD]])
            return kdst, vdst

        P.wait("gpsimd", tV, kv_tV)
        P.wait("gpsimd", tA, nvta)
        nk = P.add("gpsimd", lambda e, s=s, kst=kst: e.dma_start(
            out=share_writes(e)[0], in_=kst), inc=qSh, k=16)
        nv = P.add("gpsimd", lambda e, s=s, vtst=vtst: e.dma_start(
            out=share_writes(e)[1], in_=vtst), inc=qSh, k=16)
        P.wait("gpsimd", qSh, nv)

        def flag_write(e, s=s):
            r = gp_regs(e)
            e.reg_mov(r["foff"], r["half"])
            e.reg_alu(r["foff"], r["foff"], FHALF, ALU.mult)
            e.reg_alu(r["foff"], r["foff"], s * 128, ALU.add)
            fdst = bass.AP(self.sh_f, r["foff"], [[128, 1], [1, 128]])
            return e.dma_start(out=fdst, in_=self.fwm)

        P.wait("gpsimd", P.sem("tV"), self.fwm_ready)
        nf = P.add("gpsimd", lambda e, s=s: flag_write(e, s), inc=qSh, k=16)
        guards = [(qSh, nf)]
        if s == 0:
            guards.append((P.sem("qQ"), self.q1_store))
        self._kv_share_done = guards

    def _reload_xn0(self):
        P = self.P
        qG, tT = P.sem("qG"), P.sem("tT")
        P.wait("gpsimd", tT, tT.n)
        self.xn0_reload = P.add("gpsimd", lambda e: e.dma_start(
            out=self.xnb, in_=self.xn0_d), inc=qG, k=16)
        P.wait("tensor", qG, self.xn0_reload)

    def _poll_and_load(self):
        P = self.P
        nc = self.nc
        qG, tT = P.sem("qG"), P.sem("tT")
        qP = P.sem("qP")
        qKV = P.sem("qKV")
        engs = OrderedEngineSet([mybir.EngineType.Pool])
        KHALF = 4 * HD * T
        VHALF = 4 * 128 * NT * HD
        FHALF = 4 * 128
        K_all = self.av(0, 64, NCORE * T, BF16)
        V_all = self.av(32 * KB, 128, (NCORE * T // 128) * HD, BF16)
        self.K_all, self.V_all = K_all, V_all
        moe_end_tT = tT.n
        self.moe_end_tT = moe_end_tT

        def poll(e):
            r = self._regs
            rv, re_ = r["koff"], r["voff"]
            rc_c = nc.alloc_registers("pollc", engines=engs)
            rt_c = nc.alloc_registers("pollt", engines=engs)
            rc, rt = rc_c.handles[0], rt_c.handles[0]
            rv_c = bass.RegisterHandles(rv)
            del rv_c
            e.reg_mov(rv, 0)
            e.reg_mov(re_, 0)
            e.reg_mov(rc, 0)
            e.reg_mov(r["foff"], r["phalf"])
            e.reg_alu(r["foff"], r["foff"], FHALF, ALU.mult)
            e.reg_alu(r["foff"], r["foff"], 3 * 128, ALU.add)
            fsrc = bass.AP(self.sh_f, r["foff"], [[128, 1], [1, 128]])
            lbl, done = "fpollL", "fpollD"
            nc.br(lbl, engines=engs)
            with nc.body(lbl, valid_engines=engs):
                e.reg_alu(re_, re_, 16, ALU.add)
                e.dma_start(out=self.fpoll, in_=fsrc).then_inc(qP.h, 16)
                e.wait_ge(qP.h, re_)
                e.reg_load(rv, self.fpoll[0:1, 0:1])
                e.reg_alu(rc, rc, 1, ALU.add)
                e.reg_mov(rt, rc)
                e.reg_alu(rt, rt, MAXPOLL, ALU.divide)
                e.reg_alu(rt, rt, MAGIC, ALU.mult)
                e.reg_alu(rt, rt, rv, ALU.add)
                nc.br_lt(rt_c, MAGIC, on_true=lbl, on_false=done, engines=engs)
            nc.switch_bb(done)
            e.reg_save(self.fpoll[0:1, 64:65], rc)
            e.reg_save(self.fpoll[0:1, 65:66], rv)
            nc.free_register(rc)
            nc.free_register(rt)

        P.wait("gpsimd", tT, moe_end_tT)
        P.add("gpsimd", lambda e: poll(e))
        def kv_loads(e):
            r = self._regs
            insts = []
            for iw, (hreg) in enumerate(("half", "phalf")):
                e.reg_mov(r["koff"], r[hreg])
                e.reg_alu(r["koff"], r["koff"], KHALF, ALU.mult)
                ksrc = bass.AP(self.sh_k, r["koff"],
                               [[T, HD], [HD * T, 4], [1, T]])
                kdst = K_all.rearrange("d (s t) -> d s t", s=NCORE)[
                    :, iw * 4:(iw + 1) * 4, :]
                insts.append(e.dma_start(out=kdst, in_=ksrc))
                e.reg_mov(r["voff"], r[hreg])
                e.reg_alu(r["voff"], r["voff"], VHALF, ALU.mult)
                vsrc = bass.AP(self.sh_v, r["voff"],
                               [[NT * HD, 128], [128 * NT * HD, 4], [1, NT * HD]])
                vdst = V_all.rearrange("p (s c) -> p s c", s=NCORE)[
                    :, iw * 4:(iw + 1) * 4, :]
                insts.append(e.dma_start(out=vdst, in_=vsrc))
            return insts

        def kv_loads_emit(e):
            for ins in kv_loads(e):
                ins.then_inc(qKV.h, 16)

        P.add("gpsimd", lambda e: kv_loads_emit(e))
        qKV.n += 64
        self.kv_loaded = qKV.n
        q1s = self.av(84 * KB, 64, T, BF16)
        self.q1s = q1s
        P.wait("sync", P.sem("qQ"), self.q1_store)
        self.q1_loaded = P.add("sync", lambda e, q1s=q1s: e.dma_start(
            out=q1s, in_=self.q1_d), inc=P.sem("qS"), k=16)

    def _flash_final(self):
        P = self.P
        qS, qO, tV, tA, tT = (P.sem(s) for s in ("qS", "qO", "tV", "tA", "tT"))
        qKV, qM = P.sem("qKV"), P.sem("qM")
        K_all, V_all, q1 = self.K_all, self.V_all, self.q1s
        Pt = self.av(48 * KB, 128, 2 * 512, BF16)
        pt_sb = self.av(50 * KB, 128, 2 * 512, BF16)
        xts_a = [self.av(52 * KB, 128, C, F32), self.av(56 * KB, 128, C, F32)]
        ots_a = [self.av(60 * KB, 128, C, F32), self.av(64 * KB, 128, C, F32)]
        accv = self.av(72 * KB, 128, HD, F32)
        ytv = self.av(72 * KB + 512, 128, HD, F32)
        stt = self.gsc
        ps_sc = self.ps_s
        pv = self.pbA
        ptp = [self.pbB, self.pbC]
        ident = self.ident_bf

        P.wait("tensor", tV, tV.n)
        P.wait("tensor", tA, tA.n)
        P.wait("tensor", qKV, self.kv_loaded)
        P.wait("tensor", qS, self.q1_loaded)
        P.wait("sync", tT, self.moe_end_tT)
        tile_chain = self.x1_tile_ready

        NCH = GTOK // 512
        xt_read = [0, 0]
        ot_free = [0, 0]
        for qt in range(NT):
            par_t = qt % 2
            P.wait("sync", tV, xt_read[par_t])
            cs_, cn_ = tile_chain[qt]
            P.wait("sync", cs_, cn_)
            ldx = P.add("sync", lambda e, qt=qt, par_t=par_t: e.dma_start(
                out=xts_a[par_t], in_=self.xo[qt * 128:(qt + 1) * 128, :]),
                inc=qS, k=16)

            nsc = [0] * (NCH + 1)
            ntr = [0] * (NCH + 1)
            npv = [0] * (NCH + 1)
            nexp = [0] * (NCH + 1)
            nev = [0] * (NCH + 1)
            nevS = [None] * (NCH + 1)
            nacc = [0] * (NCH + 1)
            nb = [0] * (NCH + 1)

            def emit_scores(ci, qt=qt):
                par = ci % 2
                nsc[ci] = P.add("tensor", lambda e, qt=qt, ci=ci, par=par: e.matmul(
                    self.psf(ps_sc, 512, base=par * 512),
                    q1[:, qt * 128:(qt + 1) * 128],
                    K_all[:, ci * 512:(ci + 1) * 512], start=True, stop=True),
                    inc=tT)

            P.wait("tensor", tA, tA.n)
            P.wait("tensor", tV, tV.n)
            emit_scores(0)
            for ci in range(NCH + 1):
                par = ci % 2
                so = 8 + 8 * par
                if ci < NCH:
                    P.wait("vector", tT, nsc[ci])
                    P.addf("vector", lambda e, par=par, so=so: e.reduce_max(
                        stt[:, so:so + 1], self.psf(ps_sc, 512, base=par * 512),
                        axis=AX.X, negate=True))
                    if ci == 0:
                        P.addf("vector", lambda e, so=so: e.tensor_copy(
                            stt[:, 0:1], stt[:, so:so + 1]))
                        nb[ci] = P.add("vector", lambda e, so=so: e.tensor_scalar(
                            stt[:, so + 2:so + 3], stt[:, 0:1], 1.0 / 32.0, None,
                            ALU.mult), inc=tV)
                    else:
                        P.addf("vector", lambda e, so=so: e.tensor_scalar(
                            stt[:, so + 1:so + 2], stt[:, so:so + 1], stt[:, 0:1],
                            0.0, ALU.subtract, ALU.min))
                        nb[ci] = P.add("vector", lambda e, so=so: e.tensor_scalar(
                            stt[:, so + 2:so + 3], stt[:, so + 1:so + 2],
                            stt[:, 0:1], 1.0 / 32.0, ALU.add, ALU.mult), inc=tV)
                        P.addf("vector", lambda e, so=so: e.tensor_scalar(
                            stt[:, 0:1], stt[:, so + 1:so + 2], stt[:, 0:1], None,
                            ALU.add))
                if ci < NCH:
                    P.wait("scalar", tV, nb[ci])
                    if ci > 0:
                        P.add("scalar", lambda e, so=so: e.activation(
                            stt[:, so + 3:so + 4], stt[:, so + 1:so + 2], AF.Exp,
                            scale=1.0 / 32.0))
                    if ci >= 2:
                        P.wait("scalar", tT, ntr[ci - 2])
                    nexp[ci] = P.add("scalar", lambda e, par=par, so=so: e.activation(
                        Pt[:, par * 512:(par + 1) * 512],
                        self.psf(ps_sc, 512, base=par * 512), AF.Exp,
                        scale=1.0 / 32.0, bias=stt[:, so + 2:so + 3],
                        accum_out=stt[:, so + 4:so + 5]), inc=tA)
                if ci + 1 < NCH:
                    if ci >= 1:
                        P.wait("tensor", tA, nexp[ci - 1])
                    emit_scores(ci + 1)
                if ci < NCH:
                    P.wait("tensor", tA, nexp[ci])
                    if ci >= 2:
                        P.wait("tensor", nevS[ci - 2], nev[ci - 2])
                    for kt in range(4):
                        P.add("tensor", lambda e, kt=kt, par=par: e.transpose(
                            ptp[par].ap().bitcast(BF16)[:, kt * 128:(kt + 1) * 128],
                            Pt[:, par * 512 + kt * 128:par * 512 + (kt + 1) * 128],
                            ident), inc=(tT if kt == 3 else None))
                    ntr[ci] = tT.n
                if ci >= 1:
                    cj = ci - 1
                    pj = cj % 2
                    P.wait("tensor", nevS[cj], nev[cj])
                    P.wait("tensor", tV, nacc[cj - 1] if cj >= 1 else 0)
                    for kt in range(4):
                        gk = cj * 4 + kt
                        P.add("tensor", lambda e, gk=gk, kt=kt, pj=pj: e.matmul(
                            pv.ap()[:, 0:HD],
                            pt_sb[:, pj * 512 + kt * 128:pj * 512 + (kt + 1) * 128],
                            V_all[:, gk * HD:(gk + 1) * HD],
                            start=(kt == 0), stop=(kt == 3)),
                            inc=(tT if kt == 3 else None))
                    npv[cj] = tT.n
                if ci < NCH:
                    if par == 0:
                        P.wait("vector", tT, ntr[ci])
                        if ci >= 2:
                            P.wait("vector", tT, npv[ci - 2])
                        nev[ci] = P.add("vector", lambda e, par=par: e.tensor_copy(
                            pt_sb[:, par * 512:(par + 1) * 512],
                            ptp[par].ap().bitcast(BF16)[:, 0:512]), inc=tV)
                        nevS[ci] = tV
                    else:
                        P.wait("scalar", tT, ntr[ci])
                        if ci >= 2:
                            P.wait("scalar", tT, npv[ci - 2])
                        nev[ci] = P.add("scalar", lambda e, par=par: e.activation(
                            pt_sb[:, par * 512:(par + 1) * 512],
                            ptp[par].ap().bitcast(BF16)[:, 0:512], AF.Copy),
                            inc=tA)
                        nevS[ci] = tA
                if ci < NCH:
                    P.wait("vector", tA, nexp[ci])
                    if ci == 0:
                        P.addf("vector", lambda e, so=so: e.tensor_copy(
                            stt[:, 1:2], stt[:, so + 4:so + 5]))
                    else:
                        P.addf("vector", lambda e, so=so: e.scalar_tensor_tensor(
                            stt[:, 1:2], stt[:, 1:2], stt[:, so + 3:so + 4],
                            stt[:, so + 4:so + 5], ALU.mult, ALU.add))
                if ci >= 1:
                    cj = ci - 1
                    soj = 8 + 8 * (cj % 2)
                    P.wait("vector", tT, npv[cj])
                    if cj == 0:
                        nacc[cj] = P.add("vector", lambda e: e.tensor_copy(
                            accv, pv.ap()[:, 0:HD]), inc=tV)
                    else:
                        nacc[cj] = P.add("vector", lambda e, soj=soj:
                                         e.scalar_tensor_tensor(
                                             accv, accv, stt[:, soj + 3:soj + 4],
                                             pv.ap()[:, 0:HD], ALU.mult, ALU.add),
                                         inc=tV)
                    P.wait("vector", tV, nacc[cj])
            P.addf("vector", lambda e: e.reciprocal(stt[:, 2:3], stt[:, 1:2]))
            P.addf("vector", lambda e: e.tensor_scalar(
                ytv, accv, stt[:, 2:3], None, ALU.mult))
            if self.dbgF:
                dv = self.av(88 * KB, 128, NT * 4, F32)
                self._fdump = dv
                P.addf("vector", lambda e, qt=qt: e.tensor_copy(
                    dv[:, qt * 4:qt * 4 + 3], stt[:, 0:3]))
            P.wait("vector", qS, ldx)
            P.wait("vector", qO, ot_free[par_t])
            not_ = P.add("vector", lambda e, par_t=par_t: e.tensor_add(
                ots_a[par_t].rearrange("p (r d) -> p r d", d=HD),
                xts_a[par_t].rearrange("p (r d) -> p r d", d=HD),
                ytv.unsqueeze(1).broadcast_to([128, NHEAD, HD])), inc=tV)
            xt_read[par_t] = not_
            P.wait("sync", tV, not_)
            nst = P.add("sync", lambda e, qt=qt, par_t=par_t: e.dma_start(
                out=self.out_d[qt * 128:(qt + 1) * 128, :], in_=ots_a[par_t]),
                inc=qO, k=16)
            ot_free[par_t] = nst
        P.wait("sync", qO, P.sem("qO").n)



_BUILD_CACHE = {}

FUSED = True


def kernel(**inputs):
    g = make_host_globals(inputs)
    x = np.asarray(inputs["L0_attn_x"], np.float32)

    if FUSED:
        if "F" not in _BUILD_CACHE:
            _BUILD_CACHE["F"] = BuildF()
        F = _BUILD_CACHE["F"]
        in_maps = []
        for c in range(NCORE):
            m = dict(g)
            m["xmine"] = np.ascontiguousarray(np.concatenate(
                [x[(c + 2 * s) % NCORE] for s in range(4)], axis=0))
            in_maps.append(m)
        res = run_bass_kernel_spmd(F.nc, in_maps, core_ids=list(range(NCORE)))
        return np.stack([res.results[c]["out"] for c in range(NCORE)], axis=0)

    if "A" not in _BUILD_CACHE:
        _BUILD_CACHE["A"] = BuildA()
        _BUILD_CACHE["B"] = BuildB()
    A, B = _BUILD_CACHE["A"], _BUILD_CACHE["B"]

    in_maps = []
    for b in range(NCORE):
        m = dict(g)
        m["x"] = np.ascontiguousarray(x[b])
        in_maps.append(m)
    resA = run_bass_kernel_spmd(A.nc, in_maps, core_ids=list(range(NCORE)))

    K_all = np.concatenate([resA.results[c]["ko"] for c in range(NCORE)], axis=1)
    vparts = [resA.results[c]["vo"].reshape(128, NT, HD) for c in range(NCORE)]
    V_all = np.ascontiguousarray(
        np.stack(vparts, axis=1).reshape(128, NCORE * NT * HD))
    in_mapsB = []
    for c in range(NCORE):
        in_mapsB.append({
            "xin": resA.results[c]["xo"],
            "q": resA.results[c]["qo"],
            "kall": np.ascontiguousarray(K_all),
            "vall": V_all,
            "ident_bf": g["ident_bf"],
        })
    resB = run_bass_kernel_spmd(B.nc, in_mapsB, core_ids=list(range(NCORE)))
    out = np.stack([resB.results[c]["out"] for c in range(NCORE)], axis=0)
    return out


# version 10
# speedup vs baseline: 1.1392x; 1.0138x over previous
import numpy as np
import ml_dtypes
import concourse.bass as bass
import concourse.mybir as mybir
from concourse.bass_utils import run_bass_kernel_spmd
from concourse.ordered_engine_set import OrderedEngineSet

F32 = mybir.dt.float32
BF16 = mybir.dt.bfloat16
U8 = mybir.dt.uint8
AF = mybir.ActivationFunctionType
ALU = mybir.AluOpType
AX = mybir.AxisListType

ENGINES = ("sync", "gpsimd", "tensor", "vector", "scalar")

T = 2048
C = 1024
NT = T // 128
NC = C // 128
HD = 64
NHEAD = 16
HMOE = 1408
NHK = HMOE // 128
NEXP = 9
NCORE = 8
GTOK = NCORE * T
KB = 1024


class Sem:
    __slots__ = ("h", "n", "name")

    def __init__(self, nc, name):
        self.h = nc.alloc_semaphore(name)
        self.n = 0
        self.name = name


class Prog:
    def __init__(self, nc):
        self.nc = nc
        self.q = {e: [] for e in ENGINES}
        self._sems = {}

    def sem(self, name):
        if name not in self._sems:
            self._sems[name] = Sem(self.nc, name)
        return self._sems[name]

    def add(self, eng, fn, inc=None, k=1):
        if inc is not None:
            inc.n += k
            self.q[eng].append(lambda e, fn=fn, s=inc.h, k=k: fn(e).then_inc(s, k))
            return inc.n
        self.q[eng].append(fn)
        return None

    def wait(self, eng, sem, n):
        if n and n > 0:
            self.q[eng].append(lambda e, s=sem.h, n=n: e.wait_ge(s, n))

    def addf(self, eng, fn):
        sem = self.sem("flush_" + eng)
        n = self.add(eng, fn, inc=sem)
        self.wait(eng, sem, n)
        return n

    def finalize(self):
        with self.nc.Block() as block:
            block.sync(lambda e: [f(e) for f in self.q["sync"]])
            block.gpsimd(lambda e: [f(e) for f in self.q["gpsimd"]])
            block.tensor(lambda e: [f(e) for f in self.q["tensor"]])
            block.vector(lambda e: [f(e) for f in self.q["vector"]])
            block.scalar(lambda e: [f(e) for f in self.q["scalar"]])


def _dt_size(dt):
    return {F32: 4, BF16: 2, U8: 1}[dt]



def _cm(W):
    Cin, Cout = np.asarray(W).shape
    return np.ascontiguousarray(
        np.asarray(W, np.float32).reshape(Cin // 128, 128, Cout)
        .transpose(1, 0, 2).reshape(128, -1))


def _pp(v):
    v = np.asarray(v, dtype=np.float32)
    K = v.shape[0] // 128
    return np.ascontiguousarray(v.reshape(K, 128).T)


def make_host_globals(inputs):
    g = {}
    g["w_qkv"] = _cm(inputs["L0_attn_Wqkv"])
    g["b_qkv_row"] = np.asarray(inputs["L0_attn_bqkv"], np.float32).reshape(1, 3 * C)
    g["w_proj"] = _cm(inputs["L0_attn_Wproj"])
    g["b_proj_pp"] = _pp(inputs["L0_attn_bproj"])
    g["w_r"] = _cm(np.asarray(inputs["L0_ffn_W_r"], np.float32).T)
    g["w_k"] = _cm(np.asarray(inputs["L0_ffn_W_k"], np.float32).T)
    g["w_v"] = _cm(np.asarray(inputs["L0_ffn_W_v"], np.float32).T)
    g["mu_pp"] = np.ascontiguousarray(
        np.concatenate([_pp(inputs["L0_ffn_mu_r"]), _pp(inputs["L0_ffn_mu_k"])],
                       axis=1))
    g["w_q1"] = _cm(np.asarray(inputs["L1_attn_Wq"], np.float32).T)
    g["w_k1"] = _cm(np.asarray(inputs["L1_attn_Wk"], np.float32).T)
    g["w_v1"] = _cm(np.asarray(inputs["L1_attn_Wv"], np.float32).T)
    g["b_qkv1_pp"] = np.ascontiguousarray(
        np.stack([np.asarray(inputs["L1_attn_bq"], np.float32),
                  np.asarray(inputs["L1_attn_bk"], np.float32),
                  np.asarray(inputs["L1_attn_bv"], np.float32)], axis=1))
    g["w_gate"] = _cm(np.asarray(inputs["L1_ffn_gate_w"], np.float32).T)
    g["b_gate_row"] = np.asarray(inputs["L1_ffn_gate_b"], np.float32).reshape(1, 8)

    w1 = np.asarray(inputs["L1_ffn_w1"], np.float32)
    w3 = np.asarray(inputs["L1_ffn_w3"], np.float32)
    w2 = np.asarray(inputs["L1_ffn_w2"], np.float32)
    sw1 = np.asarray(inputs["L1_ffn_sw1"], np.float32)
    sw3 = np.asarray(inputs["L1_ffn_sw3"], np.float32)
    sw2 = np.asarray(inputs["L1_ffn_sw2"], np.float32)

    def blocks13(w_all, sw):
        out = np.empty((NEXP * NHK, 128, NC * 128), np.float32)
        for e in range(NEXP):
            w = sw if e == 8 else w_all[e]
            wr = w.reshape(NC, 128, NHK, 128)
            for hk in range(NHK):
                out[e * NHK + hk] = wr[:, :, hk, :].transpose(1, 0, 2).reshape(128, -1)
        return np.ascontiguousarray(out)

    g["w1b"] = blocks13(w1, sw1)
    g["w3b"] = blocks13(w3, sw3)
    w2b = np.empty((NEXP * NHK, 128, C), np.float32)
    for e in range(NEXP):
        w2b[e * NHK:(e + 1) * NHK] = (sw2 if e == 8 else w2[e]).reshape(NHK, 128, C)
    g["w2b"] = np.ascontiguousarray(w2b)

    def b13(b_all, sb):
        cols = [(sb if e == 8 else b_all[e]).reshape(NHK, 128).T for e in range(NEXP)]
        return np.ascontiguousarray(np.concatenate(cols, axis=1), dtype=np.float32)

    g["b1_pp"] = b13(np.asarray(inputs["L1_ffn_b1"], np.float32),
                     np.asarray(inputs["L1_ffn_sb1"], np.float32))
    g["b3_pp"] = b13(np.asarray(inputs["L1_ffn_b3"], np.float32),
                     np.asarray(inputs["L1_ffn_sb3"], np.float32))
    g["b2cat"] = np.ascontiguousarray(
        np.concatenate([np.asarray(inputs["L1_ffn_b2"], np.float32),
                        np.asarray(inputs["L1_ffn_sb2"], np.float32).reshape(1, C)],
                       axis=0))

    m4 = np.zeros((128, 4 * 512), np.float32)
    for pp in range(4):
        for qtl in range(4):
            blk = m4[:, pp * 512 + qtl * 128:pp * 512 + (qtl + 1) * 128]
            if qtl < pp:
                blk[:] = -1e9
            elif qtl == pp:
                ii = np.arange(128)
                blk[:] = np.where(ii[:, None] <= ii[None, :], 0.0, -1e9)
    g["mask4t"] = m4.astype(ml_dtypes.bfloat16)
    g["ones_col"] = np.ones((128, 1), dtype=ml_dtypes.bfloat16)
    g["ident_bf"] = np.eye(128, dtype=ml_dtypes.bfloat16)
    g["ident_f32"] = np.eye(128, dtype=np.float32)
    g["mask_bf"] = np.where(np.tril(np.ones((128, 128), bool)), 0.0, -1e9
                            ).astype(ml_dtypes.bfloat16)
    g["ones_bf"] = np.ones((1, 512), dtype=ml_dtypes.bfloat16)
    g["ones_f32"] = np.ones((1, 128), dtype=np.float32)
    return g


GLOBAL_SPECS = [
    ("w_qkv", (128, NC * 3 * C), F32), ("b_qkv_row", (1, 3 * C), F32),
    ("w_proj", (128, NC * C), F32), ("b_proj_pp", (128, NC), F32),
    ("w_r", (128, NC * C), F32), ("w_k", (128, NC * C), F32),
    ("w_v", (128, NC * C), F32), ("mu_pp", (128, 16), F32),
    ("w_q1", (128, NC * HD), F32), ("w_k1", (128, NC * HD), F32),
    ("w_v1", (128, NC * HD), F32), ("b_qkv1_pp", (64, 3), F32),
    ("w_gate", (128, NC * 8), F32), ("b_gate_row", (1, 8), F32),
    ("w1b", (NEXP * NHK, 128, NC * 128), F32),
    ("w3b", (NEXP * NHK, 128, NC * 128), F32),
    ("w2b", (NEXP * NHK, 128, C), F32),
    ("b1_pp", (128, NEXP * NHK), F32), ("b3_pp", (128, NEXP * NHK), F32),
    ("b2cat", (NEXP, C), F32),
    ("mask4t", (128, 4 * 512), BF16), ("ones_col", (128, 1), BF16),
    ("ident_bf", (128, 128), BF16), ("ident_f32", (128, 128), F32),
    ("mask_bf", (128, 128), BF16), ("ones_bf", (1, 512), BF16),
    ("ones_f32", (1, 128), F32),
]



class BuildA:
    def __init__(self, stop="all", dbg=()):
        self.stop = stop
        self.dbg = set(dbg)
        nc = bass.Bass("TRN2", target_bir_lowering=False)
        self.nc = nc
        self.P = Prog(nc)
        self.x_in = nc.dram_tensor("x", [T, C], F32, kind="ExternalInput").ap()
        self.g = {}
        for name, shape, dt in GLOBAL_SPECS:
            self.g[name] = nc.dram_tensor(name, list(shape), dt,
                                          kind="ExternalInput").ap()
        self.xo = nc.dram_tensor("xo", [T, C], F32, kind="ExternalOutput").ap()
        self.qo = nc.dram_tensor("qo", [HD, T], BF16, kind="ExternalOutput").ap()
        self.ko = nc.dram_tensor("ko", [HD, T], BF16, kind="ExternalOutput").ap()
        self.vo = nc.dram_tensor("vo", [128, NT * HD], BF16, kind="ExternalOutput").ap()
        self._setup_sbuf()
        self._build()

    def _setup_sbuf(self):
        nc = self.nc
        sbt = nc.alloc_sbuf_tensor
        self.mask4t = sbt("s_mask4t", [128, 4 * 512], BF16).ap()
        self.ones_col = sbt("s_ones_col", [128, 1], BF16).ap()
        self.ident_bf = sbt("s_ident_bf", [128, 128], BF16).ap()
        self.ident_f32 = sbt("s_ident_f32", [128, 128], F32).ap()
        self.mask_bf = sbt("s_mask_bf", [128, 128], BF16).ap()
        self.ones_bf = sbt("s_ones_bf", [1, 512], BF16).ap()
        self.ones_f32 = sbt("s_ones_f32", [1, 128], F32).ap()
        self.bqkv_row = sbt("bqkv_row", [1, 3 * C], BF16).ap()
        self.bproj_pp = sbt("bproj_pp", [128, NC], F32).ap()
        self.mu_pp = sbt("s_mu_pp", [128, 16], F32).ap()
        self.bqkv1_pp = sbt("bqkv1_pp", [64, 3], F32).ap()
        self.bgate_row = sbt("bgate_row", [1, 8], F32).ap()
        self.b1_pp = sbt("s_b1_pp", [128, NEXP * NHK], F32).ap()
        self.b3_pp = sbt("s_b3_pp", [128, NEXP * NHK], F32).ap()
        self.b2cat = sbt("s_b2cat", [NEXP, C], BF16).ap()
        self.wgate = sbt("wgate", [128, NC * 8], BF16).ap()
        self.gbb = sbt("gbb", [128, 8], F32).ap()
        self.stat = sbt("stat", [128, 16], F32).ap()
        self.gsc = sbt("gsc", [128, 32], F32).ap()
        self.rw_tm = sbt("rw_tm", [128, NT * NEXP], F32).ap()
        self.rw_fm = sbt("rw_fm", [NEXP, T], BF16).ap()
        self.scr = sbt("scr", [128, 2048], F32).ap()
        self.px = sbt("px", [128, T + 1], F32).ap()
        self.xnt = sbt("xnt", [128, 2 * C], BF16).ap()
        self.xnb = sbt("xnb", [128, NC * T], BF16).ap()

        rem = nc.sbuf_bytes_remaining
        self.arena_bytes = (rem - 3 * KB) // 4 * 4
        assert self.arena_bytes >= 127 * KB, f"arena too small: {self.arena_bytes}"
        self.AR = sbt("arena", [128, self.arena_bytes], U8).ap()

        self.ps_s = nc.alloc_psum_tensor("ps_s", [128, 2048], F32)
        self.pbA = nc.alloc_psum_tensor("pbA", [128, 512], F32)
        self.pbB = nc.alloc_psum_tensor("pbB", [128, 512], F32)
        self.pbC = nc.alloc_psum_tensor("pbC", [128, 512], F32)
        self.pbD = nc.alloc_psum_tensor("pbD", [128, 512], F32)

    def av(self, off_b, parts, cols, dt):
        n = cols * _dt_size(dt)
        assert off_b % 4 == 0 and off_b + n <= self.arena_bytes, (off_b, n)
        return self.AR[0:parts, off_b:off_b + n].bitcast(dt)

    def psf(self, t, cols=512, base=0, parts=128, pbase=0):
        return t.ap()[pbase:pbase + parts, base:base + cols]

    def psb(self, t, cols, base=0, parts=128, pbase=0):
        return t.ap().bitcast(BF16)[pbase:pbase + parts, base:base + cols]

    def dbg_store(self, name, src_ap, shape, dt, waits):
        if name not in self.dbg:
            return
        od = self.nc.dram_tensor("dbg_" + name, list(shape), dt,
                                 kind="ExternalOutput").ap()
        P, st = self.P, self.P.sem("st")
        for s, n in waits:
            P.wait("sync", s, n)
        P.add("sync", lambda e, o=od, a=src_ap: e.dma_start(out=o, in_=a), inc=st, k=16)

    def _build(self):
        self._load_consts()
        self._ln(src_kind="dram", dram=self.x_in, ready=None)
        self.dbg_store("xnb0", self.xnb, (128, NC * T), BF16,
                       [(self.P.sem("tA"), self.P.sem("tA").n)])
        if self.stop == "ln":
            return self._finish()
        self._ffn()
        if self.stop == "ffn":
            return self._finish()
        self._attn()
        if self.stop == "attn":
            return self._finish()
        self._proj_resid()
        if self.stop == "resid":
            return self._finish()
        self._ln(src_kind="dram", dram=self.xo, ready=self.x1_tile_ready)
        self.dbg_store("xnb1", self.xnb, (128, NC * T), BF16,
                       [(self.P.sem("tA"), self.P.sem("tA").n)])
        if self.stop == "ln1":
            return self._finish()
        self._qkv1()
        if self.stop == "qkv1":
            return self._finish()
        self._gate()
        if self.stop == "gate":
            return self._finish()
        self._moe()
        return self._finish()

    def _finish(self):
        P = self.P
        for s in ("st", "qXo", "qQKV"):
            sem = P.sem(s)
            if sem.n:
                P.wait("sync", sem, sem.n)
        qM = P.sem("qM")
        if qM.n:
            P.wait("gpsimd", qM, qM.n)
        P.finalize()

    def _load_consts(self):
        P = self.P
        qS, qG, tT, tV = P.sem("qS"), P.sem("qG"), P.sem("tT"), P.sem("tV")
        for t, name in [
            (self.mask4t, "mask4t"), (self.ones_col, "ones_col"),
            (self.ident_bf, "ident_bf"), (self.ident_f32, "ident_f32"),
            (self.mask_bf, "mask_bf"), (self.ones_bf, "ones_bf"),
            (self.ones_f32, "ones_f32"), (self.bproj_pp, "b_proj_pp"),
            (self.mu_pp, "mu_pp"), (self.bqkv1_pp, "b_qkv1_pp"),
            (self.b1_pp, "b1_pp"), (self.b3_pp, "b3_pp"),
            (self.bgate_row, "b_gate_row"),
        ]:
            P.add("sync", lambda e, t=t, a=self.g[name]: e.dma_start(out=t, in_=a),
                  inc=qS, k=16)
        for t, name in [(self.bqkv_row, "b_qkv_row"), (self.b2cat, "b2cat"),
                        (self.wgate, "w_gate")]:
            P.add("gpsimd", lambda e, t=t, a=self.g[name]: e.dma_start(out=t, in_=a),
                  inc=qG, k=16)
        self.n_const_s = qS.n
        self.n_const_g = qG.n
        P.wait("tensor", qS, self.n_const_s)
        n = P.add("tensor", lambda e: e.matmul(
            self.psf(self.pbD, 8), self.ones_f32[0:1, :], self.bgate_row,
            start=True, stop=True), inc=tT)
        P.wait("vector", tT, n)
        self.gbb_done = P.add("vector", lambda e: e.tensor_copy(
            self.gbb, self.psf(self.pbD, 8)), inc=tV)

    def _ln(self, src_kind, dram, ready):
        P = self.P
        qS, tV, tA, tT = P.sem("qS"), P.sem("tV"), P.sem("tA"), P.sem("tT")
        P.wait("tensor", tV, tV.n)
        P.wait("tensor", tA, tA.n)
        g_ = getattr(self, "_xnb_guard", None)
        if g_:
            P.wait("scalar", g_[0], g_[1])
            self._xnb_guard = None
        st = self.stat
        ln_x = [self.scr[:, 0:1024], self.scr[:, 1024:2048]]
        jk = self.px[:, 0:1024]
        packs = [self.pbA, self.pbB]
        evac_done = [0, 0]
        xnt_free = [0, 0]
        dve_read = [tV.n, tV.n]
        for n in range(NT):
            par = n % 2
            so = 8 * par
            xt = ln_x[par]
            if dve_read[par]:
                P.wait("sync", tV, dve_read[par])
            if ready is not None:
                P.wait("sync", ready[n][0], ready[n][1])
            ldn = P.add("sync", lambda e, n=n, xt=xt, dram=dram: e.dma_start(
                out=xt, in_=dram[n * 128:(n + 1) * 128, :]), inc=qS, k=16)
            P.wait("vector", qS, ldn)
            P.add("vector", lambda e, xt=xt, so=so: e.scalar_tensor_tensor(
                jk, xt, 1.0, xt, ALU.mult, ALU.mult, accum_out=st[:, so + 1:so + 2]))
            P.addf("vector", lambda e, xt=xt, so=so: e.tensor_scalar(
                jk, xt, 1.0, 0.0, ALU.mult, ALU.add, accum_out=st[:, so:so + 1]))
            P.add("vector", lambda e, so=so: e.tensor_scalar(
                st[:, so + 2:so + 3], st[:, so:so + 1], 1.0 / C, None, ALU.mult))
            P.addf("vector", lambda e, so=so: e.tensor_scalar(
                st[:, so + 3:so + 4], st[:, so + 1:so + 2], 1.0 / C, 1e-5,
                ALU.mult, ALU.add))
            P.addf("vector", lambda e, so=so: e.scalar_tensor_tensor(
                st[:, so + 5:so + 6], st[:, so + 2:so + 3], -1.0,
                st[:, so + 2:so + 3], ALU.mult, ALU.mult))
            nvar = P.add("vector", lambda e, so=so: e.tensor_add(
                st[:, so + 3:so + 4], st[:, so + 3:so + 4], st[:, so + 5:so + 6]),
                inc=tV)
            P.wait("scalar", tV, nvar)
            P.addf("scalar", lambda e, so=so: e.activation(
                st[:, so + 5:so + 6], st[:, so + 3:so + 4], AF.Ln))
            nrstd = P.add("scalar", lambda e, so=so: e.activation(
                st[:, so + 4:so + 5], st[:, so + 5:so + 6], AF.Exp, scale=-0.5),
                inc=tA)
            P.wait("vector", tA, nrstd)
            if xnt_free[par]:
                P.wait("vector", tA, xnt_free[par])
            P.addf("vector", lambda e, so=so: e.scalar_tensor_tensor(
                st[:, so + 5:so + 6], st[:, so + 2:so + 3], -1.0,
                st[:, so + 4:so + 5], ALU.mult, ALU.mult))
            xnt = self.xnt[:, par * C:(par + 1) * C]
            nxn = P.add("vector", lambda e, xt=xt, xnt=xnt, so=so: e.tensor_scalar(
                xnt, xt, st[:, so + 4:so + 5], st[:, so + 5:so + 6],
                ALU.mult, ALU.add), inc=tV)
            dve_read[par] = nxn
            P.wait("tensor", tV, nxn)
            if evac_done[par]:
                P.wait("tensor", tA, evac_done[par])
            pk = packs[par]
            for ck in range(NC):
                P.add("tensor", lambda e, ck=ck, pk=pk, xnt=xnt: e.transpose(
                    self.psb(pk, 128, base=ck * 128), xnt[:, ck * 128:(ck + 1) * 128],
                    self.ident_bf), inc=(tT if ck == NC - 1 else None))
            P.wait("scalar", tT, tT.n)
            dst = self.xnb.rearrange("p (ck t) -> p ck t", ck=NC)[:, :, n * 128:(n + 1) * 128]
            srcv = self.psb(pk, 1024).rearrange("p (ck t) -> p ck t", ck=NC)
            ne = P.add("scalar", lambda e, dst=dst, srcv=srcv: e.activation(
                dst, srcv, AF.Copy), inc=tA)
            evac_done[par] = ne
            xnt_free[par] = ne
        self.ln_tA = tA.n

    def _ffn(self):
        P = self.P
        qG, tV, tA, tT = P.sem("qG"), P.sem("tV"), P.sem("tA"), P.sem("tT")
        xr = self.av(0, 128, NC * T, BF16)
        xk = self.av(32 * KB, 128, NC * T, BF16)
        kact = self.av(64 * KB, 128, NC * T, BF16)
        r_bf = self.av(32 * KB, 128, NC * T, BF16)
        ffn_C = self.av(0, 128, NC * T, BF16)
        wbuf = self.av(96 * KB, 128, NC * 1024, BF16)
        self.ffn_C = ffn_C

        P.addf("vector", lambda e: e.memset(self.px[:, 0:1], 0.0))
        P.wait("vector", P.sem("tA"), self.ln_tA)
        d = self.scr[:, 0:2048]
        xr_v = xr.rearrange("p (ck t) -> p ck t", ck=NC)
        xk_v = xk.rearrange("p (ck t) -> p ck t", ck=NC)
        for ck in range(NC):
            src = self.xnb[:, ck * T:(ck + 1) * T]
            P.addf("vector", lambda e, src=src: e.tensor_copy(self.px[:, 1:T + 1], src))
            P.addf("vector", lambda e: e.tensor_sub(d, self.px[:, 1:T + 1],
                                                    self.px[:, 0:T]))
            P.add("vector", lambda e, ck=ck: e.scalar_tensor_tensor(
                xr_v[:, ck], d, self.mu_pp[:, ck:ck + 1], self.px[:, 0:T],
                ALU.mult, ALU.add))
            self.mix_done = P.add("vector", lambda e, ck=ck: e.scalar_tensor_tensor(
                xk_v[:, ck], d, self.mu_pp[:, 8 + ck:9 + ck], self.px[:, 0:T],
                ALU.mult, ALU.add), inc=tV)

        def load_w(name, wait_t):
            P.wait("gpsimd", tT, wait_t)
            return P.add("gpsimd", lambda e, a=self.g[name]: e.dma_start(
                out=wbuf, in_=a), inc=qG, k=16)

        banks8 = [self.pbA, self.pbB, self.pbC, self.pbD] + [self.ps_s] * 4
        bank_base = [0, 0, 0, 0, 0, 512, 1024, 1536]

        def bank(i):
            return banks8[i], bank_base[i]

        nwk = load_w("w_k", 0)
        P.wait("tensor", qG, nwk)
        P.wait("tensor", tV, self.mix_done)
        kact_v = kact.rearrange("p (dk t) -> p dk t", dk=NC)
        tmp = [self.scr[:, 0:512], self.scr[:, 512:1024]]
        tmp_tA = [0, 0]
        bank_free = [0] * 8
        for dk in range(NC):
            for ns in range(4):
                bi = (dk % 2) * 4 + ns
                bt, bb = bank(bi)
                P.wait("tensor", tV, bank_free[bi])
                for ck in range(NC):
                    P.add("tensor", lambda e, dk=dk, ns=ns, ck=ck, bt=bt, bb=bb:
                          e.matmul(self.psf(bt, 512, base=bb),
                                   wbuf[:, ck * 1024 + dk * 128:ck * 1024 + (dk + 1) * 128],
                                   xk[:, ck * T + ns * 512:ck * T + (ns + 1) * 512],
                                   start=(ck == 0), stop=(ck == NC - 1)),
                          inc=(tT if ck == NC - 1 else None))
                nmm = tT.n
                wpar = (dk * 4 + ns) % 2
                P.wait("vector", tT, nmm)
                P.wait("vector", tA, tmp_tA[wpar])
                nrelu = P.add("vector", lambda e, bt=bt, bb=bb, wpar=wpar:
                              e.tensor_scalar(tmp[wpar], self.psf(bt, 512, base=bb),
                                              0.0, None, ALU.max), inc=tV)
                bank_free[bi] = nrelu
                P.wait("scalar", tV, nrelu)
                tmp_tA[wpar] = P.add("scalar", lambda e, dk=dk, ns=ns, wpar=wpar:
                                     e.activation(kact_v[:, dk, ns * 512:(ns + 1) * 512],
                                                  tmp[wpar], AF.Square), inc=tA)
        k_mm_done = tT.n
        kact_tA = tA.n

        nwr = load_w("w_r", k_mm_done)
        P.wait("tensor", qG, nwr)
        rbf_v = r_bf.rearrange("p (dk t) -> p dk t", dk=NC)
        r_bank_tA = [0] * 4
        for dk in range(NC):
            for ns in range(4):
                bi = ns
                bt, bb = bank(bi)
                P.wait("tensor", tV, bank_free[bi])
                P.wait("tensor", tA, r_bank_tA[bi])
                for ck in range(NC):
                    P.add("tensor", lambda e, dk=dk, ns=ns, ck=ck, bt=bt, bb=bb:
                          e.matmul(self.psf(bt, 512, base=bb),
                                   wbuf[:, ck * 1024 + dk * 128:ck * 1024 + (dk + 1) * 128],
                                   xr[:, ck * T + ns * 512:ck * T + (ns + 1) * 512],
                                   start=(ck == 0), stop=(ck == NC - 1)),
                          inc=(tT if ck == NC - 1 else None))
                nmm = tT.n
                P.wait("scalar", tT, nmm)
                r_bank_tA[bi] = P.add("scalar", lambda e, dk=dk, ns=ns, bt=bt, bb=bb:
                                      e.activation(rbf_v[:, dk, ns * 512:(ns + 1) * 512],
                                                   self.psf(bt, 512, base=bb),
                                                   AF.Sigmoid), inc=tA)
        r_mm_done = tT.n
        sig_all = tA.n

        nwv = load_w("w_v", r_mm_done)
        P.wait("tensor", qG, nwv)
        P.wait("tensor", tA, kact_tA)
        ffnC_v = ffn_C.rearrange("p (dk t) -> p dk t", dk=NC)
        v_bank_tV = [bank_free[4 + i] for i in range(4)]
        v_bank_tV = [0, 0, 0, 0]
        for i in range(4):
            v_bank_tV[i] = bank_free[4 + i]
        for dk in range(NC):
            for ns in range(4):
                bi = 4 + ns
                bt, bb = bank(bi)
                P.wait("tensor", tV, v_bank_tV[ns])
                for ck in range(NC):
                    P.add("tensor", lambda e, dk=dk, ns=ns, ck=ck, bt=bt, bb=bb:
                          e.matmul(self.psf(bt, 512, base=bb),
                                   wbuf[:, ck * 1024 + dk * 128:ck * 1024 + (dk + 1) * 128],
                                   kact[:, ck * T + ns * 512:ck * T + (ns + 1) * 512],
                                   start=(ck == 0), stop=(ck == NC - 1)),
                          inc=(tT if ck == NC - 1 else None))
                nmm = tT.n
                P.wait("vector", tT, nmm)
                P.wait("vector", tA, sig_all if (dk == 0 and ns == 0) else 0)
                v_bank_tV[ns] = P.add("vector", lambda e, dk=dk, ns=ns, bt=bt, bb=bb:
                                      e.tensor_mul(ffnC_v[:, dk, ns * 512:(ns + 1) * 512],
                                                   self.psf(bt, 512, base=bb),
                                                   rbf_v[:, dk, ns * 512:(ns + 1) * 512]),
                                      inc=tV)
        self.ffn_tV = tV.n
        self.ffn_v_mm_done = tT.n
        self.dbg_store("ffn", ffn_C, (128, NC * T), BF16, [(tV, tV.n)])

    def _attn(self):
        P = self.P
        qG, tV, tA, tT = P.sem("qG"), P.sem("tV"), P.sem("tA"), P.sem("tT")
        wqkv = self.av(32 * KB, 128, NC * 3 * C, BF16)
        yD = self.av(80 * KB, 128, NC * T, BF16)
        QT = self.av(112 * KB, 64, T, BF16)
        KT = self.av(116 * KB, 64, T, BF16)
        VTP = self.av(125 * KB, 128, T, BF16)
        VTm = self.av(120 * KB, 128, NT * (HD + 1), BF16)
        PTc = [self.av(122 * KB + 256, 128, 512, BF16),
               self.av(123 * KB + 256, 128, 512, BF16)]
        scrb = self.scr.bitcast(BF16)
        rinv_bf = scrb[0:1, 0:512]
        rbc_sb = self.scr[0:64, 512:1024]
        tm_h = self.px[:, 0:1536].bitcast(BF16)
        self.yD = yD

        nwq = None
        P.wait("gpsimd", tT, self.ffn_v_mm_done)
        nwq = P.add("gpsimd", lambda e: e.dma_start(out=wqkv, in_=self.g["w_qkv"]),
                    inc=qG, k=16)
        P.wait("tensor", tV, tV.n)
        P.wait("tensor", tA, tA.n)
        kvg = getattr(self, "_kv_share_done", None)
        if kvg:
            for s_, n_ in kvg:
                P.wait("vector", s_, n_)
                P.wait("scalar", s_, n_)
            self._kv_share_done = None
        P.wait("tensor", qG, nwq)
        P.wait("tensor", qG, self.n_const_g)
        st8 = self.stat
        nvone = P.add("vector", lambda e: e.memset(
            VTm.rearrange("p (n c) -> p n c", c=HD + 1)[:, :, HD], 1.0),
            inc=tV)
        P.wait("scalar", tV, nvone)

        qkv_banks = [self.pbA, self.pbB]
        qkv_free_tA = [0, 0]
        pack_banks = [self.pbC, self.pbD]
        pack_free = [0, 0]
        pack_free_eng = [None, None]

        for h in range(NHEAD):
            P.wait("tensor", tV, tV.n)
            P.wait("tensor", tA, tA.n)
            for slab in range(6):
                par = slab % 2
                bt = qkv_banks[par]
                P.wait("tensor", tA, qkv_free_tA[par])
                for ck in range(NC):
                    P.add("tensor", lambda e, h=h, slab=slab, ck=ck, bt=bt: e.matmul(
                        self.psf(bt, 512),
                        self.xnb[:, ck * T + h * 128:ck * T + (h + 1) * 128],
                        wqkv[:, ck * 3072 + slab * 512:ck * 3072 + (slab + 1) * 512],
                        start=(ck == 0), stop=False))
                nmm = P.add("tensor", lambda e, slab=slab, bt=bt: e.matmul(
                    self.psf(bt, 512), self.ones_bf[0:1, 0:128],
                    self.bqkv_row[0:1, slab * 512:(slab + 1) * 512],
                    start=False, stop=True), inc=tT)
                P.wait("scalar", tT, nmm)
                qkv_free_tA[par] = P.add("scalar", lambda e, slab=slab, bt=bt:
                                         e.activation(tm_h[:, slab * 512:(slab + 1) * 512],
                                                      self.psf(bt, 512), AF.Copy),
                                         inc=tA)
            qkv_done_tA = tA.n

            def emit_pack(src_col0, dst_fm, s_lo, pk_i, on_act):
                pk = pack_banks[pk_i]
                P.wait("tensor", tA, qkv_done_tA)
                if pack_free[pk_i]:
                    P.wait("tensor", P.sem(pack_free_eng[pk_i]), pack_free[pk_i])
                for i in range(8):
                    s = s_lo + i
                    P.add("tensor", lambda e, s=s, i=i, pk=pk, src_col0=src_col0:
                          e.transpose(self.psb(pk, 128, base=i * 128, parts=64),
                                      tm_h[:, src_col0 + s * 64:src_col0 + (s + 1) * 64],
                                      self.ident_bf),
                          inc=(tT if i == 7 else None))
                ntr = tT.n
                eng = "scalar" if on_act else "vector"
                P.wait(eng, tT, ntr)
                dstv = dst_fm.rearrange("d (j s) -> d s j", s=16)[:, s_lo:s_lo + 8, :]
                srcv = self.psb(pk, 1024, parts=64).rearrange("d (s j) -> d s j", s=8)
                if on_act:
                    nn = P.add("scalar", lambda e, dstv=dstv, srcv=srcv:
                               e.activation(dstv, srcv, AF.Copy), inc=tA)
                else:
                    nn = P.add("vector", lambda e, dstv=dstv, srcv=srcv:
                               e.tensor_copy(dstv, srcv), inc=tV)
                pack_free[pk_i] = nn
                pack_free_eng[pk_i] = "tA" if on_act else "tV"
                return nn

            emit_pack(0, QT, 0, 0, False)
            emit_pack(0, QT, 8, 1, False)
            emit_pack(C, KT, 0, 0, False)
            nk = emit_pack(C, KT, 8, 1, False)
            emit_pack(2 * C, VTP[0:64, :], 0, 0, True)
            nvt = emit_pack(2 * C, VTP[0:64, :], 8, 1, True)
            pk = pack_banks[0]
            P.wait("tensor", P.sem(pack_free_eng[0]), pack_free[0])
            P.wait("tensor", tA, nvt)
            for gph in range(16):
                P.add("tensor", lambda e, gph=gph, pk=pk: e.transpose(
                    self.psb(pk, 64, base=gph * 64),
                    VTP[0:64, gph * 128:(gph + 1) * 128],
                    self.ident_bf[0:64, 0:64]),
                    inc=(tT if gph == 15 else None))
            nvtm_tr = tT.n
            P.wait("scalar", tT, nvtm_tr)
            nvtm = P.add("scalar", lambda e, pk=pk: e.activation(
                VTm.rearrange("p (n c) -> p n c", c=HD + 1)[:, :, 0:HD],
                self.psb(pk, 1024).rearrange("p (n c) -> p n c", c=HD),
                AF.Copy), inc=tA)
            pack_free[0] = nvtm
            pack_free_eng[0] = "tA"

            P.wait("tensor", tV, nk)
            P.wait("tensor", tA, nvtm)
            ybase = [1024, 1536]
            prev_y_ev = [0, 0]
            prev_rbc_tA = 0
            prev_mult_tV = 0
            pend = None

            def ep_bcast(pd):
                gp_, qg_, nrec_ = pd
                P.wait("tensor", tV, nrec_)
                P.wait("tensor", tA, self._rbc_tA)
                return P.add("tensor", lambda e: e.matmul(
                    self.psf(self.pbD, 512, parts=64),
                    self.ones_bf[0:1, 0:64], rinv_bf,
                    start=True, stop=True), inc=tT)

            def ep_rbc(pd, nbc_, pmult):
                P.wait("scalar", tT, nbc_)
                P.wait("scalar", tV, pmult)
                self._rbc_tA = P.add("scalar", lambda e: e.activation(
                    rbc_sb, self.psf(self.pbD, 512, parts=64), AF.Copy),
                    inc=tA)
                return self._rbc_tA

            def ep_mult(pd, h=h):
                gp_, qg_, nrec_ = pd
                P.wait("vector", tA, self._rbc_tA)
                dst = yD[(h % 2) * 64:(h % 2) * 64 + 64,
                         (h // 2) * T + qg_ * 512:(h // 2) * T
                         + (qg_ + 1) * 512]
                n = P.add("vector", lambda e, dst=dst, gp_=gp_:
                          e.tensor_mul(
                    dst, self.psf(self.ps_s, 512, base=ybase[gp_], parts=64),
                    rbc_sb), inc=tV)
                prev_y_ev[gp_] = n
                return n

            self._rbc_tA = prev_rbc_tA
            for qg in range(4):
                nch = 4 * qg + 4
                gp = qg % 2
                qc0 = qg * 512
                nst = [0] * nch
                nexpc = [0] * nch
                npvc = [0] * nch
                nbc_p = 0
                P.wait("tensor", tV, prev_y_ev[gp])
                for kt in range(nch + 1):
                    sp = kt % 2
                    if pend is not None:
                        if kt == 1:
                            nbc_p = ep_bcast(pend)
                        elif kt == 2:
                            ep_rbc(pend, nbc_p, prev_mult_tV)
                        elif kt == 3:
                            prev_mult_tV = ep_mult(pend)
                            pend = None
                    if kt < nch:
                        if kt >= 2:
                            P.wait("tensor", tA, nexpc[kt - 2])
                        masked = kt >= 4 * qg
                        P.add("tensor", lambda e, kt=kt, sp=sp, qc0=qc0,
                              masked=masked: e.matmul(
                            self.psf(self.ps_s, 512, base=sp * 512),
                            KT[:, kt * 128:(kt + 1) * 128], QT[:, qc0:qc0 + 512],
                            start=True, stop=not masked),
                            inc=(None if masked else tT))
                        if masked:
                            pp = kt - 4 * qg
                            P.add("tensor", lambda e, sp=sp, pp=pp: e.matmul(
                                self.psf(self.ps_s, 512, base=sp * 512),
                                self.ident_bf,
                                self.mask4t[:, pp * 512:(pp + 1) * 512],
                                start=False, stop=True), inc=tT)
                        nst[kt] = tT.n
                        P.wait("scalar", tT, nst[kt])
                        if kt >= 2:
                            P.wait("scalar", tT, npvc[kt - 2])
                        nexpc[kt] = P.add("scalar", lambda e, sp=sp: e.activation(
                            PTc[sp], self.psf(self.ps_s, 512, base=sp * 512),
                            AF.Exp, scale=0.125), inc=tA)
                    if kt >= 1:
                        cj = kt - 1
                        cp = cj % 2
                        P.wait("tensor", tA, nexpc[cj])
                        npvc[cj] = P.add("tensor", lambda e, cp=cp, gp=gp,
                                         cj=cj, nch=nch: e.matmul(
                            self.psf(self.ps_s, 512, base=ybase[gp], parts=65),
                            VTm[:, cj * (HD + 1):(cj + 1) * (HD + 1)],
                            PTc[cp],
                            start=(cj == 0), stop=(cj == nch - 1)), inc=tT)
                P.wait("vector", tT, npvc[nch - 1])
                def _recip(e, gp=gp):
                    with self.nc.allow_low_precision(
                            reason="rinv rounded to bf16 matmul operand"):
                        return e.reciprocal(
                            rinv_bf, self.psf(self.ps_s, 512, base=ybase[gp],
                                              parts=1, pbase=64))
                nrec = P.add("vector", _recip, inc=tV)
                pend = (gp, qg, nrec)
                if qg == 3:
                    nbc_p = ep_bcast(pend)
                    ep_rbc(pend, nbc_p, prev_mult_tV)
                    prev_mult_tV = ep_mult(pend)
                    pend = None
            prev_rbc_tA = self._rbc_tA
        self.attn_tA = tA.n
        self.attn_mm_done = tT.n
        self.dbg_store("yd", yD, (128, NC * T), BF16, [(tA, tA.n), (tV, tV.n)])

    def _Pv(self, L):
        return self.av(120 * KB, 128, L, BF16)

    def _proj_resid(self, x_dram=None, xo_dram=None):
        if x_dram is None:
            x_dram = self.x_in
        if xo_dram is None:
            xo_dram = self.xo
        P = self.P
        qG, qS, tV, tA, tT = (P.sem(s) for s in ("qG", "qS", "tV", "tA", "tT"))
        qXo = P.sem("qXo")
        wproj = self.av(32 * KB, 128, NC * C, BF16)
        x1s = [self.av(48 * KB, 128, C, F32), self.av(52 * KB, 128, C, F32)]
        ffn_C = self.ffn_C
        yD = self.yD

        P.wait("gpsimd", tT, self.attn_mm_done)
        nwp = P.add("gpsimd", lambda e: e.dma_start(out=wproj, in_=self.g["w_proj"]),
                    inc=qG, k=16)
        P.wait("tensor", tV, tV.n)
        P.wait("tensor", tA, tA.n)
        P.wait("tensor", qG, nwp)
        banks = [self.pbA, self.pbB]
        bank_free_tV = [0, 0]
        comb_tV = 0
        for cko in range(NC):
            for ns in range(4):
                par = (cko * 4 + ns) % 2
                bt = banks[par]
                P.wait("tensor", tV, bank_free_tV[par])
                for ck in range(NC):
                    P.add("tensor", lambda e, cko=cko, ns=ns, ck=ck, bt=bt: e.matmul(
                        self.psf(bt, 512),
                        wproj[:, ck * C + cko * 128:ck * C + (cko + 1) * 128],
                        yD[:, ck * T + ns * 512:ck * T + (ns + 1) * 512],
                        start=(ck == 0), stop=(ck == NC - 1)),
                        inc=(tT if ck == NC - 1 else None))
                nmm = tT.n
                P.wait("vector", tT, nmm)
                cv = ffn_C.rearrange("p (ck t) -> p ck t", ck=NC)[:, cko,
                                                                  ns * 512:(ns + 1) * 512]
                bank_free_tV[par] = P.add("vector", lambda e, cko=cko, cv=cv, bt=bt:
                                          e.scalar_tensor_tensor(
                                              cv, self.psf(bt, 512),
                                              self.bproj_pp[:, cko:cko + 1], cv,
                                              ALU.add, ALU.add), inc=tV)
        comb_tV = tV.n

        self.x1_tile_ready = []
        packs = [self.pbC, self.pbD]
        pack_free_tV = [0, 0]
        x1s_free = [0, 0]
        xt_slots = [self.scr[:, 0:1024], self.scr[:, 1024:2048]]
        xt_read_tV = [0, 0]
        P.wait("tensor", tV, comb_tV)
        P.wait("sync", tV, comb_tV)
        P.wait("sync", tA, tA.n)
        for n in range(NT):
            par = n % 2
            P.wait("sync", tV, xt_read_tV[par])
            ldn = P.add("sync", lambda e, n=n, par=par, x_dram=x_dram: e.dma_start(
                out=xt_slots[par], in_=x_dram[n * 128:(n + 1) * 128, :]),
                inc=qS, k=16)
            pk = packs[par]
            P.wait("tensor", tV, pack_free_tV[par])
            for ck in range(NC):
                P.add("tensor", lambda e, n=n, ck=ck, pk=pk: e.transpose(
                    self.psb(pk, 128, base=ck * 128),
                    ffn_C[:, ck * T + n * 128:ck * T + (n + 1) * 128],
                    self.ident_bf), inc=(tT if ck == NC - 1 else None))
            ntr = tT.n
            P.wait("vector", tT, ntr)
            P.wait("vector", qS, ldn)
            P.wait("vector", qXo, x1s_free[par])
            nadd = P.add("vector", lambda e, par=par, pk=pk: e.tensor_add(
                x1s[par], xt_slots[par], self.psb(pk, 1024)), inc=tV)
            pack_free_tV[par] = nadd
            xt_read_tV[par] = nadd
            P.wait("sync", tV, nadd)
            nst = P.add("sync", lambda e, n=n, par=par, xo_dram=xo_dram: e.dma_start(
                out=xo_dram[n * 128:(n + 1) * 128, :], in_=x1s[par]), inc=qXo, k=16)
            x1s_free[par] = nst
            self.x1_tile_ready.append((qXo, nst))

    def _qkv1(self):
        P = self.P
        qG, qS, tV, tA, tT = (P.sem(s) for s in ("qG", "qS", "tV", "tA", "tT"))
        qQKV = P.sem("qQKV")
        wq1 = self.av(112 * KB, 128, NC * HD, BF16)
        wk1 = self.av(113 * KB, 128, NC * HD, BF16)
        wv1 = self.av(114 * KB, 128, NC * HD, BF16)
        v1t = self.av(115 * KB, 128, NT * HD, BF16)
        q1 = self.av(117 * KB, 64, T, BF16)
        k1 = self.av(121 * KB, 64, T, BF16)
        v1f = self.av(125 * KB, 64, T, BF16)

        P.wait("gpsimd", tT, self.attn_mm_done)
        nl = None
        for w, name in ((wq1, "w_q1"), (wk1, "w_k1"), (wv1, "w_v1")):
            nl = P.add("gpsimd", lambda e, w=w, a=self.g[name]: e.dma_start(
                out=w, in_=a), inc=qG, k=16)
        P.wait("tensor", tV, P.sem("tV").n)
        P.wait("tensor", tA, self.ln_tA)
        P.wait("tensor", qG, nl)
        banks = [self.pbA, self.pbB]
        bank_free_tV = [0, 0]
        outs = [(wq1, q1, 0), (wk1, k1, 1), (wv1, v1f, 2)]
        for wi, (w, dstf, bcol) in enumerate(outs):
            for ns in range(4):
                par = ns % 2
                bt = banks[par]
                P.wait("tensor", tV, bank_free_tV[par])
                for ck in range(NC):
                    P.add("tensor", lambda e, w=w, ck=ck, ns=ns, bt=bt: e.matmul(
                        self.psf(bt, 512, parts=64),
                        w[:, ck * HD:(ck + 1) * HD],
                        self.xnb[:, ck * T + ns * 512:ck * T + (ns + 1) * 512],
                        start=(ck == 0), stop=(ck == NC - 1)),
                        inc=(tT if ck == NC - 1 else None))
                nmm = tT.n
                P.wait("vector", tT, nmm)
                bank_free_tV[par] = P.add("vector", lambda e, dstf=dstf, ns=ns,
                                          bcol=bcol, bt=bt: e.tensor_scalar(
                    dstf[:, ns * 512:(ns + 1) * 512], self.psf(bt, 512, parts=64),
                    self.bqkv1_pp[:, bcol:bcol + 1], None, ALU.add), inc=tV)
        qkv_tV = tV.n
        P.wait("tensor", tV, qkv_tV)
        for gph in range(16):
            P.add("tensor", lambda e, gph=gph: e.transpose(
                self.psb(self.pbC, 64, base=gph * 64),
                v1f[:, gph * 128:(gph + 1) * 128], self.ident_bf[0:64, 0:64]),
                inc=(tT if gph == 15 else None))
        nvt = tT.n
        P.wait("scalar", tT, nvt)
        nvta = P.add("scalar", lambda e: e.activation(v1t, self.psb(self.pbC, 1024),
                                                      AF.Copy), inc=tA)
        P.wait("sync", tV, qkv_tV)
        P.add("sync", lambda e: e.dma_start(out=self.qo, in_=q1), inc=qQKV, k=16)
        P.add("sync", lambda e: e.dma_start(out=self.ko, in_=k1), inc=qQKV, k=16)
        P.wait("sync", tA, nvta)
        P.add("sync", lambda e: e.dma_start(out=self.vo, in_=v1t), inc=qQKV, k=16)
        self.qkv1_pbc_tA = nvta

    def _gate(self):
        P = self.P
        tV, tA, tT, qG = P.sem("tV"), P.sem("tA"), P.sem("tT"), P.sem("qG")
        gs = self.gsc
        P.add("vector", lambda e: e.memset(
            self.rw_tm.rearrange("p (n e) -> p n e", e=NEXP)[:, :, 8], 1.0))
        P.wait("vector", tV, self.gbb_done)
        P.wait("tensor", qG, self.n_const_g)
        exp_tA = 0
        rwfm_tV = 0
        for n in range(NT):
            P.wait("tensor", tA, exp_tA)
            P.wait("tensor", tA, self.qkv1_pbc_tA if n == 0 else 0)
            for ck in range(NC):
                P.add("tensor", lambda e, n=n, ck=ck: e.matmul(
                    self.psf(self.pbD, 8),
                    self.xnb[:, ck * T + n * 128:ck * T + (n + 1) * 128],
                    self.wgate[:, ck * 8:(ck + 1) * 8],
                    start=(ck == 0), stop=(ck == NC - 1)),
                    inc=(tT if ck == NC - 1 else None))
            nmm = tT.n
            P.wait("vector", tT, nmm)
            nng = P.add("vector", lambda e: e.reduce_max(
                gs[:, 24:25], self.psf(self.pbD, 8), axis=AX.X, negate=True), inc=tV)
            P.wait("scalar", tV, nng)
            exp_tA = P.add("scalar", lambda e: e.activation(
                gs[:, 0:8], self.psf(self.pbD, 8), AF.Exp, bias=gs[:, 24:25],
                accum_out=gs[:, 27:28]), inc=tA)
            P.wait("vector", tA, exp_tA)
            P.addf("vector", lambda e: e.reciprocal(gs[:, 28:29], gs[:, 27:28]))
            P.addf("vector", lambda e: e.tensor_scalar(
                gs[:, 0:8], gs[:, 0:8], gs[:, 28:29], None, ALU.mult))
            P.addf("vector", lambda e: e.tensor_add(gs[:, 8:16], gs[:, 0:8], self.gbb))
            P.addf("vector", lambda e: e.reduce_max(gs[:, 24:25], gs[:, 8:16], axis=AX.X))
            P.addf("vector", lambda e: e.tensor_scalar(
                gs[:, 16:24], gs[:, 8:16], gs[:, 24:25], None, ALU.is_ge))
            P.addf("vector", lambda e: e.scalar_tensor_tensor(
                gs[:, 16:24], gs[:, 16:24], -1e30, gs[:, 8:16], ALU.mult, ALU.add))
            P.addf("vector", lambda e: e.reduce_max(gs[:, 25:26], gs[:, 16:24], axis=AX.X))
            P.addf("vector", lambda e: e.tensor_scalar(
                gs[:, 16:24], gs[:, 8:16], gs[:, 25:26], None, ALU.is_ge))
            rw = self.rw_tm[:, n * NEXP:n * NEXP + 8]
            nrw = P.add("vector", lambda e, rw=rw: e.tensor_mul(
                rw, gs[:, 0:8], gs[:, 16:24]), inc=tV)
            P.wait("tensor", tV, nrw)
            P.wait("tensor", tV, rwfm_tV)
            ntr = P.add("tensor", lambda e, n=n: e.transpose(
                self.psf(self.pbC, 128, parts=NEXP),
                self.rw_tm[:, n * NEXP:(n + 1) * NEXP], self.ident_f32), inc=tT)
            P.wait("vector", tT, ntr)
            rwfm_tV = P.add("vector", lambda e, n=n: e.tensor_copy(
                self.rw_fm[:, n * 128:(n + 1) * 128],
                self.psf(self.pbC, 128, parts=NEXP)), inc=tV)
        self.gate_tV = tV.n
        self.gate_mm_tT = tT.n
        self.dbg_store("rw", self.rw_tm, (128, NT * NEXP), F32, [(tV, tV.n)])

    def _moe(self):
        P = self.P
        qG, tV, tA, tT = P.sem("qG"), P.sem("tV"), P.sem("tA"), P.sem("tT")
        qM, qXo = P.sem("qM"), P.sem("qXo")
        h_buf = self.av(0, 128, NHK * T, BF16)
        blk = [[self.av(44 * KB + (i * 2 + p) * 2 * KB, 128, 1024, BF16)
                for p in range(2)] for i in range(2)]
        w2 = self.av(52 * KB, 128, NHK * C, BF16)
        stg = [self.av(74 * KB, 128, C, F32), self.av(78 * KB, 128, C, F32)]

        tile_chain = [self.x1_tile_ready[n] for n in range(NT)]
        P.wait("gpsimd", qXo, P.sem("qXo").n)

        P.wait("tensor", tV, self.gate_tV)
        P.wait("tensor", qG, self.n_const_g)
        stg_free = [0, 0]
        psq_free = [0, 0, 0, 0]
        for n in range(NT):
            par = n % 2
            for cs in range(2):
                qi = par * 2 + cs
                P.wait("tensor", tV, psq_free[qi])
                P.add("tensor", lambda e, n=n, cs=cs, qi=qi: e.matmul(
                    self.psf(self.ps_s, 512, base=qi * 512),
                    self.rw_fm[:, n * 128:(n + 1) * 128],
                    self.b2cat[:, cs * 512:(cs + 1) * 512],
                    start=True, stop=True), inc=(tT if cs == 1 else None))
            nmm = tT.n
            P.wait("vector", tT, nmm)
            P.wait("vector", qM, stg_free[par])
            ncp = None
            for cs in range(2):
                qi = par * 2 + cs
                psq_free[qi] = P.add("vector", lambda e, par=par, cs=cs, qi=qi:
                                     e.tensor_copy(stg[par][:, cs * 512:(cs + 1) * 512],
                                                   self.psf(self.ps_s, 512, base=qi * 512)),
                                     inc=tV)
            ncp = psq_free[par * 2 + 1]
            P.wait("gpsimd", tV, ncp)
            cs_, cn_ = tile_chain[n]
            P.wait("gpsimd", cs_, cn_)
            nadd = P.add("gpsimd", lambda e, n=n, par=par: e.dma_start(
                out=self.xo[n * 128:(n + 1) * 128, :], in_=stg[par],
                accum_op=ALU.add), inc=qM, k=16)
            stg_free[par] = nadd
            tile_chain[n] = (qM, nadd)

        hv = h_buf.rearrange("p (hk t) -> p hk t", hk=NHK)
        w2v = w2.rearrange("p (hk c) -> p hk c", hk=NHK)
        esA = [self.scr[:, 0:512], self.scr[:, 512:1024]]
        esB = [self.scr[:, 1024:1536], self.scr[:, 1536:2048]]
        blk_free = [[0, 0], [0, 0]]
        w2_free = 0
        h_free = 0
        bankA = [(self.pbA, self.pbB), (self.pbC, self.pbD)]
        h1_free = [0, 0]
        h3_free = [0, 0]
        es_free = [0, 0]
        psh_free = [0, 0]
        P.wait("tensor", tV, tV.n)
        P.wait("tensor", tA, tA.n)
        for e_i in range(NEXP):
            P.wait("gpsimd", tT, w2_free)
            nw2 = None
            for hk in range(NHK):
                nw2 = P.add("gpsimd", lambda e, e_i=e_i, hk=hk: e.dma_start(
                    out=w2v[:, hk], in_=self.g["w2b"][e_i * NHK + hk]),
                    inc=qG, k=16)
            w2_loaded = nw2
            P.wait("scalar", tT, h_free)
            for hk in range(NHK):
                par = hk % 2
                P.wait("gpsimd", tT, blk_free[0][par])
                n1 = P.add("gpsimd", lambda e, e_i=e_i, hk=hk, par=par: e.dma_start(
                    out=blk[0][par], in_=self.g["w1b"][e_i * NHK + hk]), inc=qG, k=16)
                P.wait("gpsimd", tT, blk_free[1][par])
                n3 = P.add("gpsimd", lambda e, e_i=e_i, hk=hk, par=par: e.dma_start(
                    out=blk[1][par], in_=self.g["w3b"][e_i * NHK + hk]), inc=qG, k=16)
                bt1, bt3 = bankA[par]
                P.wait("tensor", qG, n3)
                for ns in range(4):
                    P.wait("tensor", tV, h1_free[par])
                    for ck in range(NC):
                        P.add("tensor", lambda e, hk=hk, ns=ns, ck=ck, par=par, bt1=bt1:
                              e.matmul(self.psf(bt1, 512),
                                       blk[0][par][:, ck * 128:(ck + 1) * 128],
                                       self.xnb[:, ck * T + ns * 512:ck * T + (ns + 1) * 512],
                                       start=(ck == 0), stop=(ck == NC - 1)),
                              inc=(tT if ck == NC - 1 else None))
                    n1mm = tT.n
                    P.wait("tensor", tA, h3_free[par])
                    for ck in range(NC):
                        P.add("tensor", lambda e, hk=hk, ns=ns, ck=ck, par=par, bt3=bt3:
                              e.matmul(self.psf(bt3, 512),
                                       blk[1][par][:, ck * 128:(ck + 1) * 128],
                                       self.xnb[:, ck * T + ns * 512:ck * T + (ns + 1) * 512],
                                       start=(ck == 0), stop=(ck == NC - 1)),
                              inc=(tT if ck == NC - 1 else None))
                    n3mm = tT.n
                    bcol = e_i * NHK + hk
                    P.wait("vector", tT, n1mm)
                    P.wait("vector", tA, es_free[par])
                    nt1 = P.add("vector", lambda e, par=par, bcol=bcol, bt1=bt1:
                                e.tensor_scalar(esA[par], self.psf(bt1, 512),
                                                self.b1_pp[:, bcol:bcol + 1], None,
                                                ALU.add), inc=tV)
                    P.wait("vector", tV, nt1)
                    h1_free[par] = nt1
                    P.wait("scalar", tT, n3mm)
                    nt2 = P.add("scalar", lambda e, par=par, bcol=bcol, bt3=bt3:
                                e.activation(esB[par], self.psf(bt3, 512), AF.Identity,
                                             bias=self.b3_pp[:, bcol:bcol + 1]),
                                inc=tA)
                    h3_free[par] = nt2
                    P.wait("vector", tA, nt2)
                    nt3 = P.add("vector", lambda e, par=par: e.tensor_mul(
                        esA[par], esA[par], esB[par]), inc=tV)
                    P.wait("scalar", tV, nt3)
                    nsl = P.add("scalar", lambda e, hk=hk, ns=ns, par=par:
                                e.activation(hv[:, hk, ns * 512:(ns + 1) * 512],
                                             esA[par], AF.Silu), inc=tA)
                    es_free[par] = nsl
                blk_free[0][par] = n1mm
                blk_free[1][par] = n3mm
            silu_all = P.sem("tA").n
            P.wait("tensor", tA, silu_all)
            P.wait("tensor", qG, w2_loaded)
            for tq in range(NT):
                par = tq % 2
                P.wait("tensor", tV, psh_free[par])
                for hk in range(NHK):
                    for cs in range(2):
                        P.add("tensor", lambda e, tq=tq, hk=hk, cs=cs, par=par:
                              e.matmul(self.psf(self.ps_s, 512, base=par * 1024 + cs * 512),
                                       hv[:, hk, tq * 128:(tq + 1) * 128],
                                       w2v[:, hk, cs * 512:(cs + 1) * 512],
                                       start=(hk == 0), stop=(hk == NHK - 1)),
                              inc=(tT if hk == NHK - 1 and cs == 1 else None))
                nmm = tT.n
                P.wait("vector", tT, nmm)
                P.wait("vector", qM, stg_free[par])
                rwcol = self.rw_tm[:, tq * NEXP + e_i:tq * NEXP + e_i + 1]
                nev = None
                for cs in range(2):
                    nev = P.add("vector", lambda e, par=par, cs=cs, rwcol=rwcol:
                                e.tensor_scalar(stg[par][:, cs * 512:(cs + 1) * 512],
                                                self.psf(self.ps_s, 512,
                                                         base=par * 1024 + cs * 512),
                                                rwcol, None, ALU.mult), inc=tV)
                psh_free[par] = nev
                P.wait("gpsimd", tV, nev)
                cs_, cn_ = tile_chain[tq]
                P.wait("gpsimd", cs_, cn_)
                nadd = P.add("gpsimd", lambda e, tq=tq, par=par: e.dma_start(
                    out=self.xo[tq * 128:(tq + 1) * 128, :], in_=stg[par],
                    accum_op=ALU.add), inc=qM, k=16)
                stg_free[par] = nadd
                tile_chain[tq] = (qM, nadd)
            w2_free = tT.n
            h_free = tT.n
        self.x1_tile_ready = tile_chain



class BuildB:

    def __init__(self):
        nc = bass.Bass("TRN2", target_bir_lowering=False)
        self.nc = nc
        P = Prog(nc)
        self.P = P
        xin = nc.dram_tensor("xin", [T, C], F32, kind="ExternalInput").ap()
        qin = nc.dram_tensor("q", [HD, T], BF16, kind="ExternalInput").ap()
        kall = nc.dram_tensor("kall", [HD, GTOK], BF16, kind="ExternalInput").ap()
        vall = nc.dram_tensor("vall", [128, (GTOK // 128) * HD], BF16,
                              kind="ExternalInput").ap()
        identd = nc.dram_tensor("ident_bf", [128, 128], BF16,
                                kind="ExternalInput").ap()
        out = nc.dram_tensor("out", [T, C], F32, kind="ExternalOutput").ap()

        sbt = nc.alloc_sbuf_tensor
        ident = sbt("ident", [128, 128], BF16).ap()
        q1 = sbt("q1", [HD, T], BF16).ap()
        ka = sbt("ka", [HD, GTOK], BF16).ap()
        va = sbt("va", [128, (GTOK // 128) * HD], BF16).ap()
        Pt = sbt("Pt", [128, 2 * 512], BF16).ap()
        pt_sb = sbt("pt_sb", [128, 2 * 512], BF16).ap()
        acc = sbt("acc", [128, HD], F32).ap()
        yt = sbt("yt", [128, HD], F32).ap()
        stt = sbt("stt", [128, 24], F32).ap()
        xts = sbt("xts", [128, 2 * C], F32).ap()
        ots = sbt("ots", [128, 2 * C], F32).ap()

        ps_sc = nc.alloc_psum_tensor("ps_sc", [128, 1024], F32)
        pv = nc.alloc_psum_tensor("pv", [128, 512], F32)
        ptp = [nc.alloc_psum_tensor("ptp0", [128, 512], F32),
               nc.alloc_psum_tensor("ptp1", [128, 512], F32)]

        qS, qO, tV, tA, tT = (P.sem(s) for s in ("qS", "qO", "tV", "tA", "tT"))

        for t, a in ((ident, identd), (q1, qin), (ka, kall), (va, vall)):
            P.add("sync", lambda e, t=t, a=a: e.dma_start(out=t, in_=a), inc=qS, k=16)
        n_ld = qS.n
        P.wait("tensor", qS, n_ld)

        NCH = GTOK // 512
        xt_read = [0, 0]
        ot_free = [0, 0]
        for qt in range(NT):
            par_t = qt % 2
            P.wait("sync", tV, xt_read[par_t])
            ldx = P.add("sync", lambda e, qt=qt, par_t=par_t: e.dma_start(
                out=xts[:, par_t * C:(par_t + 1) * C],
                in_=xin[qt * 128:(qt + 1) * 128, :]), inc=qS, k=16)

            nsc = [0] * (NCH + 1)
            ntr = [0] * (NCH + 1)
            npv = [0] * (NCH + 1)
            nexp = [0] * (NCH + 1)
            nev = [0] * (NCH + 1)
            nevS = [None] * (NCH + 1)
            nacc = [0] * (NCH + 1)
            nb = [0] * (NCH + 1)

            def emit_scores(ci):
                par = ci % 2
                nsc[ci] = P.add("tensor", lambda e, qt=qt, ci=ci, par=par: e.matmul(
                    ps_sc.ap()[:, par * 512:(par + 1) * 512],
                    q1[:, qt * 128:(qt + 1) * 128],
                    ka[:, ci * 512:(ci + 1) * 512], start=True, stop=True), inc=tT)

            P.wait("tensor", tA, tA.n)
            P.wait("tensor", tV, tV.n)
            emit_scores(0)
            for ci in range(NCH + 1):
                par = ci % 2
                so = 8 + 8 * par
                if ci < NCH:
                    P.wait("vector", tT, nsc[ci])
                    P.addf("vector", lambda e, par=par, so=so: e.reduce_max(
                        stt[:, so:so + 1], ps_sc.ap()[:, par * 512:(par + 1) * 512],
                        axis=AX.X, negate=True))
                    if ci == 0:
                        P.addf("vector", lambda e, so=so: e.tensor_copy(
                            stt[:, 0:1], stt[:, so:so + 1]))
                        nb[ci] = P.add("vector", lambda e, so=so: e.tensor_scalar(
                            stt[:, so + 2:so + 3], stt[:, 0:1], 1.0 / 32.0, None,
                            ALU.mult), inc=tV)
                    else:
                        P.addf("vector", lambda e, so=so: e.tensor_scalar(
                            stt[:, so + 1:so + 2], stt[:, so:so + 1], stt[:, 0:1],
                            0.0, ALU.subtract, ALU.min))
                        nb[ci] = P.add("vector", lambda e, so=so: e.tensor_scalar(
                            stt[:, so + 2:so + 3], stt[:, so + 1:so + 2],
                            stt[:, 0:1], 1.0 / 32.0, ALU.add, ALU.mult), inc=tV)
                        P.addf("vector", lambda e, so=so: e.tensor_scalar(
                            stt[:, 0:1], stt[:, so + 1:so + 2], stt[:, 0:1], None,
                            ALU.add))
                if ci < NCH:
                    P.wait("scalar", tV, nb[ci])
                    if ci > 0:
                        P.add("scalar", lambda e, so=so: e.activation(
                            stt[:, so + 3:so + 4], stt[:, so + 1:so + 2], AF.Exp,
                            scale=1.0 / 32.0))
                    if ci >= 2:
                        P.wait("scalar", tT, ntr[ci - 2])
                    nexp[ci] = P.add("scalar", lambda e, par=par, so=so: e.activation(
                        Pt[:, par * 512:(par + 1) * 512],
                        ps_sc.ap()[:, par * 512:(par + 1) * 512], AF.Exp,
                        scale=1.0 / 32.0, bias=stt[:, so + 2:so + 3],
                        accum_out=stt[:, so + 4:so + 5]), inc=tA)
                if ci + 1 < NCH:
                    if ci >= 1:
                        P.wait("tensor", tA, nexp[ci - 1])
                    emit_scores(ci + 1)
                if ci < NCH:
                    P.wait("tensor", tA, nexp[ci])
                    if ci >= 2:
                        P.wait("tensor", nevS[ci - 2], nev[ci - 2])
                    for kt in range(4):
                        P.add("tensor", lambda e, kt=kt, par=par: e.transpose(
                            ptp[par].ap().bitcast(BF16)[:, kt * 128:(kt + 1) * 128],
                            Pt[:, par * 512 + kt * 128:par * 512 + (kt + 1) * 128],
                            ident), inc=(tT if kt == 3 else None))
                    ntr[ci] = tT.n
                if ci >= 1:
                    cj = ci - 1
                    pj = cj % 2
                    P.wait("tensor", nevS[cj], nev[cj])
                    P.wait("tensor", tV, nacc[cj - 1] if cj >= 1 else 0)
                    for kt in range(4):
                        gk = cj * 4 + kt
                        P.add("tensor", lambda e, gk=gk, kt=kt, pj=pj: e.matmul(
                            pv.ap()[:, 0:HD],
                            pt_sb[:, pj * 512 + kt * 128:pj * 512 + (kt + 1) * 128],
                            va[:, gk * HD:(gk + 1) * HD],
                            start=(kt == 0), stop=(kt == 3)),
                            inc=(tT if kt == 3 else None))
                    npv[cj] = tT.n
                if ci < NCH:
                    if par == 0:
                        P.wait("vector", tT, ntr[ci])
                        if ci >= 2:
                            P.wait("vector", tT, npv[ci - 2])
                        nev[ci] = P.add("vector", lambda e, par=par: e.tensor_copy(
                            pt_sb[:, par * 512:(par + 1) * 512],
                            ptp[par].ap().bitcast(BF16)[:, 0:512]), inc=tV)
                        nevS[ci] = tV
                    else:
                        P.wait("scalar", tT, ntr[ci])
                        if ci >= 2:
                            P.wait("scalar", tT, npv[ci - 2])
                        nev[ci] = P.add("scalar", lambda e, par=par: e.activation(
                            pt_sb[:, par * 512:(par + 1) * 512],
                            ptp[par].ap().bitcast(BF16)[:, 0:512], AF.Copy),
                            inc=tA)
                        nevS[ci] = tA
                if ci < NCH:
                    P.wait("vector", tA, nexp[ci])
                    if ci == 0:
                        P.addf("vector", lambda e, so=so: e.tensor_copy(
                            stt[:, 1:2], stt[:, so + 4:so + 5]))
                    else:
                        P.addf("vector", lambda e, so=so: e.scalar_tensor_tensor(
                            stt[:, 1:2], stt[:, 1:2], stt[:, so + 3:so + 4],
                            stt[:, so + 4:so + 5], ALU.mult, ALU.add))
                if ci >= 1:
                    cj = ci - 1
                    soj = 8 + 8 * (cj % 2)
                    P.wait("vector", tT, npv[cj])
                    if cj == 0:
                        nacc[cj] = P.add("vector", lambda e: e.tensor_copy(
                            acc, pv.ap()[:, 0:HD]), inc=tV)
                    else:
                        nacc[cj] = P.add("vector", lambda e, soj=soj:
                                         e.scalar_tensor_tensor(
                                             acc, acc, stt[:, soj + 3:soj + 4],
                                             pv.ap()[:, 0:HD], ALU.mult, ALU.add),
                                         inc=tV)
                    P.wait("vector", tV, nacc[cj])
            P.addf("vector", lambda e: e.reciprocal(stt[:, 2:3], stt[:, 1:2]))
            nyt = P.addf("vector", lambda e: e.tensor_scalar(
                yt, acc, stt[:, 2:3], None, ALU.mult))
            P.wait("vector", qS, ldx)
            P.wait("vector", qO, ot_free[par_t])
            ot = ots[:, par_t * C:(par_t + 1) * C]
            not_ = P.add("vector", lambda e, par_t=par_t, ot=ot: e.tensor_add(
                ot.rearrange("p (r d) -> p r d", d=HD),
                xts[:, par_t * C:(par_t + 1) * C].rearrange("p (r d) -> p r d", d=HD),
                yt.unsqueeze(1).broadcast_to([128, NHEAD, HD])), inc=tV)
            xt_read[par_t] = not_
            P.wait("sync", tV, not_)
            nst = P.add("sync", lambda e, qt=qt, ot=ot: e.dma_start(
                out=out[qt * 128:(qt + 1) * 128, :], in_=ot), inc=qO, k=16)
            ot_free[par_t] = nst
        P.wait("sync", qO, P.sem("qO").n)
        P.finalize()



I32 = mybir.dt.int32
MAGIC = 777777
MAXPOLL = 100000


class BuildF(BuildA):

    def __init__(self, dbgF=False):
        self.stop = "all"
        self.dbg = set()
        self.dbgF = dbgF
        nc = bass.Bass("TRN2", target_bir_lowering=False)
        self.nc = nc
        self.P = Prog(nc)
        self.xmine = nc.dram_tensor("xmine", [4 * T, C], F32,
                                    kind="ExternalInput").ap()
        self.g = {}
        for name, shape, dt in GLOBAL_SPECS:
            self.g[name] = nc.dram_tensor(name, list(shape), dt,
                                          kind="ExternalInput").ap()
        self.out_d = nc.dram_tensor("out", [T, C], F32, kind="ExternalOutput").ap()
        self.xo = nc.dram_tensor("xo", [T, C], F32, kind="Internal").ap()
        self.x1tmp = nc.dram_tensor("x1tmp", [T, C], F32, kind="Internal").ap()
        self.xn0_d = nc.dram_tensor("xn0_d", [128, NC * T], BF16,
                                    kind="Internal").ap()
        self.q1_d = nc.dram_tensor("q1_d", [HD, T], BF16, kind="Internal").ap()
        self.sh_k = nc.dram_tensor("sh_k", [2, 4, HD, T], BF16, kind="Internal",
                                   addr_space="Shared")
        self.sh_v = nc.dram_tensor("sh_v", [2, 4, 128, NT * HD], BF16,
                                   kind="Internal", addr_space="Shared")
        self.sh_f = nc.dram_tensor("sh_f", [2, 4, 128], I32, kind="Internal",
                                   addr_space="Shared")
        self.fwm = nc.alloc_sbuf_tensor("fwm", [1, 128], I32).ap()
        self.fpoll = nc.alloc_sbuf_tensor("fpoll", [1, 128], I32).ap()
        self.fwz = nc.alloc_sbuf_tensor("fwz", [4, 128], I32).ap()
        self._setup_sbuf()
        self._buildF()

    def _x_slice(self, s):
        return self.xmine[s * T:(s + 1) * T, :]

    def _buildF(self):
        P = self.P
        self._load_consts()
        self.fwm_ready = P.add("vector", lambda e: e.memset(self.fwm, MAGIC),
                               inc=P.sem("tV"))
        nz = P.add("vector", lambda e: e.memset(self.fwz, 0), inc=P.sem("tV"))
        P.wait("gpsimd", P.sem("tV"), nz)

        def zero_flags(e):
            engs_ = OrderedEngineSet([mybir.EngineType.Pool])
            if not hasattr(self, "_regs"):
                nc_ = self.nc
                r = {}
                for nmq in ("half", "phalf", "koff", "voff", "foff"):
                    r[nmq] = nc_.alloc_registers("r_" + nmq,
                                                 engines=engs_).handles[0]
                pid = e.partition_id()
                e.reg_mov(r["half"], pid)
                e.reg_alu(r["half"], r["half"], 1, ALU.bitwise_and)
                e.reg_mov(r["phalf"], r["half"])
                e.reg_alu(r["phalf"], r["phalf"], 1, ALU.bitwise_xor)
                self._regs = r
            r = self._regs
            e.reg_mov(r["foff"], r["half"])
            e.reg_alu(r["foff"], r["foff"], 4 * 128, ALU.mult)
            fdst = bass.AP(self.sh_f, r["foff"], [[128, 4], [1, 128]])
            return e.dma_start(out=fdst, in_=self.fwz)

        self.flags_zeroed = P.add("gpsimd", lambda e: zero_flags(e),
                                  inc=P.sem("qSh"), k=16)
        self._kv_share_done = None
        self._xnb_guard = None
        for s in range(4):
            self._ln(src_kind="dram", dram=self._x_slice(s), ready=None)
            self._ffn()
            self._attn()
            xdst = self.xo if s == 0 else self.x1tmp
            self._proj_resid(x_dram=self._x_slice(s), xo_dram=xdst)
            self._ln(src_kind="dram", dram=xdst, ready=self.x1_tile_ready)
            if s == 0:
                self._x1ready_slot0 = list(self.x1_tile_ready)
                self._q_proj_store()
                self.qkv1_pbc_tA = P.sem("tA").n
                self._gate()
                self._store_xn0()
            self._kv_share(s)
        self._reload_xn0()
        self.x1_tile_ready = self._x1ready_slot0
        self._moe()
        self._poll_and_load()
        self._flash_final()
        if self.dbgF:
            st = P.sem("st")
            for name, src, shape, dt in [
                ("kall", self.K_all, (HD, NCORE * T), BF16),
                ("vall", self.V_all, (128, (NCORE * T // 128) * HD), BF16),
                ("q1s", self.q1s, (HD, T), BF16),
                ("xnbf", self.xnb, (128, NC * T), BF16),
            ]:
                od = self.nc.dram_tensor("dbg_" + name, list(shape), dt,
                                         kind="ExternalOutput").ap()
                P.wait("sync", P.sem("tV"), P.sem("tV").n)
                P.add("sync", lambda e, o=od, a=src: e.dma_start(out=o, in_=a),
                      inc=st, k=16)
            if True:
                odf = self.nc.dram_tensor("dbg_flags", [2, 4, 128], I32,
                                          kind="ExternalOutput").ap()
                P.add("sync", lambda e, o=odf: e.dma_start(
                    out=o, in_=bass.AP(self.sh_f, 0, [[512, 1], [1, 1024]]).rearrange(
                        "a (h s t) -> (a h) s t", h=2, s=4)), inc=st, k=16)
                odp = self.nc.dram_tensor("dbg_poll", [1, 128], I32,
                                          kind="ExternalOutput").ap()
                P.add("sync", lambda e, o=odp: e.dma_start(out=o, in_=self.fpoll),
                      inc=st, k=16)
            if hasattr(self, "_fdump"):
                od = self.nc.dram_tensor("dbg_fstats", [128, NT * 4], F32,
                                         kind="ExternalOutput").ap()
                P.wait("sync", P.sem("tV"), P.sem("tV").n)
                P.add("sync", lambda e, o=od: e.dma_start(out=o, in_=self._fdump),
                      inc=st, k=16)
            od = self.nc.dram_tensor("dbg_xo", [T, C], F32,
                                     kind="ExternalOutput").ap()
            qM_ = P.sem("qM")
            P.wait("sync", qM_, qM_.n)
            P.add("sync", lambda e, o=od: e.dma_start(out=o, in_=self.xo),
                  inc=st, k=16)
            P.wait("sync", st, st.n)
        qM = P.sem("qM")
        if qM.n:
            P.wait("gpsimd", qM, qM.n)
        P.finalize()

    def _q_proj_store(self):
        P = self.P
        qG, qS, tV, tT, tA = (P.sem(s) for s in ("qG", "qS", "tV", "tT", "tA"))
        wq1 = self.av(112 * KB, 128, NC * HD, BF16)
        q1 = self.av(117 * KB, 64, T, BF16)
        P.wait("gpsimd", tT, self.attn_mm_done)
        nl = P.add("gpsimd", lambda e: e.dma_start(out=wq1, in_=self.g["w_q1"]),
                   inc=qG, k=16)
        P.wait("tensor", tV, P.sem("tV").n)
        P.wait("tensor", tA, self.ln_tA)
        P.wait("tensor", qG, nl)
        banks = [self.pbA, self.pbB]
        bank_free_tV = [0, 0]
        for ns in range(4):
            par = ns % 2
            bt = banks[par]
            P.wait("tensor", tV, bank_free_tV[par])
            for ck in range(NC):
                P.add("tensor", lambda e, ck=ck, ns=ns, bt=bt: e.matmul(
                    self.psf(bt, 512, parts=64),
                    wq1[:, ck * HD:(ck + 1) * HD],
                    self.xnb[:, ck * T + ns * 512:ck * T + (ns + 1) * 512],
                    start=(ck == 0), stop=(ck == NC - 1)),
                    inc=(tT if ck == NC - 1 else None))
            nmm = tT.n
            P.wait("vector", tT, nmm)
            bank_free_tV[par] = P.add("vector", lambda e, ns=ns, bt=bt:
                                      e.tensor_scalar(
                q1[:, ns * 512:(ns + 1) * 512], self.psf(bt, 512, parts=64),
                self.bqkv1_pp[:, 0:1], None, ALU.add), inc=tV)
        nq = bank_free_tV[1]
        P.wait("sync", tV, nq)
        self.q1_store = P.add("sync", lambda e, q1=q1: e.dma_start(
            out=self.q1_d, in_=q1), inc=P.sem("qQ"), k=16)

    def _store_xn0(self):
        P = self.P
        qG, tA = P.sem("qG"), P.sem("tA")
        P.wait("gpsimd", tA, self.ln_tA)
        self.xn0_store = P.add("gpsimd", lambda e: e.dma_start(
            out=self.xn0_d, in_=self.xnb), inc=qG, k=16)
        self._xnb_guard = (qG, self.xn0_store)

    def _kv_share(self, s):
        P = self.P
        qG, qS, tV, tA, tT = (P.sem(x) for x in ("qG", "qS", "tV", "tA", "tT"))
        qSh = P.sem("qSh")
        wk1 = self.av(113 * KB, 128, NC * HD, BF16)
        wv1 = self.av(114 * KB, 128, NC * HD, BF16)
        kst = self.av(121 * KB, 64, T, BF16)
        vfst = self.av(125 * KB, 64, T, BF16)
        vtst = self.av(115 * KB, 128, NT * HD, BF16)
        P.wait("gpsimd", tT, self.attn_mm_done)
        nl = None
        for w, name in ((wk1, "w_k1"), (wv1, "w_v1")):
            nl = P.add("gpsimd", lambda e, w=w, a=self.g[name]: e.dma_start(
                out=w, in_=a), inc=qG, k=16)
        P.wait("tensor", tV, P.sem("tV").n)
        P.wait("tensor", tA, self.ln_tA)
        P.wait("tensor", qG, nl)
        banks = [self.pbA, self.pbB]
        bank_free_tV = [0, 0]
        for wi, (w, dstf, bcol) in enumerate(((wk1, kst, 1), (wv1, vfst, 2))):
            for ns in range(4):
                par = ns % 2
                bt = banks[par]
                P.wait("tensor", tV, bank_free_tV[par])
                for ck in range(NC):
                    P.add("tensor", lambda e, w=w, ck=ck, ns=ns, bt=bt: e.matmul(
                        self.psf(bt, 512, parts=64),
                        w[:, ck * HD:(ck + 1) * HD],
                        self.xnb[:, ck * T + ns * 512:ck * T + (ns + 1) * 512],
                        start=(ck == 0), stop=(ck == NC - 1)),
                        inc=(tT if ck == NC - 1 else None))
                nmm = tT.n
                P.wait("vector", tT, nmm)
                bank_free_tV[par] = P.add("vector", lambda e, dstf=dstf, ns=ns,
                                          bcol=bcol, bt=bt: e.tensor_scalar(
                    dstf[:, ns * 512:(ns + 1) * 512], self.psf(bt, 512, parts=64),
                    self.bqkv1_pp[:, bcol:bcol + 1], None, ALU.add), inc=tV)
        kv_tV = tV.n
        P.wait("tensor", tV, kv_tV)
        for gph in range(16):
            P.add("tensor", lambda e, gph=gph: e.transpose(
                self.psb(self.pbC, 64, base=gph * 64),
                vfst[:, gph * 128:(gph + 1) * 128], self.ident_bf[0:64, 0:64]),
                inc=(tT if gph == 15 else None))
        nvt = tT.n
        P.wait("scalar", tT, nvt)
        nvta = P.add("scalar", lambda e, vtst=vtst: e.activation(
            vtst, self.psb(self.pbC, 1024), AF.Copy), inc=tA)
        self.qkv1_pbc_tA = nvta
        engs = OrderedEngineSet([mybir.EngineType.Pool])

        def gp_regs(e):
            if not hasattr(self, "_regs"):
                nc = self.nc
                r = {}
                for nmq in ("half", "phalf", "koff", "voff", "foff"):
                    r[nmq] = nc.alloc_registers("r_" + nmq, engines=engs).handles[0]
                pid = e.partition_id()
                e.reg_mov(r["half"], pid)
                e.reg_alu(r["half"], r["half"], 1, ALU.bitwise_and)
                e.reg_mov(r["phalf"], r["half"])
                e.reg_alu(r["phalf"], r["phalf"], 1, ALU.bitwise_xor)
                self._regs = r
            return self._regs

        KHALF = 4 * HD * T
        VHALF = 4 * 128 * NT * HD
        FHALF = 4 * 128

        def share_writes(e, s=s):
            r = gp_regs(e)
            e.reg_mov(r["koff"], r["half"])
            e.reg_alu(r["koff"], r["koff"], KHALF, ALU.mult)
            e.reg_alu(r["koff"], r["koff"], s * HD * T, ALU.add)
            e.reg_mov(r["voff"], r["half"])
            e.reg_alu(r["voff"], r["voff"], VHALF, ALU.mult)
            e.reg_alu(r["voff"], r["voff"], s * 128 * NT * HD, ALU.add)
            kdst = bass.AP(self.sh_k, r["koff"], [[T, HD], [1, T]])
            vdst = bass.AP(self.sh_v, r["voff"], [[NT * HD, 128], [1, NT * HD]])
            return kdst, vdst

        P.wait("gpsimd", tV, kv_tV)
        P.wait("gpsimd", tA, nvta)
        nk = P.add("gpsimd", lambda e, s=s, kst=kst: e.dma_start(
            out=share_writes(e)[0], in_=kst), inc=qSh, k=16)
        nv = P.add("gpsimd", lambda e, s=s, vtst=vtst: e.dma_start(
            out=share_writes(e)[1], in_=vtst), inc=qSh, k=16)
        P.wait("gpsimd", qSh, nv)

        def flag_write(e, s=s):
            r = gp_regs(e)
            e.reg_mov(r["foff"], r["half"])
            e.reg_alu(r["foff"], r["foff"], FHALF, ALU.mult)
            e.reg_alu(r["foff"], r["foff"], s * 128, ALU.add)
            fdst = bass.AP(self.sh_f, r["foff"], [[128, 1], [1, 128]])
            return e.dma_start(out=fdst, in_=self.fwm)

        P.wait("gpsimd", P.sem("tV"), self.fwm_ready)
        nf = P.add("gpsimd", lambda e, s=s: flag_write(e, s), inc=qSh, k=16)
        guards = [(qSh, nf)]
        if s == 0:
            guards.append((P.sem("qQ"), self.q1_store))
        self._kv_share_done = guards

    def _reload_xn0(self):
        P = self.P
        qG, tT = P.sem("qG"), P.sem("tT")
        P.wait("gpsimd", tT, tT.n)
        self.xn0_reload = P.add("gpsimd", lambda e: e.dma_start(
            out=self.xnb, in_=self.xn0_d), inc=qG, k=16)
        P.wait("tensor", qG, self.xn0_reload)

    def _poll_and_load(self):
        P = self.P
        nc = self.nc
        qG, tT = P.sem("qG"), P.sem("tT")
        qP = P.sem("qP")
        qKV = P.sem("qKV")
        engs = OrderedEngineSet([mybir.EngineType.Pool])
        KHALF = 4 * HD * T
        VHALF = 4 * 128 * NT * HD
        FHALF = 4 * 128
        K_all = self.av(0, 64, NCORE * T, BF16)
        V_all = self.av(32 * KB, 128, (NCORE * T // 128) * HD, BF16)
        self.K_all, self.V_all = K_all, V_all
        moe_end_tT = tT.n
        self.moe_end_tT = moe_end_tT

        def poll(e):
            r = self._regs
            rv, re_ = r["koff"], r["voff"]
            rc_c = nc.alloc_registers("pollc", engines=engs)
            rt_c = nc.alloc_registers("pollt", engines=engs)
            rc, rt = rc_c.handles[0], rt_c.handles[0]
            rv_c = bass.RegisterHandles(rv)
            del rv_c
            e.reg_mov(rv, 0)
            e.reg_mov(re_, 0)
            e.reg_mov(rc, 0)
            e.reg_mov(r["foff"], r["phalf"])
            e.reg_alu(r["foff"], r["foff"], FHALF, ALU.mult)
            e.reg_alu(r["foff"], r["foff"], 3 * 128, ALU.add)
            fsrc = bass.AP(self.sh_f, r["foff"], [[128, 1], [1, 128]])
            lbl, done = "fpollL", "fpollD"
            nc.br(lbl, engines=engs)
            with nc.body(lbl, valid_engines=engs):
                e.reg_alu(re_, re_, 16, ALU.add)
                e.dma_start(out=self.fpoll, in_=fsrc).then_inc(qP.h, 16)
                e.wait_ge(qP.h, re_)
                e.reg_load(rv, self.fpoll[0:1, 0:1])
                e.reg_alu(rc, rc, 1, ALU.add)
                e.reg_mov(rt, rc)
                e.reg_alu(rt, rt, MAXPOLL, ALU.divide)
                e.reg_alu(rt, rt, MAGIC, ALU.mult)
                e.reg_alu(rt, rt, rv, ALU.add)
                nc.br_lt(rt_c, MAGIC, on_true=lbl, on_false=done, engines=engs)
            nc.switch_bb(done)
            e.reg_save(self.fpoll[0:1, 64:65], rc)
            e.reg_save(self.fpoll[0:1, 65:66], rv)
            nc.free_register(rc)
            nc.free_register(rt)

        P.wait("gpsimd", tT, moe_end_tT)
        P.add("gpsimd", lambda e: poll(e))
        def kv_loads(e):
            r = self._regs
            insts = []
            for iw, (hreg) in enumerate(("half", "phalf")):
                e.reg_mov(r["koff"], r[hreg])
                e.reg_alu(r["koff"], r["koff"], KHALF, ALU.mult)
                ksrc = bass.AP(self.sh_k, r["koff"],
                               [[T, HD], [HD * T, 4], [1, T]])
                kdst = K_all.rearrange("d (s t) -> d s t", s=NCORE)[
                    :, iw * 4:(iw + 1) * 4, :]
                insts.append(e.dma_start(out=kdst, in_=ksrc))
                e.reg_mov(r["voff"], r[hreg])
                e.reg_alu(r["voff"], r["voff"], VHALF, ALU.mult)
                vsrc = bass.AP(self.sh_v, r["voff"],
                               [[NT * HD, 128], [128 * NT * HD, 4], [1, NT * HD]])
                vdst = V_all.rearrange("p (s c) -> p s c", s=NCORE)[
                    :, iw * 4:(iw + 1) * 4, :]
                insts.append(e.dma_start(out=vdst, in_=vsrc))
            return insts

        def kv_loads_emit(e):
            for ins in kv_loads(e):
                ins.then_inc(qKV.h, 16)

        P.add("gpsimd", lambda e: kv_loads_emit(e))
        qKV.n += 64
        self.kv_loaded = qKV.n
        q1s = self.av(84 * KB, 64, T, BF16)
        self.q1s = q1s
        P.wait("sync", P.sem("qQ"), self.q1_store)
        self.q1_loaded = P.add("sync", lambda e, q1s=q1s: e.dma_start(
            out=q1s, in_=self.q1_d), inc=P.sem("qS"), k=16)

    def _flash_final(self):
        P = self.P
        qS, qO, tV, tA, tT = (P.sem(s) for s in ("qS", "qO", "tV", "tA", "tT"))
        qKV, qM = P.sem("qKV"), P.sem("qM")
        K_all, V_all, q1 = self.K_all, self.V_all, self.q1s
        Pt = self.av(48 * KB, 128, 2 * 512, BF16)
        pt_sb = self.av(50 * KB, 128, 2 * 512, BF16)
        xts_a = [self.av(52 * KB, 128, C, F32), self.av(56 * KB, 128, C, F32)]
        ots_a = [self.av(60 * KB, 128, C, F32), self.av(64 * KB, 128, C, F32)]
        accv = self.av(72 * KB, 128, HD, F32)
        ytv = self.av(72 * KB + 512, 128, HD, F32)
        stt = self.gsc
        ps_sc = self.ps_s
        pv = self.pbA
        ptp = [self.pbB, self.pbC]
        ident = self.ident_bf

        P.wait("tensor", tV, tV.n)
        P.wait("tensor", tA, tA.n)
        P.wait("tensor", qKV, self.kv_loaded)
        P.wait("tensor", qS, self.q1_loaded)
        P.wait("sync", tT, self.moe_end_tT)
        tile_chain = self.x1_tile_ready

        NCH = GTOK // 512
        xt_read = [0, 0]
        ot_free = [0, 0]
        for qt in range(NT):
            par_t = qt % 2
            P.wait("sync", tV, xt_read[par_t])
            cs_, cn_ = tile_chain[qt]
            P.wait("sync", cs_, cn_)
            ldx = P.add("sync", lambda e, qt=qt, par_t=par_t: e.dma_start(
                out=xts_a[par_t], in_=self.xo[qt * 128:(qt + 1) * 128, :]),
                inc=qS, k=16)

            nsc = [0] * (NCH + 1)
            ntr = [0] * (NCH + 1)
            npv = [0] * (NCH + 1)
            nexp = [0] * (NCH + 1)
            nev = [0] * (NCH + 1)
            nevS = [None] * (NCH + 1)
            nacc = [0] * (NCH + 1)
            nb = [0] * (NCH + 1)

            def emit_scores(ci, qt=qt):
                par = ci % 2
                nsc[ci] = P.add("tensor", lambda e, qt=qt, ci=ci, par=par: e.matmul(
                    self.psf(ps_sc, 512, base=par * 512),
                    q1[:, qt * 128:(qt + 1) * 128],
                    K_all[:, ci * 512:(ci + 1) * 512], start=True, stop=True),
                    inc=tT)

            P.wait("tensor", tA, tA.n)
            P.wait("tensor", tV, tV.n)
            emit_scores(0)
            for ci in range(NCH + 1):
                par = ci % 2
                so = 8 + 8 * par
                if ci < NCH:
                    P.wait("vector", tT, nsc[ci])
                    P.addf("vector", lambda e, par=par, so=so: e.reduce_max(
                        stt[:, so:so + 1], self.psf(ps_sc, 512, base=par * 512),
                        axis=AX.X, negate=True))
                    if ci == 0:
                        P.addf("vector", lambda e, so=so: e.tensor_copy(
                            stt[:, 0:1], stt[:, so:so + 1]))
                        nb[ci] = P.add("vector", lambda e, so=so: e.tensor_scalar(
                            stt[:, so + 2:so + 3], stt[:, 0:1], 1.0 / 32.0, None,
                            ALU.mult), inc=tV)
                    else:
                        P.addf("vector", lambda e, so=so: e.tensor_scalar(
                            stt[:, so + 1:so + 2], stt[:, so:so + 1], stt[:, 0:1],
                            0.0, ALU.subtract, ALU.min))
                        nb[ci] = P.add("vector", lambda e, so=so: e.tensor_scalar(
                            stt[:, so + 2:so + 3], stt[:, so + 1:so + 2],
                            stt[:, 0:1], 1.0 / 32.0, ALU.add, ALU.mult), inc=tV)
                        P.addf("vector", lambda e, so=so: e.tensor_scalar(
                            stt[:, 0:1], stt[:, so + 1:so + 2], stt[:, 0:1], None,
                            ALU.add))
                if ci < NCH:
                    P.wait("scalar", tV, nb[ci])
                    if ci > 0:
                        P.add("scalar", lambda e, so=so: e.activation(
                            stt[:, so + 3:so + 4], stt[:, so + 1:so + 2], AF.Exp,
                            scale=1.0 / 32.0))
                    if ci >= 2:
                        P.wait("scalar", tT, ntr[ci - 2])
                    nexp[ci] = P.add("scalar", lambda e, par=par, so=so: e.activation(
                        Pt[:, par * 512:(par + 1) * 512],
                        self.psf(ps_sc, 512, base=par * 512), AF.Exp,
                        scale=1.0 / 32.0, bias=stt[:, so + 2:so + 3],
                        accum_out=stt[:, so + 4:so + 5]), inc=tA)
                if ci + 1 < NCH:
                    if ci >= 1:
                        P.wait("tensor", tA, nexp[ci - 1])
                    emit_scores(ci + 1)
                if ci < NCH:
                    P.wait("tensor", tA, nexp[ci])
                    if ci >= 2:
                        P.wait("tensor", nevS[ci - 2], nev[ci - 2])
                    for kt in range(4):
                        P.add("tensor", lambda e, kt=kt, par=par: e.transpose(
                            ptp[par].ap().bitcast(BF16)[:, kt * 128:(kt + 1) * 128],
                            Pt[:, par * 512 + kt * 128:par * 512 + (kt + 1) * 128],
                            ident), inc=(tT if kt == 3 else None))
                    ntr[ci] = tT.n
                if ci >= 1:
                    cj = ci - 1
                    pj = cj % 2
                    P.wait("tensor", nevS[cj], nev[cj])
                    P.wait("tensor", tV, nacc[cj - 1] if cj >= 1 else 0)
                    for kt in range(4):
                        gk = cj * 4 + kt
                        P.add("tensor", lambda e, gk=gk, kt=kt, pj=pj: e.matmul(
                            pv.ap()[:, 0:HD],
                            pt_sb[:, pj * 512 + kt * 128:pj * 512 + (kt + 1) * 128],
                            V_all[:, gk * HD:(gk + 1) * HD],
                            start=(kt == 0), stop=(kt == 3)),
                            inc=(tT if kt == 3 else None))
                    npv[cj] = tT.n
                if ci < NCH:
                    if par == 0:
                        P.wait("vector", tT, ntr[ci])
                        if ci >= 2:
                            P.wait("vector", tT, npv[ci - 2])
                        nev[ci] = P.add("vector", lambda e, par=par: e.tensor_copy(
                            pt_sb[:, par * 512:(par + 1) * 512],
                            ptp[par].ap().bitcast(BF16)[:, 0:512]), inc=tV)
                        nevS[ci] = tV
                    else:
                        P.wait("scalar", tT, ntr[ci])
                        if ci >= 2:
                            P.wait("scalar", tT, npv[ci - 2])
                        nev[ci] = P.add("scalar", lambda e, par=par: e.activation(
                            pt_sb[:, par * 512:(par + 1) * 512],
                            ptp[par].ap().bitcast(BF16)[:, 0:512], AF.Copy),
                            inc=tA)
                        nevS[ci] = tA
                if ci < NCH:
                    P.wait("vector", tA, nexp[ci])
                    if ci == 0:
                        P.addf("vector", lambda e, so=so: e.tensor_copy(
                            stt[:, 1:2], stt[:, so + 4:so + 5]))
                    else:
                        P.addf("vector", lambda e, so=so: e.scalar_tensor_tensor(
                            stt[:, 1:2], stt[:, 1:2], stt[:, so + 3:so + 4],
                            stt[:, so + 4:so + 5], ALU.mult, ALU.add))
                if ci >= 1:
                    cj = ci - 1
                    soj = 8 + 8 * (cj % 2)
                    P.wait("vector", tT, npv[cj])
                    if cj == 0:
                        nacc[cj] = P.add("vector", lambda e: e.tensor_copy(
                            accv, pv.ap()[:, 0:HD]), inc=tV)
                    else:
                        nacc[cj] = P.add("vector", lambda e, soj=soj:
                                         e.scalar_tensor_tensor(
                                             accv, accv, stt[:, soj + 3:soj + 4],
                                             pv.ap()[:, 0:HD], ALU.mult, ALU.add),
                                         inc=tV)
                    P.wait("vector", tV, nacc[cj])
            P.addf("vector", lambda e: e.reciprocal(stt[:, 2:3], stt[:, 1:2]))
            P.addf("vector", lambda e: e.tensor_scalar(
                ytv, accv, stt[:, 2:3], None, ALU.mult))
            if self.dbgF:
                dv = self.av(88 * KB, 128, NT * 4, F32)
                self._fdump = dv
                P.addf("vector", lambda e, qt=qt: e.tensor_copy(
                    dv[:, qt * 4:qt * 4 + 3], stt[:, 0:3]))
            P.wait("vector", qS, ldx)
            P.wait("vector", qO, ot_free[par_t])
            not_ = P.add("vector", lambda e, par_t=par_t: e.tensor_add(
                ots_a[par_t].rearrange("p (r d) -> p r d", d=HD),
                xts_a[par_t].rearrange("p (r d) -> p r d", d=HD),
                ytv.unsqueeze(1).broadcast_to([128, NHEAD, HD])), inc=tV)
            xt_read[par_t] = not_
            P.wait("sync", tV, not_)
            nst = P.add("sync", lambda e, qt=qt, par_t=par_t: e.dma_start(
                out=self.out_d[qt * 128:(qt + 1) * 128, :], in_=ots_a[par_t]),
                inc=qO, k=16)
            ot_free[par_t] = nst
        P.wait("sync", qO, P.sem("qO").n)



_BUILD_CACHE = {}

FUSED = True


def kernel(**inputs):
    g = make_host_globals(inputs)
    x = np.asarray(inputs["L0_attn_x"], np.float32)

    if FUSED:
        if "F" not in _BUILD_CACHE:
            _BUILD_CACHE["F"] = BuildF()
        F = _BUILD_CACHE["F"]
        in_maps = []
        for c in range(NCORE):
            m = dict(g)
            m["xmine"] = np.ascontiguousarray(np.concatenate(
                [x[(c + 2 * s) % NCORE] for s in range(4)], axis=0))
            in_maps.append(m)
        res = run_bass_kernel_spmd(F.nc, in_maps, core_ids=list(range(NCORE)))
        return np.stack([res.results[c]["out"] for c in range(NCORE)], axis=0)

    if "A" not in _BUILD_CACHE:
        _BUILD_CACHE["A"] = BuildA()
        _BUILD_CACHE["B"] = BuildB()
    A, B = _BUILD_CACHE["A"], _BUILD_CACHE["B"]

    in_maps = []
    for b in range(NCORE):
        m = dict(g)
        m["x"] = np.ascontiguousarray(x[b])
        in_maps.append(m)
    resA = run_bass_kernel_spmd(A.nc, in_maps, core_ids=list(range(NCORE)))

    K_all = np.concatenate([resA.results[c]["ko"] for c in range(NCORE)], axis=1)
    vparts = [resA.results[c]["vo"].reshape(128, NT, HD) for c in range(NCORE)]
    V_all = np.ascontiguousarray(
        np.stack(vparts, axis=1).reshape(128, NCORE * NT * HD))
    in_mapsB = []
    for c in range(NCORE):
        in_mapsB.append({
            "xin": resA.results[c]["xo"],
            "q": resA.results[c]["qo"],
            "kall": np.ascontiguousarray(K_all),
            "vall": V_all,
            "ident_bf": g["ident_bf"],
        })
    resB = run_bass_kernel_spmd(B.nc, in_mapsB, core_ids=list(range(NCORE)))
    out = np.stack([resB.results[c]["out"] for c in range(NCORE)], axis=0)
    return out


# version 11 (final)
# speedup vs baseline: 1.1693x; 1.0265x over previous
import numpy as np
import ml_dtypes
import concourse.bass as bass
import concourse.mybir as mybir
from concourse.bass_utils import run_bass_kernel_spmd
from concourse.ordered_engine_set import OrderedEngineSet

F32 = mybir.dt.float32
BF16 = mybir.dt.bfloat16
U8 = mybir.dt.uint8
AF = mybir.ActivationFunctionType
ALU = mybir.AluOpType
AX = mybir.AxisListType

ENGINES = ("sync", "gpsimd", "tensor", "vector", "scalar")

T = 2048
C = 1024
NT = T // 128
NC = C // 128
HD = 64
NHEAD = 16
HMOE = 1408
NHK = HMOE // 128
NEXP = 9
NCORE = 8
GTOK = NCORE * T
KB = 1024


class Sem:
    __slots__ = ("h", "n", "name")

    def __init__(self, nc, name):
        self.h = nc.alloc_semaphore(name)
        self.n = 0
        self.name = name


class Prog:
    def __init__(self, nc):
        self.nc = nc
        self.q = {e: [] for e in ENGINES}
        self._sems = {}

    def sem(self, name):
        if name not in self._sems:
            self._sems[name] = Sem(self.nc, name)
        return self._sems[name]

    def add(self, eng, fn, inc=None, k=1):
        if inc is not None:
            inc.n += k
            self.q[eng].append(lambda e, fn=fn, s=inc.h, k=k: fn(e).then_inc(s, k))
            return inc.n
        self.q[eng].append(fn)
        return None

    def wait(self, eng, sem, n):
        if n and n > 0:
            self.q[eng].append(lambda e, s=sem.h, n=n: e.wait_ge(s, n))

    def addf(self, eng, fn):
        sem = self.sem("flush_" + eng)
        n = self.add(eng, fn, inc=sem)
        self.wait(eng, sem, n)
        return n

    def finalize(self):
        with self.nc.Block() as block:
            block.sync(lambda e: [f(e) for f in self.q["sync"]])
            block.gpsimd(lambda e: [f(e) for f in self.q["gpsimd"]])
            block.tensor(lambda e: [f(e) for f in self.q["tensor"]])
            block.vector(lambda e: [f(e) for f in self.q["vector"]])
            block.scalar(lambda e: [f(e) for f in self.q["scalar"]])


def _dt_size(dt):
    return {F32: 4, BF16: 2, U8: 1}[dt]



def _cm(W):
    Cin, Cout = np.asarray(W).shape
    return np.ascontiguousarray(
        np.asarray(W, np.float32).reshape(Cin // 128, 128, Cout)
        .transpose(1, 0, 2).reshape(128, -1))


def _pp(v):
    v = np.asarray(v, dtype=np.float32)
    K = v.shape[0] // 128
    return np.ascontiguousarray(v.reshape(K, 128).T)


def make_host_globals(inputs):
    g = {}
    g["w_qkv"] = _cm(inputs["L0_attn_Wqkv"])
    g["b_qkv_row"] = np.asarray(inputs["L0_attn_bqkv"], np.float32).reshape(1, 3 * C)
    g["w_proj"] = _cm(inputs["L0_attn_Wproj"])
    g["b_proj_pp"] = _pp(inputs["L0_attn_bproj"])
    g["w_r"] = _cm(np.asarray(inputs["L0_ffn_W_r"], np.float32).T)
    g["w_k"] = _cm(np.asarray(inputs["L0_ffn_W_k"], np.float32).T)
    g["w_v"] = _cm(np.asarray(inputs["L0_ffn_W_v"], np.float32).T)
    g["mu_pp"] = np.ascontiguousarray(
        np.concatenate([_pp(inputs["L0_ffn_mu_r"]), _pp(inputs["L0_ffn_mu_k"])],
                       axis=1))
    g["w_q1"] = _cm(np.asarray(inputs["L1_attn_Wq"], np.float32).T)
    g["w_k1"] = _cm(np.asarray(inputs["L1_attn_Wk"], np.float32).T)
    g["w_v1"] = _cm(np.asarray(inputs["L1_attn_Wv"], np.float32).T)
    g["b_qkv1_pp"] = np.ascontiguousarray(
        np.stack([np.asarray(inputs["L1_attn_bq"], np.float32),
                  np.asarray(inputs["L1_attn_bk"], np.float32),
                  np.asarray(inputs["L1_attn_bv"], np.float32)], axis=1))
    g["w_gate"] = _cm(np.asarray(inputs["L1_ffn_gate_w"], np.float32).T)
    g["b_gate_row"] = np.asarray(inputs["L1_ffn_gate_b"], np.float32).reshape(1, 8)

    w1 = np.asarray(inputs["L1_ffn_w1"], np.float32)
    w3 = np.asarray(inputs["L1_ffn_w3"], np.float32)
    w2 = np.asarray(inputs["L1_ffn_w2"], np.float32)
    sw1 = np.asarray(inputs["L1_ffn_sw1"], np.float32)
    sw3 = np.asarray(inputs["L1_ffn_sw3"], np.float32)
    sw2 = np.asarray(inputs["L1_ffn_sw2"], np.float32)

    def blocks13(w_all, sw):
        out = np.empty((NEXP * NHK, 128, NC * 128), np.float32)
        for e in range(NEXP):
            w = sw if e == 8 else w_all[e]
            wr = w.reshape(NC, 128, NHK, 128)
            for hk in range(NHK):
                out[e * NHK + hk] = wr[:, :, hk, :].transpose(1, 0, 2).reshape(128, -1)
        return np.ascontiguousarray(out)

    g["w1b"] = blocks13(w1, sw1)
    g["w3b"] = blocks13(w3, sw3)
    w2b = np.empty((NEXP * NHK, 128, C), np.float32)
    for e in range(NEXP):
        w2b[e * NHK:(e + 1) * NHK] = (sw2 if e == 8 else w2[e]).reshape(NHK, 128, C)
    g["w2b"] = np.ascontiguousarray(w2b)

    def b13(b_all, sb):
        cols = [(sb if e == 8 else b_all[e]).reshape(NHK, 128).T for e in range(NEXP)]
        return np.ascontiguousarray(np.concatenate(cols, axis=1), dtype=np.float32)

    g["b1_pp"] = b13(np.asarray(inputs["L1_ffn_b1"], np.float32),
                     np.asarray(inputs["L1_ffn_sb1"], np.float32))
    g["b3_pp"] = b13(np.asarray(inputs["L1_ffn_b3"], np.float32),
                     np.asarray(inputs["L1_ffn_sb3"], np.float32))
    g["b2cat"] = np.ascontiguousarray(
        np.concatenate([np.asarray(inputs["L1_ffn_b2"], np.float32),
                        np.asarray(inputs["L1_ffn_sb2"], np.float32).reshape(1, C)],
                       axis=0))

    m4 = np.zeros((128, 4 * 512), np.float32)
    for pp in range(4):
        for qtl in range(4):
            blk = m4[:, pp * 512 + qtl * 128:pp * 512 + (qtl + 1) * 128]
            if qtl < pp:
                blk[:] = -1e9
            elif qtl == pp:
                ii = np.arange(128)
                blk[:] = np.where(ii[:, None] <= ii[None, :], 0.0, -1e9)
    g["mask4t"] = m4.astype(ml_dtypes.bfloat16)
    g["ones_col"] = np.ones((128, 1), dtype=ml_dtypes.bfloat16)
    g["ident_bf"] = np.eye(128, dtype=ml_dtypes.bfloat16)
    g["ident_f32"] = np.eye(128, dtype=np.float32)
    g["mask_bf"] = np.where(np.tril(np.ones((128, 128), bool)), 0.0, -1e9
                            ).astype(ml_dtypes.bfloat16)
    g["ones_bf"] = np.ones((1, 512), dtype=ml_dtypes.bfloat16)
    g["ones_f32"] = np.ones((1, 128), dtype=np.float32)
    return g


GLOBAL_SPECS = [
    ("w_qkv", (128, NC * 3 * C), F32), ("b_qkv_row", (1, 3 * C), F32),
    ("w_proj", (128, NC * C), F32), ("b_proj_pp", (128, NC), F32),
    ("w_r", (128, NC * C), F32), ("w_k", (128, NC * C), F32),
    ("w_v", (128, NC * C), F32), ("mu_pp", (128, 16), F32),
    ("w_q1", (128, NC * HD), F32), ("w_k1", (128, NC * HD), F32),
    ("w_v1", (128, NC * HD), F32), ("b_qkv1_pp", (64, 3), F32),
    ("w_gate", (128, NC * 8), F32), ("b_gate_row", (1, 8), F32),
    ("w1b", (NEXP * NHK, 128, NC * 128), F32),
    ("w3b", (NEXP * NHK, 128, NC * 128), F32),
    ("w2b", (NEXP * NHK, 128, C), F32),
    ("b1_pp", (128, NEXP * NHK), F32), ("b3_pp", (128, NEXP * NHK), F32),
    ("b2cat", (NEXP, C), F32),
    ("mask4t", (128, 4 * 512), BF16), ("ones_col", (128, 1), BF16),
    ("ident_bf", (128, 128), BF16), ("ident_f32", (128, 128), F32),
    ("mask_bf", (128, 128), BF16), ("ones_bf", (1, 512), BF16),
    ("ones_f32", (1, 128), F32),
]



class BuildA:
    def __init__(self, stop="all", dbg=()):
        self.stop = stop
        self.dbg = set(dbg)
        nc = bass.Bass("TRN2", target_bir_lowering=False)
        self.nc = nc
        self.P = Prog(nc)
        self.x_in = nc.dram_tensor("x", [T, C], F32, kind="ExternalInput").ap()
        self.g = {}
        for name, shape, dt in GLOBAL_SPECS:
            self.g[name] = nc.dram_tensor(name, list(shape), dt,
                                          kind="ExternalInput").ap()
        self.xo = nc.dram_tensor("xo", [T, C], F32, kind="ExternalOutput").ap()
        self.qo = nc.dram_tensor("qo", [HD, T], BF16, kind="ExternalOutput").ap()
        self.ko = nc.dram_tensor("ko", [HD, T], BF16, kind="ExternalOutput").ap()
        self.vo = nc.dram_tensor("vo", [128, NT * HD], BF16, kind="ExternalOutput").ap()
        self._setup_sbuf()
        self._build()

    def _setup_sbuf(self):
        nc = self.nc
        sbt = nc.alloc_sbuf_tensor
        self.mask4t = sbt("s_mask4t", [128, 4 * 512], BF16).ap()
        self.ones_col = sbt("s_ones_col", [128, 1], BF16).ap()
        self.ident_bf = sbt("s_ident_bf", [128, 128], BF16).ap()
        self.ident_f32 = sbt("s_ident_f32", [128, 128], F32).ap()
        self.mask_bf = sbt("s_mask_bf", [128, 128], BF16).ap()
        self.ones_bf = sbt("s_ones_bf", [1, 512], BF16).ap()
        self.ones_f32 = sbt("s_ones_f32", [1, 128], F32).ap()
        self.bqkv_row = sbt("bqkv_row", [1, 3 * C], BF16).ap()
        self.bproj_pp = sbt("bproj_pp", [128, NC], F32).ap()
        self.mu_pp = sbt("s_mu_pp", [128, 16], F32).ap()
        self.bqkv1_pp = sbt("bqkv1_pp", [64, 3], F32).ap()
        self.bgate_row = sbt("bgate_row", [1, 8], F32).ap()
        self.b1_pp = sbt("s_b1_pp", [128, NEXP * NHK], F32).ap()
        self.b3_pp = sbt("s_b3_pp", [128, NEXP * NHK], F32).ap()
        self.b2cat = sbt("s_b2cat", [NEXP, C], BF16).ap()
        self.wgate = sbt("wgate", [128, NC * 8], BF16).ap()
        self.gbb = sbt("gbb", [128, 8], F32).ap()
        self.stat = sbt("stat", [128, 16], F32).ap()
        self.gsc = sbt("gsc", [128, 32], F32).ap()
        self.rw_tm = sbt("rw_tm", [128, NT * NEXP], F32).ap()
        self.rw_fm = sbt("rw_fm", [NEXP, T], BF16).ap()
        self.scr = sbt("scr", [128, 2048], F32).ap()
        self.px = sbt("px", [128, T + 1], F32).ap()
        self.xnt = sbt("xnt", [128, 2 * C], BF16).ap()
        self.xnb = sbt("xnb", [128, NC * T], BF16).ap()

        rem = nc.sbuf_bytes_remaining
        self.arena_bytes = (rem - 3 * KB) // 4 * 4
        assert self.arena_bytes >= 127 * KB, f"arena too small: {self.arena_bytes}"
        self.AR = sbt("arena", [128, self.arena_bytes], U8).ap()

        self.ps_s = nc.alloc_psum_tensor("ps_s", [128, 2048], F32)
        self.pbA = nc.alloc_psum_tensor("pbA", [128, 512], F32)
        self.pbB = nc.alloc_psum_tensor("pbB", [128, 512], F32)
        self.pbC = nc.alloc_psum_tensor("pbC", [128, 512], F32)
        self.pbD = nc.alloc_psum_tensor("pbD", [128, 512], F32)

    def av(self, off_b, parts, cols, dt):
        n = cols * _dt_size(dt)
        assert off_b % 4 == 0 and off_b + n <= self.arena_bytes, (off_b, n)
        return self.AR[0:parts, off_b:off_b + n].bitcast(dt)

    def psf(self, t, cols=512, base=0, parts=128, pbase=0):
        return t.ap()[pbase:pbase + parts, base:base + cols]

    def psb(self, t, cols, base=0, parts=128, pbase=0):
        return t.ap().bitcast(BF16)[pbase:pbase + parts, base:base + cols]

    def dbg_store(self, name, src_ap, shape, dt, waits):
        if name not in self.dbg:
            return
        od = self.nc.dram_tensor("dbg_" + name, list(shape), dt,
                                 kind="ExternalOutput").ap()
        P, st = self.P, self.P.sem("st")
        for s, n in waits:
            P.wait("sync", s, n)
        P.add("sync", lambda e, o=od, a=src_ap: e.dma_start(out=o, in_=a), inc=st, k=16)

    def _build(self):
        self._load_consts()
        self._ln(src_kind="dram", dram=self.x_in, ready=None)
        self.dbg_store("xnb0", self.xnb, (128, NC * T), BF16,
                       [(self.P.sem("tA"), self.P.sem("tA").n)])
        if self.stop == "ln":
            return self._finish()
        self._ffn()
        if self.stop == "ffn":
            return self._finish()
        self._attn()
        if self.stop == "attn":
            return self._finish()
        self._proj_resid()
        if self.stop == "resid":
            return self._finish()
        self._ln(src_kind="dram", dram=self.xo, ready=self.x1_tile_ready)
        self.dbg_store("xnb1", self.xnb, (128, NC * T), BF16,
                       [(self.P.sem("tA"), self.P.sem("tA").n)])
        if self.stop == "ln1":
            return self._finish()
        self._qkv1()
        if self.stop == "qkv1":
            return self._finish()
        self._gate()
        if self.stop == "gate":
            return self._finish()
        self._moe()
        return self._finish()

    def _finish(self):
        P = self.P
        for s in ("st", "qXo", "qQKV"):
            sem = P.sem(s)
            if sem.n:
                P.wait("sync", sem, sem.n)
        qM = P.sem("qM")
        if qM.n:
            P.wait("gpsimd", qM, qM.n)
        P.finalize()

    def _load_consts(self):
        P = self.P
        qS, qG, tT, tV = P.sem("qS"), P.sem("qG"), P.sem("tT"), P.sem("tV")
        for t, name in [
            (self.mask4t, "mask4t"), (self.ones_col, "ones_col"),
            (self.ident_bf, "ident_bf"), (self.ident_f32, "ident_f32"),
            (self.mask_bf, "mask_bf"), (self.ones_bf, "ones_bf"),
            (self.ones_f32, "ones_f32"), (self.bproj_pp, "b_proj_pp"),
            (self.mu_pp, "mu_pp"), (self.bqkv1_pp, "b_qkv1_pp"),
            (self.b1_pp, "b1_pp"), (self.b3_pp, "b3_pp"),
            (self.bgate_row, "b_gate_row"),
        ]:
            P.add("sync", lambda e, t=t, a=self.g[name]: e.dma_start(out=t, in_=a),
                  inc=qS, k=16)
        for t, name in [(self.bqkv_row, "b_qkv_row"), (self.b2cat, "b2cat"),
                        (self.wgate, "w_gate")]:
            P.add("gpsimd", lambda e, t=t, a=self.g[name]: e.dma_start(out=t, in_=a),
                  inc=qG, k=16)
        self.n_const_s = qS.n
        self.n_const_g = qG.n
        P.wait("tensor", qS, self.n_const_s)
        n = P.add("tensor", lambda e: e.matmul(
            self.psf(self.pbD, 8), self.ones_f32[0:1, :], self.bgate_row,
            start=True, stop=True), inc=tT)
        P.wait("vector", tT, n)
        self.gbb_done = P.add("vector", lambda e: e.tensor_copy(
            self.gbb, self.psf(self.pbD, 8)), inc=tV)

    def _ln(self, src_kind, dram, ready):
        P = self.P
        qS, tV, tA, tT = P.sem("qS"), P.sem("tV"), P.sem("tA"), P.sem("tT")
        P.wait("tensor", tV, tV.n)
        P.wait("tensor", tA, tA.n)
        g_ = getattr(self, "_xnb_guard", None)
        if g_:
            P.wait("scalar", g_[0], g_[1])
            self._xnb_guard = None
        st = self.stat
        ln_x = [self.scr[:, 0:1024], self.scr[:, 1024:2048]]
        jk = self.px[:, 0:1024]
        packs = [self.pbA, self.pbB]
        evac_done = [0, 0]
        xnt_free = [0, 0]
        dve_read = [tV.n, tV.n]
        for n in range(NT):
            par = n % 2
            so = 8 * par
            xt = ln_x[par]
            if dve_read[par]:
                P.wait("sync", tV, dve_read[par])
            if ready is not None:
                P.wait("sync", ready[n][0], ready[n][1])
            ldn = P.add("sync", lambda e, n=n, xt=xt, dram=dram: e.dma_start(
                out=xt, in_=dram[n * 128:(n + 1) * 128, :]), inc=qS, k=16)
            P.wait("vector", qS, ldn)
            P.add("vector", lambda e, xt=xt, so=so: e.scalar_tensor_tensor(
                jk, xt, 1.0, xt, ALU.mult, ALU.mult, accum_out=st[:, so + 1:so + 2]))
            P.addf("vector", lambda e, xt=xt, so=so: e.tensor_scalar(
                jk, xt, 1.0, 0.0, ALU.mult, ALU.add, accum_out=st[:, so:so + 1]))
            P.add("vector", lambda e, so=so: e.tensor_scalar(
                st[:, so + 2:so + 3], st[:, so:so + 1], 1.0 / C, None, ALU.mult))
            P.addf("vector", lambda e, so=so: e.tensor_scalar(
                st[:, so + 3:so + 4], st[:, so + 1:so + 2], 1.0 / C, 1e-5,
                ALU.mult, ALU.add))
            P.addf("vector", lambda e, so=so: e.scalar_tensor_tensor(
                st[:, so + 5:so + 6], st[:, so + 2:so + 3], -1.0,
                st[:, so + 2:so + 3], ALU.mult, ALU.mult))
            nvar = P.add("vector", lambda e, so=so: e.tensor_add(
                st[:, so + 3:so + 4], st[:, so + 3:so + 4], st[:, so + 5:so + 6]),
                inc=tV)
            P.wait("scalar", tV, nvar)
            P.addf("scalar", lambda e, so=so: e.activation(
                st[:, so + 5:so + 6], st[:, so + 3:so + 4], AF.Ln))
            nrstd = P.add("scalar", lambda e, so=so: e.activation(
                st[:, so + 4:so + 5], st[:, so + 5:so + 6], AF.Exp, scale=-0.5),
                inc=tA)
            P.wait("vector", tA, nrstd)
            if xnt_free[par]:
                P.wait("vector", tA, xnt_free[par])
            P.addf("vector", lambda e, so=so: e.scalar_tensor_tensor(
                st[:, so + 5:so + 6], st[:, so + 2:so + 3], -1.0,
                st[:, so + 4:so + 5], ALU.mult, ALU.mult))
            xnt = self.xnt[:, par * C:(par + 1) * C]
            nxn = P.add("vector", lambda e, xt=xt, xnt=xnt, so=so: e.tensor_scalar(
                xnt, xt, st[:, so + 4:so + 5], st[:, so + 5:so + 6],
                ALU.mult, ALU.add), inc=tV)
            dve_read[par] = nxn
            P.wait("tensor", tV, nxn)
            if evac_done[par]:
                P.wait("tensor", tA, evac_done[par])
            pk = packs[par]
            for ck in range(NC):
                P.add("tensor", lambda e, ck=ck, pk=pk, xnt=xnt: e.transpose(
                    self.psb(pk, 128, base=ck * 128), xnt[:, ck * 128:(ck + 1) * 128],
                    self.ident_bf), inc=(tT if ck == NC - 1 else None))
            P.wait("scalar", tT, tT.n)
            dst = self.xnb.rearrange("p (ck t) -> p ck t", ck=NC)[:, :, n * 128:(n + 1) * 128]
            srcv = self.psb(pk, 1024).rearrange("p (ck t) -> p ck t", ck=NC)
            ne = P.add("scalar", lambda e, dst=dst, srcv=srcv: e.activation(
                dst, srcv, AF.Copy), inc=tA)
            evac_done[par] = ne
            xnt_free[par] = ne
        self.ln_tA = tA.n

    def _ffn(self):
        P = self.P
        qG, tV, tA, tT = P.sem("qG"), P.sem("tV"), P.sem("tA"), P.sem("tT")
        xr = self.av(0, 128, NC * T, BF16)
        xk = self.av(32 * KB, 128, NC * T, BF16)
        kact = self.av(64 * KB, 128, NC * T, BF16)
        r_bf = self.av(32 * KB, 128, NC * T, BF16)
        ffn_C = self.av(0, 128, NC * T, BF16)
        wbuf = self.av(96 * KB, 128, NC * 1024, BF16)
        self.ffn_C = ffn_C

        P.addf("vector", lambda e: e.memset(self.px[:, 0:1], 0.0))
        P.wait("vector", P.sem("tA"), self.ln_tA)
        d = self.scr[:, 0:2048]
        xr_v = xr.rearrange("p (ck t) -> p ck t", ck=NC)
        xk_v = xk.rearrange("p (ck t) -> p ck t", ck=NC)
        for ck in range(NC):
            src = self.xnb[:, ck * T:(ck + 1) * T]
            P.addf("vector", lambda e, src=src: e.tensor_copy(self.px[:, 1:T + 1], src))
            P.addf("vector", lambda e: e.tensor_sub(d, self.px[:, 1:T + 1],
                                                    self.px[:, 0:T]))
            P.add("vector", lambda e, ck=ck: e.scalar_tensor_tensor(
                xr_v[:, ck], d, self.mu_pp[:, ck:ck + 1], self.px[:, 0:T],
                ALU.mult, ALU.add))
            self.mix_done = P.add("vector", lambda e, ck=ck: e.scalar_tensor_tensor(
                xk_v[:, ck], d, self.mu_pp[:, 8 + ck:9 + ck], self.px[:, 0:T],
                ALU.mult, ALU.add), inc=tV)

        def load_w(name, wait_t):
            P.wait("gpsimd", tT, wait_t)
            return P.add("gpsimd", lambda e, a=self.g[name]: e.dma_start(
                out=wbuf, in_=a), inc=qG, k=16)

        banks8 = [self.pbA, self.pbB, self.pbC, self.pbD] + [self.ps_s] * 4
        bank_base = [0, 0, 0, 0, 0, 512, 1024, 1536]

        def bank(i):
            return banks8[i], bank_base[i]

        nwk = load_w("w_k", 0)
        P.wait("tensor", qG, nwk)
        P.wait("tensor", tV, self.mix_done)
        kact_v = kact.rearrange("p (dk t) -> p dk t", dk=NC)
        tmp = [self.scr[:, 0:512], self.scr[:, 512:1024]]
        tmp_tA = [0, 0]
        bank_free = [0] * 8
        for dk in range(NC):
            for ns in range(4):
                bi = (dk % 2) * 4 + ns
                bt, bb = bank(bi)
                P.wait("tensor", tV, bank_free[bi])
                for ck in range(NC):
                    P.add("tensor", lambda e, dk=dk, ns=ns, ck=ck, bt=bt, bb=bb:
                          e.matmul(self.psf(bt, 512, base=bb),
                                   wbuf[:, ck * 1024 + dk * 128:ck * 1024 + (dk + 1) * 128],
                                   xk[:, ck * T + ns * 512:ck * T + (ns + 1) * 512],
                                   start=(ck == 0), stop=(ck == NC - 1)),
                          inc=(tT if ck == NC - 1 else None))
                nmm = tT.n
                wpar = (dk * 4 + ns) % 2
                P.wait("vector", tT, nmm)
                P.wait("vector", tA, tmp_tA[wpar])
                nrelu = P.add("vector", lambda e, bt=bt, bb=bb, wpar=wpar:
                              e.tensor_scalar(tmp[wpar], self.psf(bt, 512, base=bb),
                                              0.0, None, ALU.max), inc=tV)
                bank_free[bi] = nrelu
                P.wait("scalar", tV, nrelu)
                tmp_tA[wpar] = P.add("scalar", lambda e, dk=dk, ns=ns, wpar=wpar:
                                     e.activation(kact_v[:, dk, ns * 512:(ns + 1) * 512],
                                                  tmp[wpar], AF.Square), inc=tA)
        k_mm_done = tT.n
        kact_tA = tA.n

        nwr = load_w("w_r", k_mm_done)
        P.wait("tensor", qG, nwr)
        rbf_v = r_bf.rearrange("p (dk t) -> p dk t", dk=NC)
        r_bank_tA = [0] * 4
        for dk in range(NC):
            for ns in range(4):
                bi = ns
                bt, bb = bank(bi)
                P.wait("tensor", tV, bank_free[bi])
                P.wait("tensor", tA, r_bank_tA[bi])
                for ck in range(NC):
                    P.add("tensor", lambda e, dk=dk, ns=ns, ck=ck, bt=bt, bb=bb:
                          e.matmul(self.psf(bt, 512, base=bb),
                                   wbuf[:, ck * 1024 + dk * 128:ck * 1024 + (dk + 1) * 128],
                                   xr[:, ck * T + ns * 512:ck * T + (ns + 1) * 512],
                                   start=(ck == 0), stop=(ck == NC - 1)),
                          inc=(tT if ck == NC - 1 else None))
                nmm = tT.n
                P.wait("scalar", tT, nmm)
                r_bank_tA[bi] = P.add("scalar", lambda e, dk=dk, ns=ns, bt=bt, bb=bb:
                                      e.activation(rbf_v[:, dk, ns * 512:(ns + 1) * 512],
                                                   self.psf(bt, 512, base=bb),
                                                   AF.Sigmoid), inc=tA)
        r_mm_done = tT.n
        sig_all = tA.n

        nwv = load_w("w_v", r_mm_done)
        P.wait("tensor", qG, nwv)
        P.wait("tensor", tA, kact_tA)
        ffnC_v = ffn_C.rearrange("p (dk t) -> p dk t", dk=NC)
        v_bank_tV = [bank_free[4 + i] for i in range(4)]
        v_bank_tV = [0, 0, 0, 0]
        for i in range(4):
            v_bank_tV[i] = bank_free[4 + i]
        for dk in range(NC):
            for ns in range(4):
                bi = 4 + ns
                bt, bb = bank(bi)
                P.wait("tensor", tV, v_bank_tV[ns])
                for ck in range(NC):
                    P.add("tensor", lambda e, dk=dk, ns=ns, ck=ck, bt=bt, bb=bb:
                          e.matmul(self.psf(bt, 512, base=bb),
                                   wbuf[:, ck * 1024 + dk * 128:ck * 1024 + (dk + 1) * 128],
                                   kact[:, ck * T + ns * 512:ck * T + (ns + 1) * 512],
                                   start=(ck == 0), stop=(ck == NC - 1)),
                          inc=(tT if ck == NC - 1 else None))
                nmm = tT.n
                P.wait("vector", tT, nmm)
                P.wait("vector", tA, sig_all if (dk == 0 and ns == 0) else 0)
                v_bank_tV[ns] = P.add("vector", lambda e, dk=dk, ns=ns, bt=bt, bb=bb:
                                      e.tensor_mul(ffnC_v[:, dk, ns * 512:(ns + 1) * 512],
                                                   self.psf(bt, 512, base=bb),
                                                   rbf_v[:, dk, ns * 512:(ns + 1) * 512]),
                                      inc=tV)
        self.ffn_tV = tV.n
        self.ffn_v_mm_done = tT.n
        self.dbg_store("ffn", ffn_C, (128, NC * T), BF16, [(tV, tV.n)])

    def _attn(self):
        P = self.P
        qG, tV, tA, tT = P.sem("qG"), P.sem("tV"), P.sem("tA"), P.sem("tT")
        wqkv = self.av(32 * KB, 128, NC * 3 * C, BF16)
        yD = self.av(80 * KB, 128, NC * T, BF16)
        QT = self.av(112 * KB, 64, T, BF16)
        KT = self.av(116 * KB, 64, T, BF16)
        VTP = self.av(125 * KB, 128, T, BF16)
        VTm = self.av(120 * KB, 128, NT * (HD + 1), BF16)
        PTc = [self.av(122 * KB + 256, 128, 512, BF16),
               self.av(123 * KB + 256, 128, 512, BF16)]
        scrb = self.scr.bitcast(BF16)
        lrow_bf = scrb[0:1, 0:512]
        rbc_sb = self.scr[0:64, 512:1024]
        tm_h = self.px[:, 0:1536].bitcast(BF16)
        self.yD = yD

        nwq = None
        P.wait("gpsimd", tT, self.ffn_v_mm_done)
        nwq = P.add("gpsimd", lambda e: e.dma_start(out=wqkv, in_=self.g["w_qkv"]),
                    inc=qG, k=16)
        P.wait("tensor", tV, tV.n)
        P.wait("tensor", tA, tA.n)
        kvg = getattr(self, "_kv_share_done", None)
        if kvg:
            for s_, n_ in kvg:
                P.wait("vector", s_, n_)
                P.wait("scalar", s_, n_)
            self._kv_share_done = None
        P.wait("tensor", qG, nwq)
        P.wait("tensor", qG, self.n_const_g)
        st8 = self.stat
        nvone = P.add("vector", lambda e: e.memset(
            VTm.rearrange("p (n c) -> p n c", c=HD + 1)[:, :, HD], 1.0),
            inc=tV)
        P.wait("scalar", tV, nvone)

        qkv_banks = [self.pbA, self.pbB]
        qkv_free_tA = [0, 0]
        pack_banks = [self.pbC, self.pbD]
        pack_free = [0, 0]
        pack_free_eng = [None, None]

        for h in range(NHEAD):
            P.wait("tensor", tV, tV.n)
            P.wait("tensor", tA, tA.n)
            for slab in range(6):
                par = slab % 2
                bt = qkv_banks[par]
                P.wait("tensor", tA, qkv_free_tA[par])
                for ck in range(NC):
                    P.add("tensor", lambda e, h=h, slab=slab, ck=ck, bt=bt: e.matmul(
                        self.psf(bt, 512),
                        self.xnb[:, ck * T + h * 128:ck * T + (h + 1) * 128],
                        wqkv[:, ck * 3072 + slab * 512:ck * 3072 + (slab + 1) * 512],
                        start=(ck == 0), stop=False))
                nmm = P.add("tensor", lambda e, slab=slab, bt=bt: e.matmul(
                    self.psf(bt, 512), self.ones_bf[0:1, 0:128],
                    self.bqkv_row[0:1, slab * 512:(slab + 1) * 512],
                    start=False, stop=True), inc=tT)
                P.wait("scalar", tT, nmm)
                qkv_free_tA[par] = P.add("scalar", lambda e, slab=slab, bt=bt:
                                         e.activation(tm_h[:, slab * 512:(slab + 1) * 512],
                                                      self.psf(bt, 512), AF.Copy),
                                         inc=tA)
            qkv_done_tA = tA.n

            def emit_pack(src_col0, dst_fm, s_lo, pk_i, on_act):
                pk = pack_banks[pk_i]
                P.wait("tensor", tA, qkv_done_tA)
                if pack_free[pk_i]:
                    P.wait("tensor", P.sem(pack_free_eng[pk_i]), pack_free[pk_i])
                for i in range(8):
                    s = s_lo + i
                    P.add("tensor", lambda e, s=s, i=i, pk=pk, src_col0=src_col0:
                          e.transpose(self.psb(pk, 128, base=i * 128, parts=64),
                                      tm_h[:, src_col0 + s * 64:src_col0 + (s + 1) * 64],
                                      self.ident_bf),
                          inc=(tT if i == 7 else None))
                ntr = tT.n
                eng = "scalar" if on_act else "vector"
                P.wait(eng, tT, ntr)
                dstv = dst_fm.rearrange("d (j s) -> d s j", s=16)[:, s_lo:s_lo + 8, :]
                srcv = self.psb(pk, 1024, parts=64).rearrange("d (s j) -> d s j", s=8)
                if on_act:
                    nn = P.add("scalar", lambda e, dstv=dstv, srcv=srcv:
                               e.activation(dstv, srcv, AF.Copy), inc=tA)
                else:
                    nn = P.add("vector", lambda e, dstv=dstv, srcv=srcv:
                               e.tensor_copy(dstv, srcv), inc=tV)
                pack_free[pk_i] = nn
                pack_free_eng[pk_i] = "tA" if on_act else "tV"
                return nn

            emit_pack(0, QT, 0, 0, False)
            emit_pack(0, QT, 8, 1, False)
            emit_pack(C, KT, 0, 0, False)
            nk = emit_pack(C, KT, 8, 1, False)
            emit_pack(2 * C, VTP[0:64, :], 0, 0, True)
            nvt = emit_pack(2 * C, VTP[0:64, :], 8, 1, True)
            pk = pack_banks[0]
            P.wait("tensor", P.sem(pack_free_eng[0]), pack_free[0])
            P.wait("tensor", tA, nvt)
            for gph in range(16):
                P.add("tensor", lambda e, gph=gph, pk=pk: e.transpose(
                    self.psb(pk, 64, base=gph * 64),
                    VTP[0:64, gph * 128:(gph + 1) * 128],
                    self.ident_bf[0:64, 0:64]),
                    inc=(tT if gph == 15 else None))
            nvtm_tr = tT.n
            P.wait("scalar", tT, nvtm_tr)
            nvtm = P.add("scalar", lambda e, pk=pk: e.activation(
                VTm.rearrange("p (n c) -> p n c", c=HD + 1)[:, :, 0:HD],
                self.psb(pk, 1024).rearrange("p (n c) -> p n c", c=HD),
                AF.Copy), inc=tA)
            pack_free[0] = nvtm
            pack_free_eng[0] = "tA"

            P.wait("tensor", tV, nk)
            P.wait("tensor", tA, nvtm)
            ybase = [1024, 1536]
            prev_y_ev = [0, 0]
            prev_mult_tV = 0
            pend = None

            def ep_levac(pd):
                gp_, qg_, npv_ = pd
                P.wait("scalar", tT, npv_)
                P.wait("scalar", tT, self._nbc)
                def _lv(e, gp_=gp_):
                    with self.nc.allow_low_precision(
                            reason="softmax row-sum as bf16 matmul operand"):
                        return e.activation(
                            lrow_bf, self.psf(self.ps_s, 512, base=ybase[gp_],
                                              parts=1, pbase=64), AF.Copy)
                return P.add("scalar", _lv, inc=tA)

            def ep_bcast(pd, nlv_):
                P.wait("tensor", tA, nlv_)
                P.wait("tensor", tV, self._recip_tV)
                self._nbc = P.add("tensor", lambda e: e.matmul(
                    self.psf(self.pbD, 512, parts=64),
                    self.ones_bf[0:1, 0:64], lrow_bf,
                    start=True, stop=True), inc=tT)
                return self._nbc

            def ep_mult(pd, h=h):
                gp_, qg_, npv_ = pd
                P.wait("vector", tT, self._nbc)
                self._recip_tV = P.addf("vector", lambda e: e.reciprocal(
                    rbc_sb, self.psf(self.pbD, 512, parts=64)))
                dst = yD[(h % 2) * 64:(h % 2) * 64 + 64,
                         (h // 2) * T + qg_ * 512:(h // 2) * T
                         + (qg_ + 1) * 512]
                n = P.add("vector", lambda e, dst=dst, gp_=gp_:
                          e.tensor_mul(
                    dst, self.psf(self.ps_s, 512, base=ybase[gp_], parts=64),
                    rbc_sb), inc=tV)
                prev_y_ev[gp_] = n
                return n

            self._nbc = getattr(self, "_nbc", 0)
            self._recip_tV = getattr(self, "_recip_tV", 0)
            for qg in range(4):
                nch = 4 * qg + 4
                gp = qg % 2
                qc0 = qg * 512
                nst = [0] * nch
                nexpc = [0] * nch
                npvc = [0] * nch
                nlv_p = 0
                P.wait("tensor", tV, prev_y_ev[gp])
                for kt in range(nch + 1):
                    sp = kt % 2
                    if pend is not None:
                        if kt == 1:
                            nlv_p = ep_levac(pend)
                        elif kt == 2:
                            ep_bcast(pend, nlv_p)
                        elif kt == 3:
                            prev_mult_tV = ep_mult(pend)
                            pend = None
                    if kt < nch:
                        if kt >= 2:
                            P.wait("tensor", tA, nexpc[kt - 2])
                        masked = kt >= 4 * qg
                        P.add("tensor", lambda e, kt=kt, sp=sp, qc0=qc0,
                              masked=masked: e.matmul(
                            self.psf(self.ps_s, 512, base=sp * 512),
                            KT[:, kt * 128:(kt + 1) * 128], QT[:, qc0:qc0 + 512],
                            start=True, stop=not masked),
                            inc=(None if masked else tT))
                        if masked:
                            pp = kt - 4 * qg
                            P.add("tensor", lambda e, sp=sp, pp=pp: e.matmul(
                                self.psf(self.ps_s, 512, base=sp * 512),
                                self.ident_bf,
                                self.mask4t[:, pp * 512:(pp + 1) * 512],
                                start=False, stop=True), inc=tT)
                        nst[kt] = tT.n
                        P.wait("scalar", tT, nst[kt])
                        if kt >= 2:
                            P.wait("scalar", tT, npvc[kt - 2])
                        nexpc[kt] = P.add("scalar", lambda e, sp=sp: e.activation(
                            PTc[sp], self.psf(self.ps_s, 512, base=sp * 512),
                            AF.Exp, scale=0.125), inc=tA)
                    if kt >= 1:
                        cj = kt - 1
                        cp = cj % 2
                        P.wait("tensor", tA, nexpc[cj])
                        npvc[cj] = P.add("tensor", lambda e, cp=cp, gp=gp,
                                         cj=cj, nch=nch: e.matmul(
                            self.psf(self.ps_s, 512, base=ybase[gp], parts=65),
                            VTm[:, cj * (HD + 1):(cj + 1) * (HD + 1)],
                            PTc[cp],
                            start=(cj == 0), stop=(cj == nch - 1)), inc=tT)
                pend = (gp, qg, npvc[nch - 1])
                if qg == 3:
                    nlv_p = ep_levac(pend)
                    ep_bcast(pend, nlv_p)
                    prev_mult_tV = ep_mult(pend)
                    pend = None
        self.attn_tA = tA.n
        self.attn_mm_done = tT.n
        self.dbg_store("yd", yD, (128, NC * T), BF16, [(tA, tA.n), (tV, tV.n)])

    def _Pv(self, L):
        return self.av(120 * KB, 128, L, BF16)

    def _proj_resid(self, x_dram=None, xo_dram=None):
        if x_dram is None:
            x_dram = self.x_in
        if xo_dram is None:
            xo_dram = self.xo
        P = self.P
        qG, qS, tV, tA, tT = (P.sem(s) for s in ("qG", "qS", "tV", "tA", "tT"))
        qXo = P.sem("qXo")
        wproj = self.av(32 * KB, 128, NC * C, BF16)
        x1s = [self.av(48 * KB, 128, C, F32), self.av(52 * KB, 128, C, F32)]
        ffn_C = self.ffn_C
        yD = self.yD

        P.wait("gpsimd", tT, self.attn_mm_done)
        nwp = P.add("gpsimd", lambda e: e.dma_start(out=wproj, in_=self.g["w_proj"]),
                    inc=qG, k=16)
        P.wait("tensor", tV, tV.n)
        P.wait("tensor", tA, tA.n)
        P.wait("tensor", qG, nwp)
        banks = [self.pbA, self.pbB]
        bank_free_tV = [0, 0]
        comb_tV = 0
        for cko in range(NC):
            for ns in range(4):
                par = (cko * 4 + ns) % 2
                bt = banks[par]
                P.wait("tensor", tV, bank_free_tV[par])
                for ck in range(NC):
                    P.add("tensor", lambda e, cko=cko, ns=ns, ck=ck, bt=bt: e.matmul(
                        self.psf(bt, 512),
                        wproj[:, ck * C + cko * 128:ck * C + (cko + 1) * 128],
                        yD[:, ck * T + ns * 512:ck * T + (ns + 1) * 512],
                        start=(ck == 0), stop=(ck == NC - 1)),
                        inc=(tT if ck == NC - 1 else None))
                nmm = tT.n
                P.wait("vector", tT, nmm)
                cv = ffn_C.rearrange("p (ck t) -> p ck t", ck=NC)[:, cko,
                                                                  ns * 512:(ns + 1) * 512]
                bank_free_tV[par] = P.add("vector", lambda e, cko=cko, cv=cv, bt=bt:
                                          e.scalar_tensor_tensor(
                                              cv, self.psf(bt, 512),
                                              self.bproj_pp[:, cko:cko + 1], cv,
                                              ALU.add, ALU.add), inc=tV)
        comb_tV = tV.n

        self.x1_tile_ready = []
        packs = [self.pbC, self.pbD]
        pack_free_tV = [0, 0]
        x1s_free = [0, 0]
        xt_slots = [self.scr[:, 0:1024], self.scr[:, 1024:2048]]
        xt_read_tV = [0, 0]
        P.wait("tensor", tV, comb_tV)
        P.wait("sync", tV, comb_tV)
        P.wait("sync", tA, tA.n)
        for n in range(NT):
            par = n % 2
            P.wait("sync", tV, xt_read_tV[par])
            ldn = P.add("sync", lambda e, n=n, par=par, x_dram=x_dram: e.dma_start(
                out=xt_slots[par], in_=x_dram[n * 128:(n + 1) * 128, :]),
                inc=qS, k=16)
            pk = packs[par]
            P.wait("tensor", tV, pack_free_tV[par])
            for ck in range(NC):
                P.add("tensor", lambda e, n=n, ck=ck, pk=pk: e.transpose(
                    self.psb(pk, 128, base=ck * 128),
                    ffn_C[:, ck * T + n * 128:ck * T + (n + 1) * 128],
                    self.ident_bf), inc=(tT if ck == NC - 1 else None))
            ntr = tT.n
            P.wait("vector", tT, ntr)
            P.wait("vector", qS, ldn)
            P.wait("vector", qXo, x1s_free[par])
            nadd = P.add("vector", lambda e, par=par, pk=pk: e.tensor_add(
                x1s[par], xt_slots[par], self.psb(pk, 1024)), inc=tV)
            pack_free_tV[par] = nadd
            xt_read_tV[par] = nadd
            P.wait("sync", tV, nadd)
            nst = P.add("sync", lambda e, n=n, par=par, xo_dram=xo_dram: e.dma_start(
                out=xo_dram[n * 128:(n + 1) * 128, :], in_=x1s[par]), inc=qXo, k=16)
            x1s_free[par] = nst
            self.x1_tile_ready.append((qXo, nst))

    def _qkv1(self):
        P = self.P
        qG, qS, tV, tA, tT = (P.sem(s) for s in ("qG", "qS", "tV", "tA", "tT"))
        qQKV = P.sem("qQKV")
        wq1 = self.av(112 * KB, 128, NC * HD, BF16)
        wk1 = self.av(113 * KB, 128, NC * HD, BF16)
        wv1 = self.av(114 * KB, 128, NC * HD, BF16)
        v1t = self.av(115 * KB, 128, NT * HD, BF16)
        q1 = self.av(117 * KB, 64, T, BF16)
        k1 = self.av(121 * KB, 64, T, BF16)
        v1f = self.av(125 * KB, 64, T, BF16)

        P.wait("gpsimd", tT, self.attn_mm_done)
        nl = None
        for w, name in ((wq1, "w_q1"), (wk1, "w_k1"), (wv1, "w_v1")):
            nl = P.add("gpsimd", lambda e, w=w, a=self.g[name]: e.dma_start(
                out=w, in_=a), inc=qG, k=16)
        P.wait("tensor", tV, P.sem("tV").n)
        P.wait("tensor", tA, self.ln_tA)
        P.wait("tensor", qG, nl)
        banks = [self.pbA, self.pbB]
        bank_free_tV = [0, 0]
        outs = [(wq1, q1, 0), (wk1, k1, 1), (wv1, v1f, 2)]
        for wi, (w, dstf, bcol) in enumerate(outs):
            for ns in range(4):
                par = ns % 2
                bt = banks[par]
                P.wait("tensor", tV, bank_free_tV[par])
                for ck in range(NC):
                    P.add("tensor", lambda e, w=w, ck=ck, ns=ns, bt=bt: e.matmul(
                        self.psf(bt, 512, parts=64),
                        w[:, ck * HD:(ck + 1) * HD],
                        self.xnb[:, ck * T + ns * 512:ck * T + (ns + 1) * 512],
                        start=(ck == 0), stop=(ck == NC - 1)),
                        inc=(tT if ck == NC - 1 else None))
                nmm = tT.n
                P.wait("vector", tT, nmm)
                bank_free_tV[par] = P.add("vector", lambda e, dstf=dstf, ns=ns,
                                          bcol=bcol, bt=bt: e.tensor_scalar(
                    dstf[:, ns * 512:(ns + 1) * 512], self.psf(bt, 512, parts=64),
                    self.bqkv1_pp[:, bcol:bcol + 1], None, ALU.add), inc=tV)
        qkv_tV = tV.n
        P.wait("tensor", tV, qkv_tV)
        for gph in range(16):
            P.add("tensor", lambda e, gph=gph: e.transpose(
                self.psb(self.pbC, 64, base=gph * 64),
                v1f[:, gph * 128:(gph + 1) * 128], self.ident_bf[0:64, 0:64]),
                inc=(tT if gph == 15 else None))
        nvt = tT.n
        P.wait("scalar", tT, nvt)
        nvta = P.add("scalar", lambda e: e.activation(v1t, self.psb(self.pbC, 1024),
                                                      AF.Copy), inc=tA)
        P.wait("sync", tV, qkv_tV)
        P.add("sync", lambda e: e.dma_start(out=self.qo, in_=q1), inc=qQKV, k=16)
        P.add("sync", lambda e: e.dma_start(out=self.ko, in_=k1), inc=qQKV, k=16)
        P.wait("sync", tA, nvta)
        P.add("sync", lambda e: e.dma_start(out=self.vo, in_=v1t), inc=qQKV, k=16)
        self.qkv1_pbc_tA = nvta

    def _gate(self):
        P = self.P
        tV, tA, tT, qG = P.sem("tV"), P.sem("tA"), P.sem("tT"), P.sem("qG")
        gs = self.gsc
        P.add("vector", lambda e: e.memset(
            self.rw_tm.rearrange("p (n e) -> p n e", e=NEXP)[:, :, 8], 1.0))
        P.wait("vector", tV, self.gbb_done)
        P.wait("tensor", qG, self.n_const_g)
        exp_tA = 0
        rwfm_tV = 0
        for n in range(NT):
            P.wait("tensor", tA, exp_tA)
            P.wait("tensor", tA, self.qkv1_pbc_tA if n == 0 else 0)
            for ck in range(NC):
                P.add("tensor", lambda e, n=n, ck=ck: e.matmul(
                    self.psf(self.pbD, 8),
                    self.xnb[:, ck * T + n * 128:ck * T + (n + 1) * 128],
                    self.wgate[:, ck * 8:(ck + 1) * 8],
                    start=(ck == 0), stop=(ck == NC - 1)),
                    inc=(tT if ck == NC - 1 else None))
            nmm = tT.n
            P.wait("vector", tT, nmm)
            nng = P.add("vector", lambda e: e.reduce_max(
                gs[:, 24:25], self.psf(self.pbD, 8), axis=AX.X, negate=True), inc=tV)
            P.wait("scalar", tV, nng)
            exp_tA = P.add("scalar", lambda e: e.activation(
                gs[:, 0:8], self.psf(self.pbD, 8), AF.Exp, bias=gs[:, 24:25],
                accum_out=gs[:, 27:28]), inc=tA)
            P.wait("vector", tA, exp_tA)
            P.addf("vector", lambda e: e.reciprocal(gs[:, 28:29], gs[:, 27:28]))
            P.addf("vector", lambda e: e.tensor_scalar(
                gs[:, 0:8], gs[:, 0:8], gs[:, 28:29], None, ALU.mult))
            P.addf("vector", lambda e: e.tensor_add(gs[:, 8:16], gs[:, 0:8], self.gbb))
            P.addf("vector", lambda e: e.reduce_max(gs[:, 24:25], gs[:, 8:16], axis=AX.X))
            P.addf("vector", lambda e: e.tensor_scalar(
                gs[:, 16:24], gs[:, 8:16], gs[:, 24:25], None, ALU.is_ge))
            P.addf("vector", lambda e: e.scalar_tensor_tensor(
                gs[:, 16:24], gs[:, 16:24], -1e30, gs[:, 8:16], ALU.mult, ALU.add))
            P.addf("vector", lambda e: e.reduce_max(gs[:, 25:26], gs[:, 16:24], axis=AX.X))
            P.addf("vector", lambda e: e.tensor_scalar(
                gs[:, 16:24], gs[:, 8:16], gs[:, 25:26], None, ALU.is_ge))
            rw = self.rw_tm[:, n * NEXP:n * NEXP + 8]
            nrw = P.add("vector", lambda e, rw=rw: e.tensor_mul(
                rw, gs[:, 0:8], gs[:, 16:24]), inc=tV)
            P.wait("tensor", tV, nrw)
            P.wait("tensor", tV, rwfm_tV)
            ntr = P.add("tensor", lambda e, n=n: e.transpose(
                self.psf(self.pbC, 128, parts=NEXP),
                self.rw_tm[:, n * NEXP:(n + 1) * NEXP], self.ident_f32), inc=tT)
            P.wait("vector", tT, ntr)
            rwfm_tV = P.add("vector", lambda e, n=n: e.tensor_copy(
                self.rw_fm[:, n * 128:(n + 1) * 128],
                self.psf(self.pbC, 128, parts=NEXP)), inc=tV)
        self.gate_tV = tV.n
        self.gate_mm_tT = tT.n
        self.dbg_store("rw", self.rw_tm, (128, NT * NEXP), F32, [(tV, tV.n)])

    def _moe(self):
        P = self.P
        qG, tV, tA, tT = P.sem("qG"), P.sem("tV"), P.sem("tA"), P.sem("tT")
        qM, qXo = P.sem("qM"), P.sem("qXo")
        h_buf = self.av(0, 128, NHK * T, BF16)
        blk = [[self.av(44 * KB + (i * 2 + p) * 2 * KB, 128, 1024, BF16)
                for p in range(2)] for i in range(2)]
        w2 = self.av(52 * KB, 128, NHK * C, BF16)
        stg = [self.av(74 * KB, 128, C, F32), self.av(78 * KB, 128, C, F32)]

        tile_chain = [self.x1_tile_ready[n] for n in range(NT)]
        P.wait("gpsimd", qXo, P.sem("qXo").n)

        P.wait("tensor", tV, self.gate_tV)
        P.wait("tensor", qG, self.n_const_g)
        stg_free = [0, 0]
        psq_free = [0, 0, 0, 0]
        for n in range(NT):
            par = n % 2
            for cs in range(2):
                qi = par * 2 + cs
                P.wait("tensor", tV, psq_free[qi])
                P.add("tensor", lambda e, n=n, cs=cs, qi=qi: e.matmul(
                    self.psf(self.ps_s, 512, base=qi * 512),
                    self.rw_fm[:, n * 128:(n + 1) * 128],
                    self.b2cat[:, cs * 512:(cs + 1) * 512],
                    start=True, stop=True), inc=(tT if cs == 1 else None))
            nmm = tT.n
            P.wait("vector", tT, nmm)
            P.wait("vector", qM, stg_free[par])
            ncp = None
            for cs in range(2):
                qi = par * 2 + cs
                psq_free[qi] = P.add("vector", lambda e, par=par, cs=cs, qi=qi:
                                     e.tensor_copy(stg[par][:, cs * 512:(cs + 1) * 512],
                                                   self.psf(self.ps_s, 512, base=qi * 512)),
                                     inc=tV)
            ncp = psq_free[par * 2 + 1]
            P.wait("gpsimd", tV, ncp)
            cs_, cn_ = tile_chain[n]
            P.wait("gpsimd", cs_, cn_)
            nadd = P.add("gpsimd", lambda e, n=n, par=par: e.dma_start(
                out=self.xo[n * 128:(n + 1) * 128, :], in_=stg[par],
                accum_op=ALU.add), inc=qM, k=16)
            stg_free[par] = nadd
            tile_chain[n] = (qM, nadd)

        hv = h_buf.rearrange("p (hk t) -> p hk t", hk=NHK)
        w2v = w2.rearrange("p (hk c) -> p hk c", hk=NHK)
        esA = [self.scr[:, 0:512], self.scr[:, 512:1024]]
        esB = [self.scr[:, 1024:1536], self.scr[:, 1536:2048]]
        blk_free = [[0, 0], [0, 0]]
        w2_free = 0
        h_free = 0
        bankA = [(self.pbA, self.pbB), (self.pbC, self.pbD)]
        h1_free = [0, 0]
        h3_free = [0, 0]
        es_free = [0, 0]
        psh_free = [0, 0]
        P.wait("tensor", tV, tV.n)
        P.wait("tensor", tA, tA.n)
        for e_i in range(NEXP):
            P.wait("gpsimd", tT, w2_free)
            nw2 = None
            for hk in range(NHK):
                nw2 = P.add("gpsimd", lambda e, e_i=e_i, hk=hk: e.dma_start(
                    out=w2v[:, hk], in_=self.g["w2b"][e_i * NHK + hk]),
                    inc=qG, k=16)
            w2_loaded = nw2
            P.wait("scalar", tT, h_free)
            for hk in range(NHK):
                par = hk % 2
                P.wait("gpsimd", tT, blk_free[0][par])
                n1 = P.add("gpsimd", lambda e, e_i=e_i, hk=hk, par=par: e.dma_start(
                    out=blk[0][par], in_=self.g["w1b"][e_i * NHK + hk]), inc=qG, k=16)
                P.wait("gpsimd", tT, blk_free[1][par])
                n3 = P.add("gpsimd", lambda e, e_i=e_i, hk=hk, par=par: e.dma_start(
                    out=blk[1][par], in_=self.g["w3b"][e_i * NHK + hk]), inc=qG, k=16)
                bt1, bt3 = bankA[par]
                P.wait("tensor", qG, n3)
                for ns in range(4):
                    P.wait("tensor", tV, h1_free[par])
                    for ck in range(NC):
                        P.add("tensor", lambda e, hk=hk, ns=ns, ck=ck, par=par, bt1=bt1:
                              e.matmul(self.psf(bt1, 512),
                                       blk[0][par][:, ck * 128:(ck + 1) * 128],
                                       self.xnb[:, ck * T + ns * 512:ck * T + (ns + 1) * 512],
                                       start=(ck == 0), stop=(ck == NC - 1)),
                              inc=(tT if ck == NC - 1 else None))
                    n1mm = tT.n
                    P.wait("tensor", tA, h3_free[par])
                    for ck in range(NC):
                        P.add("tensor", lambda e, hk=hk, ns=ns, ck=ck, par=par, bt3=bt3:
                              e.matmul(self.psf(bt3, 512),
                                       blk[1][par][:, ck * 128:(ck + 1) * 128],
                                       self.xnb[:, ck * T + ns * 512:ck * T + (ns + 1) * 512],
                                       start=(ck == 0), stop=(ck == NC - 1)),
                              inc=(tT if ck == NC - 1 else None))
                    n3mm = tT.n
                    bcol = e_i * NHK + hk
                    P.wait("vector", tT, n1mm)
                    P.wait("vector", tA, es_free[par])
                    nt1 = P.add("vector", lambda e, par=par, bcol=bcol, bt1=bt1:
                                e.tensor_scalar(esA[par], self.psf(bt1, 512),
                                                self.b1_pp[:, bcol:bcol + 1], None,
                                                ALU.add), inc=tV)
                    P.wait("vector", tV, nt1)
                    h1_free[par] = nt1
                    P.wait("scalar", tT, n3mm)
                    nt2 = P.add("scalar", lambda e, par=par, bcol=bcol, bt3=bt3:
                                e.activation(esB[par], self.psf(bt3, 512), AF.Identity,
                                             bias=self.b3_pp[:, bcol:bcol + 1]),
                                inc=tA)
                    h3_free[par] = nt2
                    P.wait("vector", tA, nt2)
                    nt3 = P.add("vector", lambda e, par=par: e.tensor_mul(
                        esA[par], esA[par], esB[par]), inc=tV)
                    P.wait("scalar", tV, nt3)
                    nsl = P.add("scalar", lambda e, hk=hk, ns=ns, par=par:
                                e.activation(hv[:, hk, ns * 512:(ns + 1) * 512],
                                             esA[par], AF.Silu), inc=tA)
                    es_free[par] = nsl
                blk_free[0][par] = n1mm
                blk_free[1][par] = n3mm
            silu_all = P.sem("tA").n
            P.wait("tensor", tA, silu_all)
            P.wait("tensor", qG, w2_loaded)
            for tq in range(NT):
                par = tq % 2
                P.wait("tensor", tV, psh_free[par])
                for hk in range(NHK):
                    for cs in range(2):
                        P.add("tensor", lambda e, tq=tq, hk=hk, cs=cs, par=par:
                              e.matmul(self.psf(self.ps_s, 512, base=par * 1024 + cs * 512),
                                       hv[:, hk, tq * 128:(tq + 1) * 128],
                                       w2v[:, hk, cs * 512:(cs + 1) * 512],
                                       start=(hk == 0), stop=(hk == NHK - 1)),
                              inc=(tT if hk == NHK - 1 and cs == 1 else None))
                nmm = tT.n
                P.wait("vector", tT, nmm)
                P.wait("vector", qM, stg_free[par])
                rwcol = self.rw_tm[:, tq * NEXP + e_i:tq * NEXP + e_i + 1]
                nev = None
                for cs in range(2):
                    nev = P.add("vector", lambda e, par=par, cs=cs, rwcol=rwcol:
                                e.tensor_scalar(stg[par][:, cs * 512:(cs + 1) * 512],
                                                self.psf(self.ps_s, 512,
                                                         base=par * 1024 + cs * 512),
                                                rwcol, None, ALU.mult), inc=tV)
                psh_free[par] = nev
                P.wait("gpsimd", tV, nev)
                cs_, cn_ = tile_chain[tq]
                P.wait("gpsimd", cs_, cn_)
                nadd = P.add("gpsimd", lambda e, tq=tq, par=par: e.dma_start(
                    out=self.xo[tq * 128:(tq + 1) * 128, :], in_=stg[par],
                    accum_op=ALU.add), inc=qM, k=16)
                stg_free[par] = nadd
                tile_chain[tq] = (qM, nadd)
            w2_free = tT.n
            h_free = tT.n
        self.x1_tile_ready = tile_chain



class BuildB:

    def __init__(self):
        nc = bass.Bass("TRN2", target_bir_lowering=False)
        self.nc = nc
        P = Prog(nc)
        self.P = P
        xin = nc.dram_tensor("xin", [T, C], F32, kind="ExternalInput").ap()
        qin = nc.dram_tensor("q", [HD, T], BF16, kind="ExternalInput").ap()
        kall = nc.dram_tensor("kall", [HD, GTOK], BF16, kind="ExternalInput").ap()
        vall = nc.dram_tensor("vall", [128, (GTOK // 128) * HD], BF16,
                              kind="ExternalInput").ap()
        identd = nc.dram_tensor("ident_bf", [128, 128], BF16,
                                kind="ExternalInput").ap()
        out = nc.dram_tensor("out", [T, C], F32, kind="ExternalOutput").ap()

        sbt = nc.alloc_sbuf_tensor
        ident = sbt("ident", [128, 128], BF16).ap()
        q1 = sbt("q1", [HD, T], BF16).ap()
        ka = sbt("ka", [HD, GTOK], BF16).ap()
        va = sbt("va", [128, (GTOK // 128) * HD], BF16).ap()
        Pt = sbt("Pt", [128, 2 * 512], BF16).ap()
        pt_sb = sbt("pt_sb", [128, 2 * 512], BF16).ap()
        acc = sbt("acc", [128, HD], F32).ap()
        yt = sbt("yt", [128, HD], F32).ap()
        stt = sbt("stt", [128, 24], F32).ap()
        xts = sbt("xts", [128, 2 * C], F32).ap()
        ots = sbt("ots", [128, 2 * C], F32).ap()

        ps_sc = nc.alloc_psum_tensor("ps_sc", [128, 1024], F32)
        pv = nc.alloc_psum_tensor("pv", [128, 512], F32)
        ptp = [nc.alloc_psum_tensor("ptp0", [128, 512], F32),
               nc.alloc_psum_tensor("ptp1", [128, 512], F32)]

        qS, qO, tV, tA, tT = (P.sem(s) for s in ("qS", "qO", "tV", "tA", "tT"))

        for t, a in ((ident, identd), (q1, qin), (ka, kall), (va, vall)):
            P.add("sync", lambda e, t=t, a=a: e.dma_start(out=t, in_=a), inc=qS, k=16)
        n_ld = qS.n
        P.wait("tensor", qS, n_ld)

        NCH = GTOK // 512
        xt_read = [0, 0]
        ot_free = [0, 0]
        for qt in range(NT):
            par_t = qt % 2
            P.wait("sync", tV, xt_read[par_t])
            ldx = P.add("sync", lambda e, qt=qt, par_t=par_t: e.dma_start(
                out=xts[:, par_t * C:(par_t + 1) * C],
                in_=xin[qt * 128:(qt + 1) * 128, :]), inc=qS, k=16)

            nsc = [0] * (NCH + 1)
            ntr = [0] * (NCH + 1)
            npv = [0] * (NCH + 1)
            nexp = [0] * (NCH + 1)
            nev = [0] * (NCH + 1)
            nevS = [None] * (NCH + 1)
            nacc = [0] * (NCH + 1)
            nb = [0] * (NCH + 1)

            def emit_scores(ci):
                par = ci % 2
                nsc[ci] = P.add("tensor", lambda e, qt=qt, ci=ci, par=par: e.matmul(
                    ps_sc.ap()[:, par * 512:(par + 1) * 512],
                    q1[:, qt * 128:(qt + 1) * 128],
                    ka[:, ci * 512:(ci + 1) * 512], start=True, stop=True), inc=tT)

            P.wait("tensor", tA, tA.n)
            P.wait("tensor", tV, tV.n)
            emit_scores(0)
            for ci in range(NCH + 1):
                par = ci % 2
                so = 8 + 8 * par
                if ci < NCH:
                    P.wait("vector", tT, nsc[ci])
                    P.addf("vector", lambda e, par=par, so=so: e.reduce_max(
                        stt[:, so:so + 1], ps_sc.ap()[:, par * 512:(par + 1) * 512],
                        axis=AX.X, negate=True))
                    if ci == 0:
                        P.addf("vector", lambda e, so=so: e.tensor_copy(
                            stt[:, 0:1], stt[:, so:so + 1]))
                        nb[ci] = P.add("vector", lambda e, so=so: e.tensor_scalar(
                            stt[:, so + 2:so + 3], stt[:, 0:1], 1.0 / 32.0, None,
                            ALU.mult), inc=tV)
                    else:
                        P.addf("vector", lambda e, so=so: e.tensor_scalar(
                            stt[:, so + 1:so + 2], stt[:, so:so + 1], stt[:, 0:1],
                            0.0, ALU.subtract, ALU.min))
                        nb[ci] = P.add("vector", lambda e, so=so: e.tensor_scalar(
                            stt[:, so + 2:so + 3], stt[:, so + 1:so + 2],
                            stt[:, 0:1], 1.0 / 32.0, ALU.add, ALU.mult), inc=tV)
                        P.addf("vector", lambda e, so=so: e.tensor_scalar(
                            stt[:, 0:1], stt[:, so + 1:so + 2], stt[:, 0:1], None,
                            ALU.add))
                if ci < NCH:
                    P.wait("scalar", tV, nb[ci])
                    if ci > 0:
                        P.add("scalar", lambda e, so=so: e.activation(
                            stt[:, so + 3:so + 4], stt[:, so + 1:so + 2], AF.Exp,
                            scale=1.0 / 32.0))
                    if ci >= 2:
                        P.wait("scalar", tT, ntr[ci - 2])
                    nexp[ci] = P.add("scalar", lambda e, par=par, so=so: e.activation(
                        Pt[:, par * 512:(par + 1) * 512],
                        ps_sc.ap()[:, par * 512:(par + 1) * 512], AF.Exp,
                        scale=1.0 / 32.0, bias=stt[:, so + 2:so + 3],
                        accum_out=stt[:, so + 4:so + 5]), inc=tA)
                if ci + 1 < NCH:
                    if ci >= 1:
                        P.wait("tensor", tA, nexp[ci - 1])
                    emit_scores(ci + 1)
                if ci < NCH:
                    P.wait("tensor", tA, nexp[ci])
                    if ci >= 2:
                        P.wait("tensor", nevS[ci - 2], nev[ci - 2])
                    for kt in range(4):
                        P.add("tensor", lambda e, kt=kt, par=par: e.transpose(
                            ptp[par].ap().bitcast(BF16)[:, kt * 128:(kt + 1) * 128],
                            Pt[:, par * 512 + kt * 128:par * 512 + (kt + 1) * 128],
                            ident), inc=(tT if kt == 3 else None))
                    ntr[ci] = tT.n
                if ci >= 1:
                    cj = ci - 1
                    pj = cj % 2
                    P.wait("tensor", nevS[cj], nev[cj])
                    P.wait("tensor", tV, nacc[cj - 1] if cj >= 1 else 0)
                    for kt in range(4):
                        gk = cj * 4 + kt
                        P.add("tensor", lambda e, gk=gk, kt=kt, pj=pj: e.matmul(
                            pv.ap()[:, 0:HD],
                            pt_sb[:, pj * 512 + kt * 128:pj * 512 + (kt + 1) * 128],
                            va[:, gk * HD:(gk + 1) * HD],
                            start=(kt == 0), stop=(kt == 3)),
                            inc=(tT if kt == 3 else None))
                    npv[cj] = tT.n
                if ci < NCH:
                    if par == 0:
                        P.wait("vector", tT, ntr[ci])
                        if ci >= 2:
                            P.wait("vector", tT, npv[ci - 2])
                        nev[ci] = P.add("vector", lambda e, par=par: e.tensor_copy(
                            pt_sb[:, par * 512:(par + 1) * 512],
                            ptp[par].ap().bitcast(BF16)[:, 0:512]), inc=tV)
                        nevS[ci] = tV
                    else:
                        P.wait("scalar", tT, ntr[ci])
                        if ci >= 2:
                            P.wait("scalar", tT, npv[ci - 2])
                        nev[ci] = P.add("scalar", lambda e, par=par: e.activation(
                            pt_sb[:, par * 512:(par + 1) * 512],
                            ptp[par].ap().bitcast(BF16)[:, 0:512], AF.Copy),
                            inc=tA)
                        nevS[ci] = tA
                if ci < NCH:
                    P.wait("vector", tA, nexp[ci])
                    if ci == 0:
                        P.addf("vector", lambda e, so=so: e.tensor_copy(
                            stt[:, 1:2], stt[:, so + 4:so + 5]))
                    else:
                        P.addf("vector", lambda e, so=so: e.scalar_tensor_tensor(
                            stt[:, 1:2], stt[:, 1:2], stt[:, so + 3:so + 4],
                            stt[:, so + 4:so + 5], ALU.mult, ALU.add))
                if ci >= 1:
                    cj = ci - 1
                    soj = 8 + 8 * (cj % 2)
                    P.wait("vector", tT, npv[cj])
                    if cj == 0:
                        nacc[cj] = P.add("vector", lambda e: e.tensor_copy(
                            acc, pv.ap()[:, 0:HD]), inc=tV)
                    else:
                        nacc[cj] = P.add("vector", lambda e, soj=soj:
                                         e.scalar_tensor_tensor(
                                             acc, acc, stt[:, soj + 3:soj + 4],
                                             pv.ap()[:, 0:HD], ALU.mult, ALU.add),
                                         inc=tV)
                    P.wait("vector", tV, nacc[cj])
            P.addf("vector", lambda e: e.reciprocal(stt[:, 2:3], stt[:, 1:2]))
            nyt = P.addf("vector", lambda e: e.tensor_scalar(
                yt, acc, stt[:, 2:3], None, ALU.mult))
            P.wait("vector", qS, ldx)
            P.wait("vector", qO, ot_free[par_t])
            ot = ots[:, par_t * C:(par_t + 1) * C]
            not_ = P.add("vector", lambda e, par_t=par_t, ot=ot: e.tensor_add(
                ot.rearrange("p (r d) -> p r d", d=HD),
                xts[:, par_t * C:(par_t + 1) * C].rearrange("p (r d) -> p r d", d=HD),
                yt.unsqueeze(1).broadcast_to([128, NHEAD, HD])), inc=tV)
            xt_read[par_t] = not_
            P.wait("sync", tV, not_)
            nst = P.add("sync", lambda e, qt=qt, ot=ot: e.dma_start(
                out=out[qt * 128:(qt + 1) * 128, :], in_=ot), inc=qO, k=16)
            ot_free[par_t] = nst
        P.wait("sync", qO, P.sem("qO").n)
        P.finalize()



I32 = mybir.dt.int32
MAGIC = 777777
MAXPOLL = 100000


class BuildF(BuildA):

    def __init__(self, dbgF=False):
        self.stop = "all"
        self.dbg = set()
        self.dbgF = dbgF
        nc = bass.Bass("TRN2", target_bir_lowering=False)
        self.nc = nc
        self.P = Prog(nc)
        self.xmine = nc.dram_tensor("xmine", [4 * T, C], F32,
                                    kind="ExternalInput").ap()
        self.g = {}
        for name, shape, dt in GLOBAL_SPECS:
            self.g[name] = nc.dram_tensor(name, list(shape), dt,
                                          kind="ExternalInput").ap()
        self.out_d = nc.dram_tensor("out", [T, C], F32, kind="ExternalOutput").ap()
        self.xo = nc.dram_tensor("xo", [T, C], F32, kind="Internal").ap()
        self.x1tmp = nc.dram_tensor("x1tmp", [T, C], F32, kind="Internal").ap()
        self.xn0_d = nc.dram_tensor("xn0_d", [128, NC * T], BF16,
                                    kind="Internal").ap()
        self.q1_d = nc.dram_tensor("q1_d", [HD, T], BF16, kind="Internal").ap()
        self.sh_k = nc.dram_tensor("sh_k", [2, 4, HD, T], BF16, kind="Internal",
                                   addr_space="Shared")
        self.sh_v = nc.dram_tensor("sh_v", [2, 4, 128, NT * HD], BF16,
                                   kind="Internal", addr_space="Shared")
        self.sh_f = nc.dram_tensor("sh_f", [2, 4, 128], I32, kind="Internal",
                                   addr_space="Shared")
        self.fwm = nc.alloc_sbuf_tensor("fwm", [1, 128], I32).ap()
        self.fpoll = nc.alloc_sbuf_tensor("fpoll", [1, 128], I32).ap()
        self.fwz = nc.alloc_sbuf_tensor("fwz", [4, 128], I32).ap()
        self._setup_sbuf()
        self._buildF()

    def _x_slice(self, s):
        return self.xmine[s * T:(s + 1) * T, :]

    def _buildF(self):
        P = self.P
        self._load_consts()
        self.fwm_ready = P.add("vector", lambda e: e.memset(self.fwm, MAGIC),
                               inc=P.sem("tV"))
        nz = P.add("vector", lambda e: e.memset(self.fwz, 0), inc=P.sem("tV"))
        P.wait("gpsimd", P.sem("tV"), nz)

        def zero_flags(e):
            engs_ = OrderedEngineSet([mybir.EngineType.Pool])
            if not hasattr(self, "_regs"):
                nc_ = self.nc
                r = {}
                for nmq in ("half", "phalf", "koff", "voff", "foff"):
                    r[nmq] = nc_.alloc_registers("r_" + nmq,
                                                 engines=engs_).handles[0]
                pid = e.partition_id()
                e.reg_mov(r["half"], pid)
                e.reg_alu(r["half"], r["half"], 1, ALU.bitwise_and)
                e.reg_mov(r["phalf"], r["half"])
                e.reg_alu(r["phalf"], r["phalf"], 1, ALU.bitwise_xor)
                self._regs = r
            r = self._regs
            e.reg_mov(r["foff"], r["half"])
            e.reg_alu(r["foff"], r["foff"], 4 * 128, ALU.mult)
            fdst = bass.AP(self.sh_f, r["foff"], [[128, 4], [1, 128]])
            return e.dma_start(out=fdst, in_=self.fwz)

        self.flags_zeroed = P.add("gpsimd", lambda e: zero_flags(e),
                                  inc=P.sem("qSh"), k=16)
        self._kv_share_done = None
        self._xnb_guard = None
        for s in range(4):
            self._ln(src_kind="dram", dram=self._x_slice(s), ready=None)
            self._ffn()
            self._attn()
            xdst = self.xo if s == 0 else self.x1tmp
            self._proj_resid(x_dram=self._x_slice(s), xo_dram=xdst)
            self._ln(src_kind="dram", dram=xdst, ready=self.x1_tile_ready)
            if s == 0:
                self._x1ready_slot0 = list(self.x1_tile_ready)
                self._q_proj_store()
                self.qkv1_pbc_tA = P.sem("tA").n
                self._gate()
                self._store_xn0()
            self._kv_share(s)
        self._reload_xn0()
        self.x1_tile_ready = self._x1ready_slot0
        self._moe()
        self._poll_and_load()
        self._flash_final()
        if self.dbgF:
            st = P.sem("st")
            for name, src, shape, dt in [
                ("kall", self.K_all, (HD, NCORE * T), BF16),
                ("vall", self.V_all, (128, (NCORE * T // 128) * HD), BF16),
                ("q1s", self.q1s, (HD, T), BF16),
                ("xnbf", self.xnb, (128, NC * T), BF16),
            ]:
                od = self.nc.dram_tensor("dbg_" + name, list(shape), dt,
                                         kind="ExternalOutput").ap()
                P.wait("sync", P.sem("tV"), P.sem("tV").n)
                P.add("sync", lambda e, o=od, a=src: e.dma_start(out=o, in_=a),
                      inc=st, k=16)
            if True:
                odf = self.nc.dram_tensor("dbg_flags", [2, 4, 128], I32,
                                          kind="ExternalOutput").ap()
                P.add("sync", lambda e, o=odf: e.dma_start(
                    out=o, in_=bass.AP(self.sh_f, 0, [[512, 1], [1, 1024]]).rearrange(
                        "a (h s t) -> (a h) s t", h=2, s=4)), inc=st, k=16)
                odp = self.nc.dram_tensor("dbg_poll", [1, 128], I32,
                                          kind="ExternalOutput").ap()
                P.add("sync", lambda e, o=odp: e.dma_start(out=o, in_=self.fpoll),
                      inc=st, k=16)
            if hasattr(self, "_fdump"):
                od = self.nc.dram_tensor("dbg_fstats", [128, NT * 4], F32,
                                         kind="ExternalOutput").ap()
                P.wait("sync", P.sem("tV"), P.sem("tV").n)
                P.add("sync", lambda e, o=od: e.dma_start(out=o, in_=self._fdump),
                      inc=st, k=16)
            od = self.nc.dram_tensor("dbg_xo", [T, C], F32,
                                     kind="ExternalOutput").ap()
            qM_ = P.sem("qM")
            P.wait("sync", qM_, qM_.n)
            P.add("sync", lambda e, o=od: e.dma_start(out=o, in_=self.xo),
                  inc=st, k=16)
            P.wait("sync", st, st.n)
        qM = P.sem("qM")
        if qM.n:
            P.wait("gpsimd", qM, qM.n)
        P.finalize()

    def _q_proj_store(self):
        P = self.P
        qG, qS, tV, tT, tA = (P.sem(s) for s in ("qG", "qS", "tV", "tT", "tA"))
        wq1 = self.av(112 * KB, 128, NC * HD, BF16)
        q1 = self.av(117 * KB, 64, T, BF16)
        P.wait("gpsimd", tT, self.attn_mm_done)
        nl = P.add("gpsimd", lambda e: e.dma_start(out=wq1, in_=self.g["w_q1"]),
                   inc=qG, k=16)
        P.wait("tensor", tV, P.sem("tV").n)
        P.wait("tensor", tA, self.ln_tA)
        P.wait("tensor", qG, nl)
        banks = [self.pbA, self.pbB]
        bank_free_tV = [0, 0]
        for ns in range(4):
            par = ns % 2
            bt = banks[par]
            P.wait("tensor", tV, bank_free_tV[par])
            for ck in range(NC):
                P.add("tensor", lambda e, ck=ck, ns=ns, bt=bt: e.matmul(
                    self.psf(bt, 512, parts=64),
                    wq1[:, ck * HD:(ck + 1) * HD],
                    self.xnb[:, ck * T + ns * 512:ck * T + (ns + 1) * 512],
                    start=(ck == 0), stop=(ck == NC - 1)),
                    inc=(tT if ck == NC - 1 else None))
            nmm = tT.n
            P.wait("vector", tT, nmm)
            bank_free_tV[par] = P.add("vector", lambda e, ns=ns, bt=bt:
                                      e.tensor_scalar(
                q1[:, ns * 512:(ns + 1) * 512], self.psf(bt, 512, parts=64),
                self.bqkv1_pp[:, 0:1], None, ALU.add), inc=tV)
        nq = bank_free_tV[1]
        P.wait("sync", tV, nq)
        self.q1_store = P.add("sync", lambda e, q1=q1: e.dma_start(
            out=self.q1_d, in_=q1), inc=P.sem("qQ"), k=16)

    def _store_xn0(self):
        P = self.P
        qG, tA = P.sem("qG"), P.sem("tA")
        P.wait("gpsimd", tA, self.ln_tA)
        self.xn0_store = P.add("gpsimd", lambda e: e.dma_start(
            out=self.xn0_d, in_=self.xnb), inc=qG, k=16)
        self._xnb_guard = (qG, self.xn0_store)

    def _kv_share(self, s):
        P = self.P
        qG, qS, tV, tA, tT = (P.sem(x) for x in ("qG", "qS", "tV", "tA", "tT"))
        qSh = P.sem("qSh")
        wk1 = self.av(113 * KB, 128, NC * HD, BF16)
        wv1 = self.av(114 * KB, 128, NC * HD, BF16)
        kst = self.av(121 * KB, 64, T, BF16)
        vfst = self.av(125 * KB, 64, T, BF16)
        vtst = self.av(115 * KB, 128, NT * HD, BF16)
        P.wait("gpsimd", tT, self.attn_mm_done)
        nl = None
        for w, name in ((wk1, "w_k1"), (wv1, "w_v1")):
            nl = P.add("gpsimd", lambda e, w=w, a=self.g[name]: e.dma_start(
                out=w, in_=a), inc=qG, k=16)
        P.wait("tensor", tV, P.sem("tV").n)
        P.wait("tensor", tA, self.ln_tA)
        P.wait("tensor", qG, nl)
        banks = [self.pbA, self.pbB]
        bank_free_tV = [0, 0]
        for wi, (w, dstf, bcol) in enumerate(((wk1, kst, 1), (wv1, vfst, 2))):
            for ns in range(4):
                par = ns % 2
                bt = banks[par]
                P.wait("tensor", tV, bank_free_tV[par])
                for ck in range(NC):
                    P.add("tensor", lambda e, w=w, ck=ck, ns=ns, bt=bt: e.matmul(
                        self.psf(bt, 512, parts=64),
                        w[:, ck * HD:(ck + 1) * HD],
                        self.xnb[:, ck * T + ns * 512:ck * T + (ns + 1) * 512],
                        start=(ck == 0), stop=(ck == NC - 1)),
                        inc=(tT if ck == NC - 1 else None))
                nmm = tT.n
                P.wait("vector", tT, nmm)
                bank_free_tV[par] = P.add("vector", lambda e, dstf=dstf, ns=ns,
                                          bcol=bcol, bt=bt: e.tensor_scalar(
                    dstf[:, ns * 512:(ns + 1) * 512], self.psf(bt, 512, parts=64),
                    self.bqkv1_pp[:, bcol:bcol + 1], None, ALU.add), inc=tV)
        kv_tV = tV.n
        P.wait("tensor", tV, kv_tV)
        for gph in range(16):
            P.add("tensor", lambda e, gph=gph: e.transpose(
                self.psb(self.pbC, 64, base=gph * 64),
                vfst[:, gph * 128:(gph + 1) * 128], self.ident_bf[0:64, 0:64]),
                inc=(tT if gph == 15 else None))
        nvt = tT.n
        P.wait("scalar", tT, nvt)
        nvta = P.add("scalar", lambda e, vtst=vtst: e.activation(
            vtst, self.psb(self.pbC, 1024), AF.Copy), inc=tA)
        self.qkv1_pbc_tA = nvta
        engs = OrderedEngineSet([mybir.EngineType.Pool])

        def gp_regs(e):
            if not hasattr(self, "_regs"):
                nc = self.nc
                r = {}
                for nmq in ("half", "phalf", "koff", "voff", "foff"):
                    r[nmq] = nc.alloc_registers("r_" + nmq, engines=engs).handles[0]
                pid = e.partition_id()
                e.reg_mov(r["half"], pid)
                e.reg_alu(r["half"], r["half"], 1, ALU.bitwise_and)
                e.reg_mov(r["phalf"], r["half"])
                e.reg_alu(r["phalf"], r["phalf"], 1, ALU.bitwise_xor)
                self._regs = r
            return self._regs

        KHALF = 4 * HD * T
        VHALF = 4 * 128 * NT * HD
        FHALF = 4 * 128

        def share_writes(e, s=s):
            r = gp_regs(e)
            e.reg_mov(r["koff"], r["half"])
            e.reg_alu(r["koff"], r["koff"], KHALF, ALU.mult)
            e.reg_alu(r["koff"], r["koff"], s * HD * T, ALU.add)
            e.reg_mov(r["voff"], r["half"])
            e.reg_alu(r["voff"], r["voff"], VHALF, ALU.mult)
            e.reg_alu(r["voff"], r["voff"], s * 128 * NT * HD, ALU.add)
            kdst = bass.AP(self.sh_k, r["koff"], [[T, HD], [1, T]])
            vdst = bass.AP(self.sh_v, r["voff"], [[NT * HD, 128], [1, NT * HD]])
            return kdst, vdst

        P.wait("gpsimd", tV, kv_tV)
        P.wait("gpsimd", tA, nvta)
        nk = P.add("gpsimd", lambda e, s=s, kst=kst: e.dma_start(
            out=share_writes(e)[0], in_=kst), inc=qSh, k=16)
        nv = P.add("gpsimd", lambda e, s=s, vtst=vtst: e.dma_start(
            out=share_writes(e)[1], in_=vtst), inc=qSh, k=16)
        P.wait("gpsimd", qSh, nv)

        def flag_write(e, s=s):
            r = gp_regs(e)
            e.reg_mov(r["foff"], r["half"])
            e.reg_alu(r["foff"], r["foff"], FHALF, ALU.mult)
            e.reg_alu(r["foff"], r["foff"], s * 128, ALU.add)
            fdst = bass.AP(self.sh_f, r["foff"], [[128, 1], [1, 128]])
            return e.dma_start(out=fdst, in_=self.fwm)

        P.wait("gpsimd", P.sem("tV"), self.fwm_ready)
        nf = P.add("gpsimd", lambda e, s=s: flag_write(e, s), inc=qSh, k=16)
        guards = [(qSh, nf)]
        if s == 0:
            guards.append((P.sem("qQ"), self.q1_store))
        self._kv_share_done = guards

    def _reload_xn0(self):
        P = self.P
        qG, tT = P.sem("qG"), P.sem("tT")
        P.wait("gpsimd", tT, tT.n)
        self.xn0_reload = P.add("gpsimd", lambda e: e.dma_start(
            out=self.xnb, in_=self.xn0_d), inc=qG, k=16)
        P.wait("tensor", qG, self.xn0_reload)

    def _poll_and_load(self):
        P = self.P
        nc = self.nc
        qG, tT = P.sem("qG"), P.sem("tT")
        qP = P.sem("qP")
        qKV = P.sem("qKV")
        engs = OrderedEngineSet([mybir.EngineType.Pool])
        KHALF = 4 * HD * T
        VHALF = 4 * 128 * NT * HD
        FHALF = 4 * 128
        K_all = self.av(0, 64, NCORE * T, BF16)
        V_all = self.av(32 * KB, 128, (NCORE * T // 128) * HD, BF16)
        self.K_all, self.V_all = K_all, V_all
        moe_end_tT = tT.n
        self.moe_end_tT = moe_end_tT

        def poll(e):
            r = self._regs
            rv, re_ = r["koff"], r["voff"]
            rc_c = nc.alloc_registers("pollc", engines=engs)
            rt_c = nc.alloc_registers("pollt", engines=engs)
            rc, rt = rc_c.handles[0], rt_c.handles[0]
            rv_c = bass.RegisterHandles(rv)
            del rv_c
            e.reg_mov(rv, 0)
            e.reg_mov(re_, 0)
            e.reg_mov(rc, 0)
            e.reg_mov(r["foff"], r["phalf"])
            e.reg_alu(r["foff"], r["foff"], FHALF, ALU.mult)
            e.reg_alu(r["foff"], r["foff"], 3 * 128, ALU.add)
            fsrc = bass.AP(self.sh_f, r["foff"], [[128, 1], [1, 128]])
            lbl, done = "fpollL", "fpollD"
            nc.br(lbl, engines=engs)
            with nc.body(lbl, valid_engines=engs):
                e.reg_alu(re_, re_, 16, ALU.add)
                e.dma_start(out=self.fpoll, in_=fsrc).then_inc(qP.h, 16)
                e.wait_ge(qP.h, re_)
                e.reg_load(rv, self.fpoll[0:1, 0:1])
                e.reg_alu(rc, rc, 1, ALU.add)
                e.reg_mov(rt, rc)
                e.reg_alu(rt, rt, MAXPOLL, ALU.divide)
                e.reg_alu(rt, rt, MAGIC, ALU.mult)
                e.reg_alu(rt, rt, rv, ALU.add)
                nc.br_lt(rt_c, MAGIC, on_true=lbl, on_false=done, engines=engs)
            nc.switch_bb(done)
            e.reg_save(self.fpoll[0:1, 64:65], rc)
            e.reg_save(self.fpoll[0:1, 65:66], rv)
            nc.free_register(rc)
            nc.free_register(rt)

        P.wait("gpsimd", tT, moe_end_tT)
        P.add("gpsimd", lambda e: poll(e))
        def kv_loads(e):
            r = self._regs
            insts = []
            for iw, (hreg) in enumerate(("half", "phalf")):
                e.reg_mov(r["koff"], r[hreg])
                e.reg_alu(r["koff"], r["koff"], KHALF, ALU.mult)
                ksrc = bass.AP(self.sh_k, r["koff"],
                               [[T, HD], [HD * T, 4], [1, T]])
                kdst = K_all.rearrange("d (s t) -> d s t", s=NCORE)[
                    :, iw * 4:(iw + 1) * 4, :]
                insts.append(e.dma_start(out=kdst, in_=ksrc))
                e.reg_mov(r["voff"], r[hreg])
                e.reg_alu(r["voff"], r["voff"], VHALF, ALU.mult)
                vsrc = bass.AP(self.sh_v, r["voff"],
                               [[NT * HD, 128], [128 * NT * HD, 4], [1, NT * HD]])
                vdst = V_all.rearrange("p (s c) -> p s c", s=NCORE)[
                    :, iw * 4:(iw + 1) * 4, :]
                insts.append(e.dma_start(out=vdst, in_=vsrc))
            return insts

        def kv_loads_emit(e):
            for ins in kv_loads(e):
                ins.then_inc(qKV.h, 16)

        P.add("gpsimd", lambda e: kv_loads_emit(e))
        qKV.n += 64
        self.kv_loaded = qKV.n
        q1s = self.av(84 * KB, 64, T, BF16)
        self.q1s = q1s
        P.wait("sync", P.sem("qQ"), self.q1_store)
        self.q1_loaded = P.add("sync", lambda e, q1s=q1s: e.dma_start(
            out=q1s, in_=self.q1_d), inc=P.sem("qS"), k=16)

    def _flash_final(self):
        P = self.P
        qS, qO, tV, tA, tT = (P.sem(s) for s in ("qS", "qO", "tV", "tA", "tT"))
        qKV, qM = P.sem("qKV"), P.sem("qM")
        K_all, V_all, q1 = self.K_all, self.V_all, self.q1s
        Pt = self.av(48 * KB, 128, 2 * 512, BF16)
        pt_sb = self.av(50 * KB, 128, 2 * 512, BF16)
        xts_a = [self.av(52 * KB, 128, C, F32), self.av(56 * KB, 128, C, F32)]
        ots_a = [self.av(60 * KB, 128, C, F32), self.av(64 * KB, 128, C, F32)]
        accv = self.av(72 * KB, 128, HD, F32)
        ytv = self.av(72 * KB + 512, 128, HD, F32)
        stt = self.gsc
        ps_sc = self.ps_s
        pv = self.pbA
        ptp = [self.pbB, self.pbC]
        ident = self.ident_bf

        P.wait("tensor", tV, tV.n)
        P.wait("tensor", tA, tA.n)
        P.wait("tensor", qKV, self.kv_loaded)
        P.wait("tensor", qS, self.q1_loaded)
        P.wait("sync", tT, self.moe_end_tT)
        tile_chain = self.x1_tile_ready

        NCH = GTOK // 512
        xt_read = [0, 0]
        ot_free = [0, 0]
        for qt in range(NT):
            par_t = qt % 2
            P.wait("sync", tV, xt_read[par_t])
            cs_, cn_ = tile_chain[qt]
            P.wait("sync", cs_, cn_)
            ldx = P.add("sync", lambda e, qt=qt, par_t=par_t: e.dma_start(
                out=xts_a[par_t], in_=self.xo[qt * 128:(qt + 1) * 128, :]),
                inc=qS, k=16)

            nsc = [0] * (NCH + 1)
            ntr = [0] * (NCH + 1)
            npv = [0] * (NCH + 1)
            nexp = [0] * (NCH + 1)
            nev = [0] * (NCH + 1)
            nevS = [None] * (NCH + 1)
            nacc = [0] * (NCH + 1)
            nb = [0] * (NCH + 1)

            def emit_scores(ci, qt=qt):
                par = ci % 2
                nsc[ci] = P.add("tensor", lambda e, qt=qt, ci=ci, par=par: e.matmul(
                    self.psf(ps_sc, 512, base=par * 512),
                    q1[:, qt * 128:(qt + 1) * 128],
                    K_all[:, ci * 512:(ci + 1) * 512], start=True, stop=True),
                    inc=tT)

            P.wait("tensor", tA, tA.n)
            P.wait("tensor", tV, tV.n)
            emit_scores(0)
            for ci in range(NCH + 1):
                par = ci % 2
                so = 8 + 8 * par
                if ci < NCH:
                    P.wait("vector", tT, nsc[ci])
                    P.addf("vector", lambda e, par=par, so=so: e.reduce_max(
                        stt[:, so:so + 1], self.psf(ps_sc, 512, base=par * 512),
                        axis=AX.X, negate=True))
                    if ci == 0:
                        P.addf("vector", lambda e, so=so: e.tensor_copy(
                            stt[:, 0:1], stt[:, so:so + 1]))
                        nb[ci] = P.add("vector", lambda e, so=so: e.tensor_scalar(
                            stt[:, so + 2:so + 3], stt[:, 0:1], 1.0 / 32.0, None,
                            ALU.mult), inc=tV)
                    else:
                        P.addf("vector", lambda e, so=so: e.tensor_scalar(
                            stt[:, so + 1:so + 2], stt[:, so:so + 1], stt[:, 0:1],
                            0.0, ALU.subtract, ALU.min))
                        nb[ci] = P.add("vector", lambda e, so=so: e.tensor_scalar(
                            stt[:, so + 2:so + 3], stt[:, so + 1:so + 2],
                            stt[:, 0:1], 1.0 / 32.0, ALU.add, ALU.mult), inc=tV)
                        P.addf("vector", lambda e, so=so: e.tensor_scalar(
                            stt[:, 0:1], stt[:, so + 1:so + 2], stt[:, 0:1], None,
                            ALU.add))
                if ci < NCH:
                    P.wait("scalar", tV, nb[ci])
                    if ci > 0:
                        P.add("scalar", lambda e, so=so: e.activation(
                            stt[:, so + 3:so + 4], stt[:, so + 1:so + 2], AF.Exp,
                            scale=1.0 / 32.0))
                    if ci >= 2:
                        P.wait("scalar", tT, ntr[ci - 2])
                    nexp[ci] = P.add("scalar", lambda e, par=par, so=so: e.activation(
                        Pt[:, par * 512:(par + 1) * 512],
                        self.psf(ps_sc, 512, base=par * 512), AF.Exp,
                        scale=1.0 / 32.0, bias=stt[:, so + 2:so + 3],
                        accum_out=stt[:, so + 4:so + 5]), inc=tA)
                if ci + 1 < NCH:
                    if ci >= 1:
                        P.wait("tensor", tA, nexp[ci - 1])
                    emit_scores(ci + 1)
                if ci < NCH:
                    P.wait("tensor", tA, nexp[ci])
                    if ci >= 2:
                        P.wait("tensor", nevS[ci - 2], nev[ci - 2])
                    for kt in range(4):
                        P.add("tensor", lambda e, kt=kt, par=par: e.transpose(
                            ptp[par].ap().bitcast(BF16)[:, kt * 128:(kt + 1) * 128],
                            Pt[:, par * 512 + kt * 128:par * 512 + (kt + 1) * 128],
                            ident), inc=(tT if kt == 3 else None))
                    ntr[ci] = tT.n
                if ci >= 1:
                    cj = ci - 1
                    pj = cj % 2
                    P.wait("tensor", nevS[cj], nev[cj])
                    P.wait("tensor", tV, nacc[cj - 1] if cj >= 1 else 0)
                    for kt in range(4):
                        gk = cj * 4 + kt
                        P.add("tensor", lambda e, gk=gk, kt=kt, pj=pj: e.matmul(
                            pv.ap()[:, 0:HD],
                            pt_sb[:, pj * 512 + kt * 128:pj * 512 + (kt + 1) * 128],
                            V_all[:, gk * HD:(gk + 1) * HD],
                            start=(kt == 0), stop=(kt == 3)),
                            inc=(tT if kt == 3 else None))
                    npv[cj] = tT.n
                if ci < NCH:
                    if par == 0:
                        P.wait("vector", tT, ntr[ci])
                        if ci >= 2:
                            P.wait("vector", tT, npv[ci - 2])
                        nev[ci] = P.add("vector", lambda e, par=par: e.tensor_copy(
                            pt_sb[:, par * 512:(par + 1) * 512],
                            ptp[par].ap().bitcast(BF16)[:, 0:512]), inc=tV)
                        nevS[ci] = tV
                    else:
                        P.wait("scalar", tT, ntr[ci])
                        if ci >= 2:
                            P.wait("scalar", tT, npv[ci - 2])
                        nev[ci] = P.add("scalar", lambda e, par=par: e.activation(
                            pt_sb[:, par * 512:(par + 1) * 512],
                            ptp[par].ap().bitcast(BF16)[:, 0:512], AF.Copy),
                            inc=tA)
                        nevS[ci] = tA
                if ci < NCH:
                    P.wait("vector", tA, nexp[ci])
                    if ci == 0:
                        P.addf("vector", lambda e, so=so: e.tensor_copy(
                            stt[:, 1:2], stt[:, so + 4:so + 5]))
                    else:
                        P.addf("vector", lambda e, so=so: e.scalar_tensor_tensor(
                            stt[:, 1:2], stt[:, 1:2], stt[:, so + 3:so + 4],
                            stt[:, so + 4:so + 5], ALU.mult, ALU.add))
                if ci >= 1:
                    cj = ci - 1
                    soj = 8 + 8 * (cj % 2)
                    P.wait("vector", tT, npv[cj])
                    if cj == 0:
                        nacc[cj] = P.add("vector", lambda e: e.tensor_copy(
                            accv, pv.ap()[:, 0:HD]), inc=tV)
                    else:
                        nacc[cj] = P.add("vector", lambda e, soj=soj:
                                         e.scalar_tensor_tensor(
                                             accv, accv, stt[:, soj + 3:soj + 4],
                                             pv.ap()[:, 0:HD], ALU.mult, ALU.add),
                                         inc=tV)
                    P.wait("vector", tV, nacc[cj])
            P.addf("vector", lambda e: e.reciprocal(stt[:, 2:3], stt[:, 1:2]))
            P.addf("vector", lambda e: e.tensor_scalar(
                ytv, accv, stt[:, 2:3], None, ALU.mult))
            if self.dbgF:
                dv = self.av(88 * KB, 128, NT * 4, F32)
                self._fdump = dv
                P.addf("vector", lambda e, qt=qt: e.tensor_copy(
                    dv[:, qt * 4:qt * 4 + 3], stt[:, 0:3]))
            P.wait("vector", qS, ldx)
            P.wait("vector", qO, ot_free[par_t])
            not_ = P.add("vector", lambda e, par_t=par_t: e.tensor_add(
                ots_a[par_t].rearrange("p (r d) -> p r d", d=HD),
                xts_a[par_t].rearrange("p (r d) -> p r d", d=HD),
                ytv.unsqueeze(1).broadcast_to([128, NHEAD, HD])), inc=tV)
            xt_read[par_t] = not_
            P.wait("sync", tV, not_)
            nst = P.add("sync", lambda e, qt=qt, par_t=par_t: e.dma_start(
                out=self.out_d[qt * 128:(qt + 1) * 128, :], in_=ots_a[par_t]),
                inc=qO, k=16)
            ot_free[par_t] = nst
        P.wait("sync", qO, P.sem("qO").n)



_BUILD_CACHE = {}

FUSED = True


def kernel(**inputs):
    g = make_host_globals(inputs)
    x = np.asarray(inputs["L0_attn_x"], np.float32)

    if FUSED:
        if "F" not in _BUILD_CACHE:
            _BUILD_CACHE["F"] = BuildF()
        F = _BUILD_CACHE["F"]
        in_maps = []
        for c in range(NCORE):
            m = dict(g)
            m["xmine"] = np.ascontiguousarray(np.concatenate(
                [x[(c + 2 * s) % NCORE] for s in range(4)], axis=0))
            in_maps.append(m)
        res = run_bass_kernel_spmd(F.nc, in_maps, core_ids=list(range(NCORE)))
        return np.stack([res.results[c]["out"] for c in range(NCORE)], axis=0)

    if "A" not in _BUILD_CACHE:
        _BUILD_CACHE["A"] = BuildA()
        _BUILD_CACHE["B"] = BuildB()
    A, B = _BUILD_CACHE["A"], _BUILD_CACHE["B"]

    in_maps = []
    for b in range(NCORE):
        m = dict(g)
        m["x"] = np.ascontiguousarray(x[b])
        in_maps.append(m)
    resA = run_bass_kernel_spmd(A.nc, in_maps, core_ids=list(range(NCORE)))

    K_all = np.concatenate([resA.results[c]["ko"] for c in range(NCORE)], axis=1)
    vparts = [resA.results[c]["vo"].reshape(128, NT, HD) for c in range(NCORE)]
    V_all = np.ascontiguousarray(
        np.stack(vparts, axis=1).reshape(128, NCORE * NT * HD))
    in_mapsB = []
    for c in range(NCORE):
        in_mapsB.append({
            "xin": resA.results[c]["xo"],
            "q": resA.results[c]["qo"],
            "kall": np.ascontiguousarray(K_all),
            "vall": V_all,
            "ident_bf": g["ident_bf"],
        })
    resB = run_bass_kernel_spmd(B.nc, in_mapsB, core_ids=list(range(NCORE)))
    out = np.stack([resB.results[c]["out"] for c in range(NCORE)], axis=0)
    return out
